# version 13
# baseline (speedup 1.0000x reference)
import contextlib

import numpy as np
import concourse.bass as bass
import concourse.mybir as mybir
from concourse.bass_utils import run_bass_kernel_spmd

F32 = mybir.dt.float32
BF16 = mybir.dt.bfloat16
I32 = mybir.dt.int32
ALU = mybir.AluOpType
AF = mybir.ActivationFunctionType

NCORE = 8
H = 2048
DI = 4096
NST = 16
KCV = 4
RDT = 128
L = 16384
DSL = DI // NCORE
TSL = L // NCORE
NHC = H // 128
NDC = DSL // 128
HF = 4 * H
NHH = HF // 128
HSL = HF // NCORE
EPS = 1e-5

TCF = 1024
NF = L // TCF
TCM = 2048
NM = L // TCM
TCB = 512
NB = TSL // TCB

DEBUG = False


class Ctx:
    def __init__(self, nc):
        self.nc = nc
        self.sem = {}
        self.cnt = {}

    def new(self, *names):
        for n in names:
            self.sem[n] = self.nc.semaphore(n).__enter__()
            self.cnt[n] = 0

    def inc(self, name, instr, k=1):
        instr.then_inc(self.sem[name], k)
        self.cnt[name] += k
        return self.cnt[name]

    def w(self, eng, name, val=None):
        eng.wait_ge(self.sem[name], self.cnt[name] if val is None else val)

    def v(self, name):
        return self.cnt[name]


def build(debug=False):
    nc = bass.Bass("TRN2", target_bir_lowering=False, debug=False, detect_race_conditions=False)

    xT = nc.dram_tensor("xT", [H, L], F32, kind="ExternalInput").ap()
    xTs = nc.dram_tensor("xTs", [NHC, 128, TSL], F32, kind="ExternalInput").ap()
    win_sT = nc.dram_tensor("win_sT", [NHC, 128, 2 * DSL], F32, kind="ExternalInput").ap()
    wxT = nc.dram_tensor("wxT", [NDC, 128, RDT + 2 * NST], F32, kind="ExternalInput").ap()
    wdtT = nc.dram_tensor("wdtT", [128, DSL], F32, kind="ExternalInput").ap()
    wout_t = nc.dram_tensor("wout_t", [NHC, DSL, 128], F32, kind="ExternalInput").ap()
    cfc_t = nc.dram_tensor("cfc_t", [8, NHC, 128, 128], F32, kind="ExternalInput").ap()
    cprojT_t = nc.dram_tensor("cprojT_t", [8, NHC, 128, 128], F32, kind="ExternalInput").ap()
    wconv_r = nc.dram_tensor("wconv_r", [128, NDC, KCV], F32, kind="ExternalInput").ap()
    bconv_r = nc.dram_tensor("bconv_r", [128, NDC], F32, kind="ExternalInput").ap()
    bdt_r = nc.dram_tensor("bdt_r", [128, NDC], F32, kind="ExternalInput").ap()
    d_r = nc.dram_tensor("d_r", [128, NDC], F32, kind="ExternalInput").ap()
    alog_r = nc.dram_tensor("alog_r", [128, NDC * NST], F32, kind="ExternalInput").ap()
    iden = nc.dram_tensor("iden", [128, 128], F32, kind="ExternalInput").ap()
    cidx = nc.dram_tensor("cidx", [1, NB], I32, kind="ExternalInput").ap()

    y_out = nc.dram_tensor("y_out", [TSL, H], F32, kind="ExternalOutput").ap()

    hsD = nc.dram_tensor("hsD", [128, NDC, L], BF16, kind="Internal").ap()
    sgD = nc.dram_tensor("sgD", [128, NDC, L], BF16, kind="Internal").ap()
    arDT_in = nc.dram_tensor("arDT_in", [128, L], F32, kind="Internal").ap()
    arDT_out = nc.dram_tensor("arDT_out", [128, L], F32, kind="Internal").ap()
    arBC_in = nc.dram_tensor("arBC_in", [32, L], F32, kind="Internal").ap()
    arBC_out = nc.dram_tensor("arBC_out", [32, L], F32, kind="Internal").ap()
    bcD = nc.dram_tensor("bcD", [32, L], BF16, kind="Internal").ap()
    arS_in = nc.dram_tensor("arS_in", [1, 8], F32, kind="Internal").ap()
    arS_out = nc.dram_tensor("arS_out", [1, 8], F32, kind="Internal").ap()
    rowsD = nc.dram_tensor("rowsD", [8, L], F32, kind="Internal").ap()
    y2ag_in = nc.dram_tensor("y2ag_in", [DSL, L], BF16, kind="Internal").ap()
    y2ag_out = nc.dram_tensor("y2ag_out", [DI, L], BF16, kind="Internal").ap()
    woag_in = nc.dram_tensor("woag_in", [NHC, DSL, 128], BF16, kind="Internal").ap()
    woag_out = nc.dram_tensor("woag_out", [NCORE * NHC, DSL, 128], BF16, kind="Internal").ap()
    q1ag_in = nc.dram_tensor("q1ag_in", [8, NHC, 128, 128], BF16, kind="Internal").ap()
    q1ag_out = nc.dram_tensor("q1ag_out", [NHH, NHC, 128, 128], BF16, kind="Internal").ap()
    q2ag_in = nc.dram_tensor("q2ag_in", [8, NHC, 128, 128], BF16, kind="Internal").ap()
    q2ag_out = nc.dram_tensor("q2ag_out", [NHH, NHC, 128, 128], BF16, kind="Internal").ap()
    x1D = nc.dram_tensor("x1D", [128, NHC, TSL], F32, kind="Internal").ap()

    if debug:
        dbg_hs = nc.dram_tensor("dbg_hs", [128, NDC, L], BF16, kind="ExternalOutput").ap()
        dbg_sg = nc.dram_tensor("dbg_sg", [128, NDC, L], BF16, kind="ExternalOutput").ap()
        dbg_dt = nc.dram_tensor("dbg_dt", [128, L], F32, kind="ExternalOutput").ap()
        dbg_bc = nc.dram_tensor("dbg_bc", [32, L], F32, kind="ExternalOutput").ap()
        dbg_y2 = nc.dram_tensor("dbg_y2", [DSL, L], BF16, kind="ExternalOutput").ap()
        dbg_x1 = nc.dram_tensor("dbg_x1", [128, NHC, TSL], F32, kind="ExternalOutput").ap()
        dbg_q1 = nc.dram_tensor("dbg_q1", [8, NHC, 128, 128], BF16, kind="ExternalOutput").ap()
        dbg_scl = nc.dram_tensor("dbg_scl", [4, 8], F32, kind="ExternalOutput").ap()
        dbg_dtc = nc.dram_tensor("dbg_dtc", [128, NDC, TCM], F32, kind="ExternalOutput").ap()
        dbg_sc = nc.dram_tensor("dbg_sc", [128, 6, 1024], BF16, kind="ExternalOutput").ap()
        dbg_st = nc.dram_tensor("dbg_st", [128, NDC * NST], F32, kind="ExternalOutput").ap()
        dbg_ffn = nc.dram_tensor("dbg_ffn", [128, 8, TCB], F32, kind="ExternalOutput").ap()
        dbg_x1n = nc.dram_tensor("dbg_x1n", [128, NHC, TCB], F32, kind="ExternalOutput").ap()

    cx = Ctx(nc)
    rg = [list(range(NCORE))]

    es = contextlib.ExitStack()
    with es:
        def sb(name, shape, dtype):
            return es.enter_context(nc.sbuf_tensor(name, shape, dtype))

        ident_f = sb("ident_f", [128, 128], F32)
        ident_b = sb("ident_b", [128, 128], BF16)
        ones_b = sb("ones_b", [128, 1], BF16)
        ones_f = sb("ones_f", [128, 1], F32)
        epsP = sb("epsP", [128, 1], F32)
        wcv = sb("wcv", [128, NDC, KCV], F32)
        bcv = sb("bcv", [128, NDC], F32)
        bdt_s = sb("bdt_s", [128, NDC], F32)
        dv = sb("dv", [128, NDC], F32)
        A_sb = sb("A_sb", [128, NDC * NST], F32)
        ws = sb("ws", [128, 8], F32)
        scl = sb("scl", [128, 4], F32)
        cid_sb = sb("cid_sb", [1, NB], I32)
        ones_r = sb("ones_r", [1, 128], F32)
        rowA = sb("rowA", [1, 1024], F32)
        rowB = sb("rowB", [1, 1024], F32)
        rowC = sb("rowC", [1, 1024], F32)
        rowD_ = sb("rowD_", [1, 1024], F32)
        rowE = sb("rowE", [1, 1024], F32)
        pb = [es.enter_context(nc.psum_tensor("pb%d" % i, [128, 512], F32)) for i in range(8)]
        block = es.enter_context(nc.Block())

        cx.new("qC", "qCa", "qCs", "qWin", "qWx", "qWo", "qBcd", "qWdt", "sT", "sV", "sS", "sG",
               "qFx", "qFr", "qFb", "qOdt", "qObc", "qOhs", "qOsg",
               "qQl", "qQs", "qSc",
               "qMd", "qMh", "qMb0", "qMb1", "qMy",
               "qBy", "qBx", "qBw0", "qBw1", "qBq0", "qBq1", "qB20", "qB21",
               "qBr", "qBb", "qBsh", "qBx1", "qOut", "qDbg", "qBq2")

        bank_guard = [("sV", 0)] * 8

        def mm_wait_bank(tensor, b):
            sem, c = bank_guard[b]
            tensor.wait_ge(cx.sem[sem], c)

        @block.sync
        def _(sync):
            sync.dma_start(out=ident_f[:], in_=iden).then_inc(cx.sem["qC"], 16)
            sync.dma_start(out=wcv[:], in_=wconv_r).then_inc(cx.sem["qC"], 16)
            sync.dma_start(out=bcv[:], in_=bconv_r).then_inc(cx.sem["qC"], 16)
            sync.dma_start(out=bdt_s[:], in_=bdt_r).then_inc(cx.sem["qC"], 16)
            sync.dma_start(out=dv[:], in_=d_r).then_inc(cx.sem["qC"], 16)
            sync.dma_start(out=cid_sb[:], in_=cidx).then_inc(cx.sem["qC"], 16)
        cx.cnt["qC"] += 6 * 16
        qC_base = cx.v("qC")

        @block.vector
        def _(v):
            v.memset(ones_b[:], 1.0)
            v.memset(ones_f[:], 1.0)
            v.memset(epsP[:], EPS)
            v.memset(ones_r[:], 1.0)
            cx.inc("sV", v.engine_nop())
            cx.w(v, "qC", qC_base)
            cx.inc("sV", v.tensor_copy(ident_b[:], ident_f[:]))
        sV_consts = cx.v("sV")

        with nc.sbuf_tensor("alog_t", [128, NDC * NST], F32) as alog_t:
            @block.sync
            def _(sync):
                sync.dma_start(out=alog_t[:], in_=alog_r).then_inc(cx.sem["qCa"], 16)
            cx.cnt["qCa"] += 16

            @block.scalar
            def _(s):
                cx.w(s, "qCa")
                cx.inc("sS", s.activation(alog_t[:], alog_t[:], AF.Exp))

            @block.vector
            def _(v):
                cx.w(v, "sS")
                cx.inc("sV", v.tensor_scalar(A_sb[:], alog_t[:], -1.0, None, ALU.mult))

        wes = contextlib.ExitStack()
        with wes:
            win_sb = wes.enter_context(nc.sbuf_tensor("win_sb", [128, NHC, 2 * DSL], BF16))
            wx_sb = wes.enter_context(nc.sbuf_tensor("wx_sb", [128, NDC, RDT + 2 * NST], BF16))
            @block.gpsimd
            def _(g):
                cx.w(g, "sV")
                cx.w(g, "sS")
                g.dma_start(out=win_sb[:], in_=win_sT.rearrange("kc p e -> p kc e")).then_inc(cx.sem["qWin"], 16)
                g.dma_start(out=wx_sb[:], in_=wxT.rearrange("dc p m -> p dc m")).then_inc(cx.sem["qWx"], 16)
            cx.cnt["qWin"] += 16
            cx.cnt["qWx"] += 16

            with nc.sbuf_tensor("wstg", [128, NHC * 4 * 128], BF16) as wstg:
                @block.gpsimd
                def _(g):
                    g.dma_start(out=wstg.rearrange("p (hc dcw h) -> p hc dcw h", hc=NHC, dcw=4),
                                in_=wout_t.rearrange("hc (dcw p) h -> p hc dcw h", p=128)).then_inc(cx.sem["qWo"], 16)
                cx.cnt["qWo"] += 16

                @block.sync
                def _(sync):
                    cx.w(sync, "qWo")
                    sync.dma_start(out=woag_in.rearrange("hc (dcw p) h -> p hc dcw h", p=128),
                                   in_=wstg.rearrange("p (hc dcw h) -> p hc dcw h", hc=NHC, dcw=4)).then_inc(cx.sem["qWo"], 16)
                cx.cnt["qWo"] += 16

            for e in range(8):
                b = e % 2
                @block.tensor
                def _(t, e=e, b=b):
                    if e == 0:
                        cx.w(t, "qWin")
                        cx.w(t, "sV", 1)
                    mm_wait_bank(t, b)
                    for kc in range(NHC):
                        mm = t.matmul(pb[b][:, 0:1], win_sb[:, kc, e * 128:(e + 1) * 128],
                                      ones_b[:, 0:1], start=(kc == 0), stop=(kc == NHC - 1))
                    cx.inc("sT", mm)

                @block.vector
                def _(v, e=e, b=b):
                    cx.w(v, "sT")
                    c = cx.inc("sV", v.tensor_copy(ws[:, e:e + 1], pb[b][:, 0:1]))
                    bank_guard[b] = ("sV", c)

            qes = contextlib.ExitStack()
            with qes:
                wq_in = qes.enter_context(nc.sbuf_tensor("wq_in", [128, 1024], F32))
                wq_abs = qes.enter_context(nc.sbuf_tensor("wq_abs", [128, 1024], F32))
                asum = qes.enter_context(nc.sbuf_tensor("asum", [128, 32], F32))
                as_tot = qes.enter_context(nc.sbuf_tensor("as_tot", [128, 2], F32))
                srow = qes.enter_context(nc.sbuf_tensor("srow", [1, 8], F32))
                wq_g1 = qes.enter_context(nc.sbuf_tensor("wq_g1", [128, 1024], F32))
                wq_g2 = qes.enter_context(nc.sbuf_tensor("wq_g2", [128, 1024], F32))
                wq_b = qes.enter_context(nc.sbuf_tensor("wq_b", [128, 1024], BF16))
                mats = [(cfc_t, q1ag_in), (cprojT_t, q2ag_in)]
                for mi, (src, dst) in enumerate(mats):
                    srcf = src.rearrange("a b p q -> (a b) p q")
                    for ch in range(16):
                        @block.sync
                        def _(sync, srcf=srcf, ch=ch):
                            cx.w(sync, "sS")
                            sync.dma_start(
                                out=wq_in.rearrange("p (a q) -> p a q", a=8),
                                in_=srcf[ch * 8:(ch + 1) * 8, :, :].rearrange("a p q -> p a q"),
                            ).then_inc(cx.sem["qQl"], 16)
                        cx.cnt["qQl"] += 16

                        @block.scalar
                        def _(s, mi=mi, ch=ch):
                            cx.w(s, "qQl")
                            cx.w(s, "sV")
                            cx.inc("sS", s.activation(wq_abs[:], wq_in[:], AF.Abs))

                        @block.vector
                        def _(v, mi=mi, ch=ch):
                            cx.w(v, "sS")
                            cx.inc("sV", v.reduce_sum(asum[:, mi * 16 + ch: mi * 16 + ch + 1],
                                                      wq_abs[:], axis=mybir.AxisListType.X))

                @block.vector
                def _(v):
                    v.memset(srow[:], 0.0)
                    v.reduce_sum(as_tot[:, 0:1], asum[:, 0:16], axis=mybir.AxisListType.X)
                    cx.inc("sV", v.reduce_sum(as_tot[:, 1:2], asum[:, 16:32], axis=mybir.AxisListType.X))

                @block.tensor
                def _(t):
                    cx.w(t, "sV")
                    for mi in range(2):
                        b = 2 + mi
                        mm_wait_bank(t, b)
                        mm = t.matmul(pb[b][0:1, 0:1], as_tot[:, mi:mi + 1], ones_f[:, 0:1],
                                 start=True, stop=True)
                    cx.inc("sT", mm)

                @block.vector
                def _(v):
                    cx.w(v, "sT")
                    v.tensor_copy(srow[0:1, 0:1], pb[2][0:1, 0:1])
                    c = cx.inc("sV", v.tensor_copy(srow[0:1, 1:2], pb[3][0:1, 0:1]))
                    bank_guard[2] = ("sV", c)
                    bank_guard[3] = ("sV", c)

                @block.sync
                def _(sync):
                    cx.w(sync, "sV")
                    sync.dma_start(out=arS_in[0:1, 0:8], in_=srow[0:1, 0:8]).then_inc(cx.sem["qSc"], 16)
                cx.cnt["qSc"] += 16

                @block.gpsimd
                def _(g):
                    cx.w(g, "qSc")
                    cx.inc("sG", g.collective_compute(
                        "AllReduce", ALU.add, replica_groups=rg,
                        ins=[arS_in], outs=[arS_out]))

                @block.sync
                def _(sync):
                    cx.w(sync, "sG")
                    sync.dma_start(out=rowA[0:1, 0:2], in_=arS_out[0:1, 0:2]).then_inc(cx.sem["qSc"], 16)
                cx.cnt["qSc"] += 16

                @block.vector
                def _(v):
                    cx.w(v, "qSc")
                    v.tensor_scalar(rowB[0:1, 0:512], rowA[0:1, 0:512], 1.0 / (2048.0 * 8192.0), None, ALU.mult)
                    v.tensor_scalar(rowC[0:1, 0:512], rowB[0:1, 0:512], EPS, None, ALU.max)
                    v.reciprocal(rowD_[0:1, 0:512], rowC[0:1, 0:512])
                    cx.inc("sV", v.tensor_scalar(rowE[0:1, 0:512], rowC[0:1, 0:512], 1.0 / 255.0, None, ALU.mult))

                @block.sync
                def _(sync):
                    cx.w(sync, "sV")
                    sync.dma_start(out=rowsD[7, 0:2], in_=rowD_[0:1, 0:2]).then_inc(cx.sem["qSc"], 16)
                    sync.dma_start(out=rowsD[7, 2:3], in_=rowC[0:1, 0:1]).then_inc(cx.sem["qSc"], 16)
                    sync.dma_start(out=rowsD[7, 3:4], in_=rowE[0:1, 1:2]).then_inc(cx.sem["qSc"], 16)
                    cx.w(sync, "qSc", cx.v("qSc") + 48)
                    sync.dma_start(out=scl[:], in_=rowsD[7:8, 0:4].broadcast_to((128, 4))).then_inc(cx.sem["qCs"], 16)
                cx.cnt["qSc"] += 48
                cx.cnt["qCs"] += 16

                for mi, (src, dst) in enumerate(mats):
                    srcf = src.rearrange("a b p q -> (a b) p q")
                    dstf = dst.rearrange("a b p q -> (a b) p q")
                    for ch in range(16):
                        @block.sync
                        def _(sync, srcf=srcf, ch=ch):
                            cx.w(sync, "sV")
                            sync.dma_start(
                                out=wq_in.rearrange("p (a q) -> p a q", a=8),
                                in_=srcf[ch * 8:(ch + 1) * 8, :, :].rearrange("a p q -> p a q"),
                            ).then_inc(cx.sem["qQl"], 16)
                        cx.cnt["qQl"] += 16

                        @block.vector
                        def _(v, mi=mi, ch=ch):
                            cx.w(v, "qQl")
                            cx.w(v, "qCs")
                            cx.w(v, "qQs", cx.v("qQs"))
                            v.tensor_scalar(wq_abs[:], wq_in[:], scl[:, mi:mi + 1], None, ALU.mult)
                            v.tensor_scalar(wq_g1[:], wq_abs[:], 0.5, None, ALU.is_gt)
                            v.tensor_scalar(wq_g2[:], wq_abs[:], -0.5, None, ALU.is_lt)
                            cx.inc("sV", v.tensor_tensor(wq_b[:], wq_g1[:], wq_g2[:], ALU.subtract))

                        @block.sync
                        def _(sync, dstf=dstf, ch=ch):
                            cx.w(sync, "sV")
                            sync.dma_start(
                                out=dstf[ch * 8:(ch + 1) * 8, :, :].rearrange("a p q -> p a q"),
                                in_=wq_b.rearrange("p (a q) -> p a q", a=8),
                            ).then_inc(cx.sem["qQs"], 16)
                        cx.cnt["qQs"] += 16

            @block.gpsimd
            def _(g):
                cx.w(g, "qQs")
                cx.w(g, "qWo")
                cx.inc("sG", g.collective_compute("AllGather", ALU.bypass, replica_groups=rg,
                                                  ins=[q1ag_in], outs=[q1ag_out]))
                cx.inc("sG", g.collective_compute("AllGather", ALU.bypass, replica_groups=rg,
                                                  ins=[q2ag_in], outs=[q2ag_out]))
                cx.inc("sG", g.collective_compute("AllGather", ALU.bypass, replica_groups=rg,
                                                  ins=[woag_in], outs=[woag_out]))
            sG_weights = cx.v("sG")

            if debug:
                @block.scalar
                def _(s):
                    cx.w(s, "sG", sG_weights)
                    s.dma_start(out=dbg_q1, in_=q1ag_in).then_inc(cx.sem["qDbg"], 16)
                    s.dma_start(out=dbg_scl[0:1, :], in_=arS_in).then_inc(cx.sem["qDbg"], 16)
                    s.dma_start(out=dbg_scl[1:2, :], in_=rowsD[7:8, 0:8]).then_inc(cx.sem["qDbg"], 16)
                    s.dma_start(out=dbg_scl[2:3, 0:4], in_=scl[0:1, :]).then_inc(cx.sem["qDbg"], 16)
                    s.dma_start(out=dbg_scl[3:4, 0:4], in_=scl[127:128, :]).then_inc(cx.sem["qDbg"], 16)
                cx.cnt["qDbg"] += 80

            fes = contextlib.ExitStack()
            with fes:
                xb = fes.enter_context(nc.sbuf_tensor("xb", [128, NHC, TCF], BF16))
                x2t = fes.enter_context(nc.sbuf_tensor("x2t", [128, 2, 4, TCF], F32))
                proj = fes.enter_context(nc.sbuf_tensor("proj", [128, 8, TCF], F32))
                hs_bf = fes.enter_context(nc.sbuf_tensor("hs_bf", [128, NDC, TCF], BF16))
                sg_bf = fes.enter_context(nc.sbuf_tensor("sg_bf", [128, NDC, TCF], BF16))
                conv_f = fes.enter_context(nc.sbuf_tensor("conv_f", [128, TCF], F32))
                pconv = fes.enter_context(nc.sbuf_tensor("pconv", [128, TCF + 3], F32))
                tail = fes.enter_context(nc.sbuf_tensor("tail", [128, NDC, 3], F32))
                bcmr = fes.enter_context(nc.sbuf_tensor("bcmr", [128, TCF], F32))
                bcr = fes.enter_context(nc.sbuf_tensor("bcr", [128, TCF], F32))
                wsmT = fes.enter_context(nc.sbuf_tensor("wsmT", [128, 512], F32))
                t1f = fes.enter_context(nc.sbuf_tensor("t1f", [128, 512], F32))
                sdt = fes.enter_context(nc.sbuf_tensor("sdt", [128, TCF], F32))
                sbc = fes.enter_context(nc.sbuf_tensor("sbc", [32, TCF], F32))
                @block.vector
                def _(v):
                    cx.inc("sV", v.memset(tail[:], 0.0))

                x2_free = [0, 0]

                for tc in range(NF):
                    tsl = slice(tc * TCF, (tc + 1) * TCF)

                    @block.gpsimd
                    def _(g, tc=tc, tsl=tsl):
                        if tc > 0:
                            cx.w(g, "sT")
                            cx.w(g, "sS")
                        else:
                            cx.w(g, "qQs")
                        g.dma_start(
                            out=xb[:],
                            in_=xT[:, tsl].rearrange("(kc p) t -> p kc t", p=128),
                        ).then_inc(cx.sem["qFx"], 16)
                    cx.cnt["qFx"] += 16

                    for kg in range(4):
                        @block.scalar
                        def _(s, kg=kg):
                            cx.w(s, "qFx")
                            cx.w(s, "sT", x2_free[kg % 2])
                            for kk in range(4):
                                ii = s.activation(x2t[:, kg % 2, kk, :], xb[:, kg * 4 + kk, :], AF.Square)
                            cx.inc("sS", ii)

                        @block.tensor
                        def _(t, kg=kg):
                            cx.w(t, "qFx")
                            cx.w(t, "sS")
                            if kg == 0:
                                for b in (4, 5, 6, 7):
                                    mm_wait_bank(t, b)
                            for kk in range(4):
                                kc = kg * 4 + kk
                                for th in range(2):
                                    mm = t.matmul(pb[4 + th][0:1, :], ones_b[:, 0:1],
                                             xb[:, kc, th * 512:(th + 1) * 512],
                                             start=(kc == 0), stop=(kc == NHC - 1))
                                    mm = t.matmul(pb[6 + th][0:1, :], ones_f[:, 0:1],
                                             x2t[:, kg % 2, kk, th * 512:(th + 1) * 512],
                                             start=(kc == 0), stop=(kc == NHC - 1))
                            x2_free[kg % 2] = cx.inc("sT", mm)

                    @block.vector
                    def _(v, tc=tc):
                        cx.w(v, "sT")
                        cx.w(v, "qFr", cx.v("qFr"))
                        v.tensor_copy(rowA[0:1, 0:512], pb[4][0:1, :])
                        v.tensor_copy(rowA[0:1, 512:1024], pb[5][0:1, :])
                        v.tensor_copy(rowB[0:1, 0:512], pb[6][0:1, :])
                        v.tensor_copy(rowB[0:1, 512:1024], pb[7][0:1, :])
                        v.tensor_scalar(rowC[0:1, 0:TCF], rowA[0:1, 0:TCF], 1.0 / H, None, ALU.mult)
                        v.tensor_scalar(rowB[0:1, 0:TCF], rowB[0:1, 0:TCF], 1.0 / H, None, ALU.mult)
                        v.tensor_tensor(rowD_[0:1, 0:TCF], rowC[0:1, 0:TCF], rowC[0:1, 0:TCF], ALU.mult)
                        c = cx.inc("sV", v.tensor_tensor(rowD_[0:1, 0:TCF], rowB[0:1, 0:TCF], rowD_[0:1, 0:TCF], ALU.subtract))
                        for b in (4, 5, 6, 7):
                            bank_guard[b] = ("sV", c)

                    @block.scalar
                    def _(s):
                        cx.w(s, "sV")
                        cx.inc("sS", s.activation(rowD_[0:1, 0:TCF], rowD_[0:1, 0:TCF], AF.Sqrt,
                                                  bias=epsP[0:1, 0:1]))

                    @block.vector
                    def _(v):
                        cx.w(v, "sS")
                        v.reciprocal(rowE[0:1, 0:TCF], rowD_[0:1, 0:TCF])
                        cx.inc("sV", v.tensor_tensor(rowC[0:1, 0:TCF], rowC[0:1, 0:TCF], rowE[0:1, 0:TCF], ALU.mult))

                    @block.sync
                    def _(sync, tsl=tsl):
                        cx.w(sync, "sV")
                        sync.dma_start(out=rowsD[0, tsl], in_=rowC[0:1, 0:TCF]).then_inc(cx.sem["qFr"], 16)
                        sync.dma_start(out=rowsD[1, tsl], in_=rowE[0:1, 0:TCF]).then_inc(cx.sem["qFr"], 16)
                        cx.w(sync, "qFr", cx.v("qFr") + 32)
                        sync.dma_start(out=bcmr[:], in_=rowsD[0:1, tsl].broadcast_to((128, TCF))).then_inc(cx.sem["qFb"], 16)
                        sync.dma_start(out=bcr[:], in_=rowsD[1:2, tsl].broadcast_to((128, TCF))).then_inc(cx.sem["qFb"], 16)
                    cx.cnt["qFr"] += 32
                    cx.cnt["qFb"] += 32

                    for e in range(8):
                        for th in range(2):
                            b = (e * 2 + th) % 4
                            @block.tensor
                            def _(t, e=e, th=th, b=b):
                                mm_wait_bank(t, b)
                                for kc in range(NHC):
                                    mm = t.matmul(pb[b][:, :], win_sb[:, kc, e * 128:(e + 1) * 128],
                                             xb[:, kc, th * 512:(th + 1) * 512],
                                             start=(kc == 0), stop=(kc == NHC - 1))
                                cx.inc("sT", mm)

                            @block.vector
                            def _(v, e=e, th=th, b=b):
                                cx.w(v, "sT")
                                cx.w(v, "qFb")
                                cx.w(v, "sS")
                                sl = slice(th * 512, (th + 1) * 512)
                                v.tensor_scalar(wsmT[:], bcmr[:, sl], ws[:, e:e + 1], None, ALU.mult)
                                v.tensor_tensor(t1f[:], pb[b][:, :], bcr[:, sl], ALU.mult)
                                c = cx.inc("sV", v.tensor_tensor(proj[:, e, sl], t1f[:], wsmT[:], ALU.subtract))
                                bank_guard[b] = ("sV", c)

                    for e in range(4):
                        @block.scalar
                        def _(s, e=e):
                            cx.w(s, "sV")
                            cx.w(s, "qOsg", cx.v("qOsg"))
                            cx.inc("sS", s.activation(sg_bf[:, e, :], proj[:, 4 + e, :], AF.Silu))

                    for dc in range(NDC):
                        @block.vector
                        def _(v, dc=dc):
                            cx.w(v, "sV")
                            cx.w(v, "sS")
                            cx.w(v, "qOhs", cx.v("qOhs"))
                            p = proj[:, dc, :]
                            v.tensor_copy(pconv[:, 0:3], tail[:, dc, :])
                            v.tensor_copy(pconv[:, 3:TCF + 3], p[:, 0:TCF])
                            v.tensor_scalar(conv_f[:, 0:TCF], pconv[:, 3:TCF + 3], wcv[:, dc, 3:4], None, ALU.mult)
                            for k in (2, 1, 0):
                                v.scalar_tensor_tensor(conv_f[:, 0:TCF], pconv[:, k:k + TCF],
                                                       wcv[:, dc, k:k + 1], conv_f[:, 0:TCF],
                                                       ALU.mult, ALU.add)
                            cx.inc("sV", v.tensor_copy(tail[:, dc, :], p[:, TCF - 3:TCF]))

                        @block.scalar
                        def _(s, dc=dc):
                            cx.w(s, "sV")
                            cx.w(s, "sT")
                            cx.inc("sS", s.activation(hs_bf[:, dc, :], conv_f[:, 0:TCF], AF.Silu,
                                                      bias=bcv[:, dc:dc + 1]))

                        @block.tensor
                        def _(t, dc=dc, tc=tc):
                            cx.w(t, "sS")
                            if dc == 0:
                                if tc == 0:
                                    cx.w(t, "qWx")
                                for b in (4, 5, 6, 7):
                                    mm_wait_bank(t, b)
                            for th in range(2):
                                mm = t.matmul(pb[4 + th][:, :], wx_sb[:, dc, 0:128],
                                         hs_bf[:, dc, th * 512:(th + 1) * 512],
                                         start=(dc == 0), stop=(dc == NDC - 1))
                                mm = t.matmul(pb[6 + th][0:32, :], wx_sb[:, dc, 128:160],
                                         hs_bf[:, dc, th * 512:(th + 1) * 512],
                                         start=(dc == 0), stop=(dc == NDC - 1))
                            cx.inc("sT", mm)

                    @block.vector
                    def _(v):
                        cx.w(v, "sT")
                        cx.w(v, "qOdt", cx.v("qOdt"))
                        cx.w(v, "qObc", cx.v("qObc"))
                        v.tensor_copy(sdt[:, 0:512], pb[4][:, :])
                        v.tensor_copy(sdt[:, 512:1024], pb[5][:, :])
                        v.tensor_copy(sbc[:, 0:512], pb[6][0:32, :])
                        c = cx.inc("sV", v.tensor_copy(sbc[:, 512:1024], pb[7][0:32, :]))
                        for b in (4, 5, 6, 7):
                            bank_guard[b] = ("sV", c)

                    @block.sync
                    def _(sync, tsl=tsl):
                        cx.w(sync, "sV")
                        cx.w(sync, "sS")
                        sync.dma_start(out=arDT_in[:, tsl], in_=sdt[:, 0:TCF]).then_inc(cx.sem["qOdt"], 16)
                        sync.dma_start(out=arBC_in[:, tsl], in_=sbc[:, 0:TCF]).then_inc(cx.sem["qObc"], 16)
                        sync.dma_start(out=hsD[:, :, tsl], in_=hs_bf[:]).then_inc(cx.sem["qOhs"], 16)
                        sync.dma_start(out=sgD[:, :, tsl], in_=sg_bf[:]).then_inc(cx.sem["qOsg"], 16)
                    cx.cnt["qOdt"] += 16
                    cx.cnt["qObc"] += 16
                    cx.cnt["qOhs"] += 16
                    cx.cnt["qOsg"] += 16

                @block.gpsimd
                def _(g):
                    cx.w(g, "qOdt")
                    cx.w(g, "qObc")
                    cx.inc("sG", g.collective_compute("AllReduce", ALU.add, replica_groups=rg,
                                                      ins=[arDT_in], outs=[arDT_out]))
                    cx.inc("sG", g.collective_compute("AllReduce", ALU.add, replica_groups=rg,
                                                      ins=[arBC_in], outs=[arBC_out]))
                sG_ar = cx.v("sG")

                if debug:
                    @block.scalar
                    def _(s):
                        cx.w(s, "qOhs")
                        cx.w(s, "qOsg")
                        cx.w(s, "sG", sG_ar)
                        s.dma_start(out=dbg_hs, in_=hsD).then_inc(cx.sem["qDbg"], 16)
                        s.dma_start(out=dbg_sg, in_=sgD).then_inc(cx.sem["qDbg"], 16)
                        s.dma_start(out=dbg_dt, in_=arDT_out).then_inc(cx.sem["qDbg"], 16)
                        s.dma_start(out=dbg_bc, in_=arBC_out).then_inc(cx.sem["qDbg"], 16)
                    cx.cnt["qDbg"] += 64

        with (
            nc.sbuf_tensor("bcf32", [32, L], F32) as bcf32,
            nc.sbuf_tensor("bcb16", [32, L], BF16) as bcb16,
        ):
            @block.sync
            def _(sync):
                cx.w(sync, "sG", sG_ar)
                cx.w(sync, "qOhs")
                cx.w(sync, "qOsg")
                sync.dma_start(out=bcf32[:], in_=arBC_out).then_inc(cx.sem["qBcd"], 16)
            cx.cnt["qBcd"] += 16

            @block.vector
            def _(v):
                cx.w(v, "qBcd")
                cx.inc("sV", v.tensor_copy(bcb16[:], bcf32[:]))

            @block.sync
            def _(sync):
                cx.w(sync, "sV")
                sync.dma_start(out=bcD, in_=bcb16[:]).then_inc(cx.sem["qBcd"], 16)
            cx.cnt["qBcd"] += 16

        mes = contextlib.ExitStack()
        with mes:
            wdt_m = mes.enter_context(nc.sbuf_tensor("wdt_m", [128, DSL], BF16))
            dtin_bf = mes.enter_context(nc.sbuf_tensor("dtin_bf", [128, TCM], BF16))
            dtc = mes.enter_context(nc.sbuf_tensor("dtc", [128, NDC, TCM], F32))
            dt_bf = mes.enter_context(nc.sbuf_tensor("dt_bf", [128, NDC, TCM], BF16))
            hs_m = mes.enter_context(nc.sbuf_tensor("hs_m", [128, NDC, TCM], BF16))
            sg_m = mes.enter_context(nc.sbuf_tensor("sg_m", [128, NDC, TCM], BF16))
            u_bf = mes.enter_context(nc.sbuf_tensor("u_bf", [128, NDC, TCM], BF16))
            spz = mes.enter_context(nc.sbuf_tensor("spz", [128, 512], F32))
            spa = mes.enter_context(nc.sbuf_tensor("spa", [128, 512], F32))
            sprl = mes.enter_context(nc.sbuf_tensor("sprl", [128, 512], F32))
            Bb = mes.enter_context(nc.sbuf_tensor("Bb", [128, 2, 1024], BF16))
            Cb = mes.enter_context(nc.sbuf_tensor("Cb", [128, 2, 1024], BF16))
            dA_t = mes.enter_context(nc.sbuf_tensor("dA_t", [128, 2, NDC, 1024], BF16))
            dBu_t = mes.enter_context(nc.sbuf_tensor("dBu_t", [128, 2, 1024], BF16))
            h_t = mes.enter_context(nc.sbuf_tensor("h_t", [128, 2, 1024], BF16))
            p_t = mes.enter_context(nc.sbuf_tensor("p_t", [128, 2, NDC, 1024], BF16))
            state = mes.enter_context(nc.sbuf_tensor("state", [128, NDC * NST], F32))
            ydc = mes.enter_context(nc.sbuf_tensor("ydc", [128, 1024], F32))
            yt1 = mes.enter_context(nc.sbuf_tensor("yt1", [128, 1024], BF16))
            y2t = mes.enter_context(nc.sbuf_tensor("y2t", [128, 1024], BF16))
            @block.gpsimd
            def _(g):
                cx.w(g, "qOhs")
                cx.w(g, "qOsg")
                cx.w(g, "qBcd")
                g.dma_start(out=wdt_m[:], in_=wdtT).then_inc(cx.sem["qWdt"], 16)
            cx.cnt["qWdt"] += 16

            @block.vector
            def _(v):
                cx.inc("sV", v.memset(state[:], 0.0))

            free_dA = [0, 0]
            free_bc = [0, 0]
            free_p = [0, 0]

            qMy_half = None
            for tm in range(NM):
                if tm == NM // 2 and qMy_half is None:
                    qMy_half = cx.v("qMy")
                tsl = slice(tm * TCM, (tm + 1) * TCM)

                @block.gpsimd
                def _(g, tsl=tsl, tm=tm):
                    if tm > 0:
                        cx.w(g, "sT")
                    else:
                        cx.w(g, "sG", sG_ar)
                        cx.w(g, "qOhs")
                        cx.w(g, "qOsg")
                        cx.w(g, "qBcd")
                    g.dma_start(out=dtin_bf[:], in_=arDT_out[:, tsl]).then_inc(cx.sem["qMd"], 16)
                cx.cnt["qMd"] += 16

                @block.sync
                def _(sync, tsl=tsl, tm=tm):
                    if tm > 0:
                        cx.w(sync, "sV")
                    else:
                        cx.w(sync, "qOhs")
                        cx.w(sync, "qOsg")
                    sync.dma_start(out=hs_m[:], in_=hsD[:, :, tsl]).then_inc(cx.sem["qMh"], 16)
                    sync.dma_start(out=sg_m[:], in_=sgD[:, :, tsl]).then_inc(cx.sem["qMh"], 16)
                cx.cnt["qMh"] += 32

                for dc in range(NDC):
                    for th in range(4):
                        b = (dc * 4 + th) % 2
                        @block.tensor
                        def _(t, dc=dc, th=th, b=b, tm=tm):
                            cx.w(t, "qMd")
                            if tm == 0 and dc == 0 and th == 0:
                                cx.w(t, "qWdt")
                            mm_wait_bank(t, b)
                            mm = t.matmul(pb[b][:, :], wdt_m[:, dc * 128:(dc + 1) * 128],
                                     dtin_bf[:, th * 512:(th + 1) * 512],
                                     start=True, stop=True)
                            cx.inc("sT", mm)

                        @block.scalar
                        def _(s, dc=dc, b=b):
                            cx.w(s, "sT")
                            cx.w(s, "sV")
                            s.activation(sprl[:], pb[b][:, :], AF.Relu, bias=bdt_s[:, dc:dc + 1])
                            s.activation(spa[:], pb[b][:, :], AF.Abs, bias=bdt_s[:, dc:dc + 1])
                            s.activation(spa[:], spa[:], AF.Exp, scale=-1.0)
                            cx.inc("sS", s.activation(spa[:], spa[:], AF.Ln, bias=ones_f[:, 0:1]))

                        @block.vector
                        def _(v, dc=dc, th=th, b=b):
                            cx.w(v, "sS")
                            sl = slice(th * 512, (th + 1) * 512)
                            v.tensor_tensor(dtc[:, dc, sl], spa[:], sprl[:], ALU.add)
                            c = cx.inc("sV", v.tensor_copy(dt_bf[:, dc, sl], dtc[:, dc, sl]))
                            bank_guard[b] = ("sV", c)

                for dc in range(NDC):
                    @block.vector
                    def _(v, dc=dc):
                        cx.w(v, "qMh")
                        cx.inc("sV", v.tensor_tensor(u_bf[:, dc, :], dt_bf[:, dc, :], hs_m[:, dc, :], ALU.mult))

                if debug and tm == 0:
                    @block.sync
                    def _(sync):
                        cx.w(sync, "sV")
                        sync.dma_start(out=dbg_dtc, in_=dtc[:]).then_inc(cx.sem["qDbg"], 16)
                    cx.cnt["qDbg"] += 16

                for half in range(2):
                    hsl = slice(half * 1024, (half + 1) * 1024)
                    gof = tm * TCM + half * 1024
                    for n in range(NST):
                        par = n % 2
                        qMb = "qMb%d" % par

                        @block.sync
                        def _(sync, n=n, par=par, gof=gof, qMb=qMb, tm=tm, half=half):
                            cx.w(sync, "sV", free_bc[par])
                            if tm == 0 and half == 0 and n < 2:
                                cx.w(sync, "qBcd")
                            sync.dma_start(out=Bb[:, par, :],
                                           in_=bcD[n:n + 1, gof:gof + 1024].broadcast_to((128, 1024))).then_inc(cx.sem[qMb], 16)
                        @block.scalar
                        def _(sc_, n=n, par=par, gof=gof, qMb=qMb, tm=tm, half=half):
                            cx.w(sc_, "sV", free_bc[par])
                            if tm == 0 and half == 0 and n < 2:
                                cx.w(sc_, "qBcd")
                            sc_.dma_start(out=Cb[:, par, :],
                                          in_=bcD[16 + n:17 + n, gof:gof + 1024].broadcast_to((128, 1024))).then_inc(cx.sem[qMb], 16)
                        cx.cnt[qMb] += 32

                        @block.scalar
                        def _(s, n=n, par=par, hsl=hsl):
                            cx.w(s, "sV", free_dA[par])
                            for dc in range(NDC):
                                col = dc * NST + n
                                ii = s.activation(dA_t[:, par, dc, :], dtc[:, dc, hsl], AF.Exp,
                                                  scale=A_sb[:, col:col + 1])
                            cx.inc("sS", ii)

                        @block.vector
                        def _(v, n=n, par=par, hsl=hsl, qMb=qMb):
                            cx.w(v, "sS")
                            cx.w(v, qMb)
                            cx.w(v, "sT", free_p[par])
                            for dc in range(NDC):
                                col = dc * NST + n
                                v.tensor_tensor(dBu_t[:, par, :], u_bf[:, dc, hsl], Bb[:, par, :], ALU.mult)
                                cx.inc("sV", v.tensor_tensor_scan(
                                    h_t[:, par, :], dA_t[:, par, dc, :], dBu_t[:, par, :],
                                    state[:, col:col + 1], ALU.mult, ALU.add))
                                v.tensor_copy(state[:, col:col + 1], h_t[:, par, 1023:1024])
                                cx.inc("sV", v.tensor_tensor(p_t[:, par, dc, :], h_t[:, par, :], Cb[:, par, :], ALU.mult))
                            free_dA[par] = cx.v("sV") - 1
                            free_bc[par] = cx.v("sV")

                        for dc in range(NDC):
                            @block.tensor
                            def _(t, dc=dc, n=n, par=par):
                                cx.w(t, "sV", cx.v("sV") - (NDC - 1 - dc) * 2)
                                if n == 0:
                                    mm_wait_bank(t, 2 * dc)
                                    mm_wait_bank(t, 2 * dc + 1)
                                for q in range(2):
                                    mm = t.matmul(pb[2 * dc + q][:, :], ident_b[:],
                                             p_t[:, par, dc, q * 512:(q + 1) * 512],
                                             start=(n == 0), stop=(n == NST - 1))
                                cx.inc("sT", mm)
                        free_p[par] = cx.v("sT")

                    for dc in range(NDC):
                        @block.scalar
                        def _(s, dc=dc):
                            cx.w(s, "sT")
                            cx.w(s, "sV")
                            s.copy(ydc[:, 0:512], pb[2 * dc][:, :])
                            cx.inc("sS", s.copy(ydc[:, 512:1024], pb[2 * dc + 1][:, :]))

                        @block.vector
                        def _(v, dc=dc, hsl=hsl):
                            cx.w(v, "sS")
                            cx.w(v, "qMy", cx.v("qMy"))
                            v.tensor_copy(yt1[:], ydc[:])
                            v.scalar_tensor_tensor(yt1[:], hs_m[:, dc, hsl], dv[:, dc:dc + 1],
                                                   yt1[:], ALU.mult, ALU.add)
                            c = cx.inc("sV", v.tensor_tensor(y2t[:], yt1[:], sg_m[:, dc, hsl], ALU.mult))
                            bank_guard[2 * dc] = ("sV", c)
                            bank_guard[2 * dc + 1] = ("sV", c)

                        @block.sync
                        def _(sync, dc=dc, gof=gof):
                            cx.w(sync, "sV")
                            sync.dma_start(out=y2ag_in[dc * 128:(dc + 1) * 128, gof:gof + 1024],
                                           in_=y2t[:]).then_inc(cx.sem["qMy"], 16)
                        cx.cnt["qMy"] += 16

            if debug:
                @block.sync
                def _(sync):
                    cx.w(sync, "sV")
                    sync.dma_start(out=dbg_st, in_=state[:]).then_inc(cx.sem["qDbg"], 16)
                cx.cnt["qDbg"] += 16

            @block.gpsimd
            def _(g):
                cx.w(g, "qMy")
                cx.inc("sG", g.collective_compute("AllGather", ALU.bypass, replica_groups=rg,
                                                  ins=[y2ag_in], outs=[y2ag_out]))
            sG_y2 = cx.v("sG")

            if debug:
                @block.scalar
                def _(s):
                    cx.w(s, "sG", sG_y2)
                    s.dma_start(out=dbg_y2, in_=y2ag_in).then_inc(cx.sem["qDbg"], 16)
                cx.cnt["qDbg"] += 16

        bes = contextlib.ExitStack()
        with bes:
            x1c = bes.enter_context(nc.sbuf_tensor("x1c", [128, NHC, TCB], F32))
            x1n = bes.enter_context(nc.sbuf_tensor("x1n", [128, NHC, TCB], BF16))
            oo = bes.enter_context(nc.sbuf_tensor("oo", [128, NHC, TCB], F32))
            bA = bes.enter_context(nc.sbuf_tensor("bA", [128, TCB], F32))
            bB = bes.enter_context(nc.sbuf_tensor("bB", [128, TCB], F32))
            tf1 = bes.enter_context(nc.sbuf_tensor("tf1", [128, TCB], F32))
            sqt = bes.enter_context(nc.sbuf_tensor("sqt", [128, 2, TCB], F32))
            xtc = bes.enter_context(nc.sbuf_tensor("xtc", [128, 2, TCB], F32))
            lnbc = [0]
            def ln_rows(r_scale=None):
                @block.vector
                def _(v):
                    cx.w(v, "sT")
                    cx.w(v, "qBr", cx.v("qBr"))
                    v.tensor_copy(rowA[0:1, 0:TCB], pb[6][0:1, :])
                    v.tensor_copy(rowB[0:1, 0:TCB], pb[7][0:1, :])
                    v.tensor_scalar(rowC[0:1, 0:TCB], rowA[0:1, 0:TCB], 1.0 / H, None, ALU.mult)
                    v.tensor_scalar(rowB[0:1, 0:TCB], rowB[0:1, 0:TCB], 1.0 / H, None, ALU.mult)
                    v.tensor_tensor(rowD_[0:1, 0:TCB], rowC[0:1, 0:TCB], rowC[0:1, 0:TCB], ALU.mult)
                    c = cx.inc("sV", v.tensor_tensor(rowD_[0:1, 0:TCB], rowB[0:1, 0:TCB], rowD_[0:1, 0:TCB], ALU.subtract))
                    bank_guard[6] = ("sV", c)
                    bank_guard[7] = ("sV", c)

                @block.scalar
                def _(s):
                    cx.w(s, "sV")
                    cx.inc("sS", s.activation(rowD_[0:1, 0:TCB], rowD_[0:1, 0:TCB], AF.Sqrt,
                                              bias=epsP[0:1, 0:1]))

                @block.vector
                def _(v):
                    cx.w(v, "sS")
                    v.reciprocal(rowE[0:1, 0:TCB], rowD_[0:1, 0:TCB])
                    if r_scale is not None:
                        v.tensor_scalar(rowE[0:1, 0:TCB], rowE[0:1, 0:TCB], r_scale, None, ALU.mult)
                    cx.inc("sV", v.engine_nop())

                @block.tensor
                def _(t):
                    cx.w(t, "sV")
                    mm_wait_bank(t, 2)
                    mm_wait_bank(t, 3)
                    mm = t.matmul(pb[2][:, :], ones_r[0:1, :], rowC[0:1, 0:TCB], start=True, stop=True)
                    mm = t.matmul(pb[3][:, :], ones_r[0:1, :], rowE[0:1, 0:TCB], start=True, stop=True)
                    cx.inc("sT", mm)
                lnbc[0] = cx.v("sT")

            for tb in range(NB):
                tok0 = tb * TCB

                aes = contextlib.ExitStack()
                with aes:
                    y2b = aes.enter_context(nc.sbuf_tensor("y2b_%d" % tb, [128, 32, TCB], BF16))
                    wo_t = aes.enter_context(nc.sbuf_tensor("wo_t_%d" % tb, [128, 2, 32, 128], BF16))
                    mo = aes.enter_context(nc.sbuf_tensor("mo_%d" % tb, [128, NHC, TCB], F32))
                    @block.gpsimd
                    def _(g, tb=tb):
                        cx.w(g, "qC", qC_base)
                        if tb > 0:
                            cx.w(g, "sT")
                        else:
                            cx.w(g, "sG", sG_y2)
                        off = g.value_load(in_tensor=cid_sb[0:1, tb:tb + 1])
                        g.dma_start(
                            out=y2b[:],
                            in_=y2ag_out.rearrange("(dcg p) t -> p dcg t", p=128)[:, :, bass.ts(off, TCB)],
                        ).then_inc(cx.sem["qBy"], 16)
                    cx.cnt["qBy"] += 16

                    wo_free = [0, 0]
                    for hc in range(NHC):
                        parw = hc % 2
                        qBw = "qBw%d" % parw
                        b = hc % 2

                        @block.sync
                        def _(sync, hc=hc, parw=parw, qBw=qBw, tb=tb):
                            if hc == 0:
                                if tb == 0:
                                    cx.w(sync, "sG", sG_y2)
                                else:
                                    cx.w(sync, "sT")
                            cx.w(sync, "sT", wo_free[parw])
                            for j in range(NCORE):
                                sync.dma_start(
                                    out=wo_t[:, parw, j * 4:(j + 1) * 4, :],
                                    in_=woag_out[j * NHC + hc].rearrange("(dcw p) h -> p dcw h", p=128),
                                ).then_inc(cx.sem[qBw], 16)
                        cx.cnt[qBw] += 16 * NCORE

                        @block.tensor
                        def _(t, hc=hc, b=b, parw=parw, qBw=qBw, tb=tb):
                            cx.w(t, qBw)
                            cx.w(t, "qBy")
                            mm_wait_bank(t, b)
                            if hc == 0:
                                mm_wait_bank(t, 6)
                                mm_wait_bank(t, 7)
                                if tb == 0:
                                    cx.w(t, "sG", sG_weights)
                            for dc in range(32):
                                mm = t.matmul(pb[b][:, :], wo_t[:, parw, dc, :], y2b[:, dc, :],
                                         start=(dc == 0), stop=(dc == 31))
                            wo_free[parw] = cx.inc("sT", mm)

                        @block.scalar
                        def _(s, hc=hc, b=b):
                            cx.w(s, "sT")
                            s.copy(mo[:, hc, :], pb[b][:, :])
                            c = cx.inc("sS", s.activation(oo[:, hc, :], pb[b][:, :], AF.Square))
                            bank_guard[b] = ("sS", c)

                    @block.tensor
                    def _(t):
                        cx.w(t, "sS")
                        mm_wait_bank(t, 6)
                        mm_wait_bank(t, 7)
                        for hc in range(NHC):
                            mm = t.matmul(pb[6][0:1, :], ones_f[:, 0:1], mo[:, hc, :],
                                     start=(hc == 0), stop=(hc == NHC - 1))
                            mm = t.matmul(pb[7][0:1, :], ones_f[:, 0:1], oo[:, hc, :],
                                     start=(hc == 0), stop=(hc == NHC - 1))
                        cx.inc("sT", mm)

                    ln_rows()

                    for hc in range(NHC):
                        parx = hc % 2
                        @block.sync
                        def _(sync, hc=hc, parx=parx, tb=tb):
                            cx.w(sync, "sV")
                            sync.dma_start(
                                out=xtc[:, parx, :],
                                in_=xTs[hc, :, tok0:tok0 + TCB]).then_inc(cx.sem["qBx"], 16)
                        cx.cnt["qBx"] += 16

                        @block.vector
                        def _(v, hc=hc, parx=parx):
                            cx.w(v, "sT", lnbc[0])
                            cx.w(v, "qBx")
                            cx.w(v, "sS")
                            v.tensor_tensor(tf1[:], mo[:, hc, :], pb[2][:, :], ALU.subtract)
                            v.tensor_tensor(tf1[:], tf1[:], pb[3][:, :], ALU.mult)
                            c = cx.inc("sV", v.tensor_tensor(x1c[:, hc, :], tf1[:], xtc[:, parx, :], ALU.add))
                            bank_guard[2] = ("sV", c)
                            bank_guard[3] = ("sV", c)

                        @block.scalar
                        def _(s, hc=hc):
                            cx.w(s, "sV")
                            cx.w(s, "sT")
                            cx.inc("sS", s.activation(sqt[:, hc % 2, :], x1c[:, hc, :], AF.Square))

                        @block.tensor
                        def _(t, hc=hc):
                            cx.w(t, "sS")
                            cx.w(t, "sV")
                            if hc == 0:
                                mm_wait_bank(t, 6)
                                mm_wait_bank(t, 7)
                            mm = t.matmul(pb[6][0:1, :], ones_f[:, 0:1], x1c[:, hc, :],
                                     start=(hc == 0), stop=(hc == NHC - 1))
                            mm = t.matmul(pb[7][0:1, :], ones_f[:, 0:1], sqt[:, hc % 2, :],
                                     start=(hc == 0), stop=(hc == NHC - 1))
                            cx.inc("sT", mm)

                    @block.sync
                    def _(sync, tb=tb):
                        cx.w(sync, "sV")
                        cx.w(sync, "qBx1", cx.v("qBx1"))
                        sync.dma_start(out=x1D[:, :, tok0:tok0 + TCB], in_=x1c[:]).then_inc(cx.sem["qBx1"], 16)
                    cx.cnt["qBx1"] += 16

                    ln_rows(r_scale=scl[0:1, 2:3])

                    for hc in range(NHC):
                        @block.vector
                        def _(v, hc=hc):
                            cx.w(v, "sT", lnbc[0])
                            v.tensor_tensor(tf1[:], x1c[:, hc, :], pb[2][:, :], ALU.subtract)
                            c = cx.inc("sV", v.tensor_tensor(x1n[:, hc, :], tf1[:], pb[3][:, :], ALU.mult))
                            bank_guard[2] = ("sV", c)
                            bank_guard[3] = ("sV", c)

                    if debug and tb == 0:
                        @block.gpsimd
                        def _(g):
                            cx.w(g, "sV")
                            g.dma_start(out=dbg_x1n, in_=x1n[:]).then_inc(cx.sem["qDbg"], 16)
                            g.dma_start(out=dbg_ffn[:, 0, :], in_=x1n[:, 0, :]).then_inc(cx.sem["qDbg"], 16)
                            g.dma_start(out=dbg_ffn[:, 4, :], in_=bA[:]).then_inc(cx.sem["qDbg"], 16)
                            g.dma_start(out=dbg_ffn[:, 5, :], in_=bB[:]).then_inc(cx.sem["qDbg"], 16)
                        cx.cnt["qDbg"] += 64

                sbes = contextlib.ExitStack()
                with sbes:
                    rb = sbes.enter_context(nc.sbuf_tensor("rb_%d" % tb, [128, NHH, TCB], BF16))
                    q1t = sbes.enter_context(nc.sbuf_tensor("q1t_%d" % tb, [128, 3, NHC, 128], BF16))
                    q2t = sbes.enter_context(nc.sbuf_tensor("q2t_%d" % tb, [128, 2, 8, 128], BF16))
                    mx = sbes.enter_context(nc.sbuf_tensor("mx_%d" % tb, [128, TCB], BF16))
                    x2c = sbes.enter_context(nc.sbuf_tensor("x2c_%d" % tb, [128, TCB], BF16))
                    rxb = sbes.enter_context(nc.sbuf_tensor("rxb_%d" % tb, [128, TCB], BF16))
                    xms2 = sbes.enter_context(nc.sbuf_tensor("xms2_%d" % tb, [128, TCB], BF16))
                    qf = sbes.enter_context(nc.sbuf_tensor("qf_%d" % tb, [128, TCB], F32))
                    qi = sbes.enter_context(nc.sbuf_tensor("qi_%d" % tb, [128, TCB], I32))
                    qb = sbes.enter_context(nc.sbuf_tensor("qb_%d" % tb, [128, TCB], BF16))
                    @block.vector
                    def _(v):
                        cx.w(v, "sV")
                        cx.w(v, "sT")
                        cx.inc("sV", v.memset(mx[:], 0.0))

                    q1_free = [0, 0, 0]
                    for hh in range(NHH):
                        parq = hh % 3
                        qBq = ("qBq0", "qBq1", "qBq2")[parq]
                        b = hh % 2

                        def q1load(eng, hh=hh, parq=parq, qBq=qBq):
                            if hh < 3:
                                cx.w(eng, "sT")
                            cx.w(eng, "sT", q1_free[parq])
                            eng.dma_start(
                                out=q1t[:, parq, :, :],
                                in_=q1ag_out[hh].rearrange("kc p q -> p kc q"),
                            ).then_inc(cx.sem[qBq], 16)
                        @block.sync
                        def _(sync, f=q1load):
                            f(sync)
                        cx.cnt[qBq] += 16

                        if debug and tb == 0 and hh == 0:
                            @block.gpsimd
                            def _(g, parq=parq, qBq=qBq):
                                cx.w(g, qBq)
                                g.dma_start(out=dbg_ffn[:, 6, 0:128], in_=q1t[:, parq, 0, :]).then_inc(cx.sem["qDbg"], 16)
                                g.dma_start(out=dbg_ffn[:, 6, 128:256], in_=q1t[:, parq, 1, :]).then_inc(cx.sem["qDbg"], 16)
                            cx.cnt["qDbg"] += 32
                            dbq = cx.v("qDbg")
                            @block.tensor
                            def _(t, g=dbq):
                                cx.w(t, "qDbg", g)

                        @block.tensor
                        def _(t, hh=hh, b=b, parq=parq, qBq=qBq):
                            cx.w(t, qBq)
                            if hh == 0:
                                cx.w(t, "sV")
                            mm_wait_bank(t, b)
                            for kc in range(NHC):
                                mm = t.matmul(pb[b][:, :], q1t[:, parq, kc, :], x1n[:, kc, :],
                                         start=(kc == 0), stop=(kc == NHC - 1))
                            q1_free[parq] = cx.inc("sT", mm)

                        if debug and tb == 0 and hh == 0:
                            @block.scalar
                            def _(s, b=b):
                                cx.w(s, "sT")
                                cx.inc("sS", s.copy(qg[:], pb[b][:, :]))
                            @block.gpsimd
                            def _(g):
                                cx.w(g, "sS")
                                g.dma_start(out=dbg_ffn[:, 7, :], in_=qg[:]).then_inc(cx.sem["qDbg"], 16)
                            cx.cnt["qDbg"] += 16
                            dbq2 = cx.v("qDbg")
                            @block.vector
                            def _(v, g=dbq2):
                                cx.w(v, "qDbg", g)

                        @block.scalar
                        def _(s, hh=hh, b=b):
                            cx.w(s, "sT")
                            cx.inc("sS", s.activation(rb[:, hh, :], pb[b][:, :], AF.Relu))

                        @block.vector
                        def _(v, hh=hh, b=b):
                            cx.w(v, "sS")
                            v.tensor_tensor(x2c[:], rb[:, hh, :], rb[:, hh, :], ALU.mult)
                            c = cx.inc("sV", v.tensor_tensor(mx[:], mx[:], x2c[:], ALU.max))
                            bank_guard[b] = ("sV", c)

                    if debug and tb == 0:
                        @block.gpsimd
                        def _(g):
                            cx.w(g, "sS")
                            cx.w(g, "sV")
                            g.dma_start(out=dbg_ffn[:, 1, :], in_=rb[:, 0, :]).then_inc(cx.sem["qDbg"], 16)
                        cx.cnt["qDbg"] += 16

                    half = 64
                    while half >= 1:
                        @block.scalar
                        def _(sc_, half=half):
                            cx.w(sc_, "sV")
                            sc_.dma_start(out=qb[0:half, :], in_=mx[half:2 * half, :]).then_inc(cx.sem["qBsh"], 16)
                        cx.cnt["qBsh"] += 16

                        @block.vector
                        def _(v, half=half):
                            cx.w(v, "qBsh")
                            cx.inc("sV", v.tensor_tensor(mx[0:half, :], mx[0:half, :], qb[0:half, :], ALU.max))
                        half //= 2

                    @block.vector
                    def _(v):
                        cx.w(v, "qBr", cx.v("qBr"))
                        v.tensor_copy(rowA[0:1, 0:TCB], mx[0:1, :])
                        cx.inc("sV", v.tensor_scalar(rowA[0:1, 0:TCB], rowA[0:1, 0:TCB], EPS, None, ALU.max))

                    @block.sync
                    def _(sync):
                        cx.w(sync, "sV")
                        sync.dma_start(out=rowsD[4, 0:TCB], in_=rowA[0:1, 0:TCB]).then_inc(cx.sem["qBr"], 16)
                        cx.w(sync, "qBr", cx.v("qBr") + 16)
                        sync.dma_start(out=qf[:], in_=rowsD[4:5, 0:TCB].broadcast_to((128, TCB))).then_inc(cx.sem["qBb"], 16)
                    cx.cnt["qBr"] += 16
                    cx.cnt["qBb"] += 16

                    @block.vector
                    def _(v):
                        cx.w(v, "qBb")
                        v.tensor_scalar(xms2[:], qf[:], scl[:, 3:4], None, ALU.mult)
                        v.reciprocal(qf[:], qf[:])
                        cx.inc("sV", v.tensor_scalar(rxb[:], qf[:], 255.0, None, ALU.mult))

                    quant_done = []
                    for hh in range(NHH):
                        @block.vector
                        def _(v, hh=hh):
                            v.tensor_tensor(x2c[:], rb[:, hh, :], rb[:, hh, :], ALU.mult)
                            v.tensor_tensor(qf[:], x2c[:], rxb[:], ALU.mult)
                            v.tensor_copy(qi[:], qf[:])
                            v.tensor_copy(qb[:], qi[:])
                            v.tensor_tensor(qf[:], qf[:], qb[:], ALU.subtract)
                            v.tensor_scalar(qf[:], qf[:], 0.5, None, ALU.is_gt)
                            v.tensor_tensor(x2c[:], qb[:], qf[:], ALU.add)
                            quant_done.append(cx.inc("sV", v.tensor_tensor(rb[:, hh, :], x2c[:], xms2[:], ALU.mult)))

                    q2_free = [0, 0]
                    for dh in range(2):
                        for hh in range(NHH):
                            parq = hh % 2
                            qB2 = "qB2%d" % (parq % 2)

                            def q2load(eng, hh=hh, dh=dh, parq=parq, qB2=qB2):
                                if dh == 0 and hh < 2:
                                    cx.w(eng, "sT")
                                cx.w(eng, "sT", q2_free[parq])
                                eng.dma_start(
                                    out=q2t[:, parq, :, :],
                                    in_=q2ag_out[hh, dh * 8:(dh + 1) * 8].rearrange("dc p q -> p dc q"),
                                ).then_inc(cx.sem[qB2], 16)
                            @block.sync
                            def _(sync, f=q2load):
                                f(sync)
                            cx.cnt[qB2] += 16

                            @block.tensor
                            def _(t, hh=hh, dh=dh, parq=parq, qB2=qB2):
                                cx.w(t, qB2)
                                cx.w(t, "sV", quant_done[hh])
                                if hh == 0:
                                    for bq in range(8):
                                        mm_wait_bank(t, bq)
                                for dcw in range(8):
                                    mm = t.matmul(pb[dcw][:, :], q2t[:, parq, dcw, :], rb[:, hh, :],
                                             start=(hh == 0), stop=(hh == NHH - 1))
                                q2_free[parq] = cx.inc("sT", mm)

                        for dcw in range(8):
                            hc = dh * 8 + dcw
                            @block.scalar
                            def _(s, dcw=dcw, hc=hc):
                                cx.w(s, "sT")
                                c = cx.inc("sS", s.copy(oo[:, hc, :], pb[dcw][:, :]))
                                bank_guard[dcw] = ("sS", c)

                    for hc in range(NHC):
                        @block.scalar
                        def _(s, hc=hc):
                            cx.w(s, "sV" if hc == 0 else "sS")
                            cx.w(s, "sT")
                            cx.inc("sS", s.activation(sqt[:, hc % 2, :], oo[:, hc, :], AF.Square))

                        @block.tensor
                        def _(t, hc=hc):
                            cx.w(t, "sS")
                            if hc == 0:
                                mm_wait_bank(t, 6)
                                mm_wait_bank(t, 7)
                            mm = t.matmul(pb[6][0:1, :], ones_f[:, 0:1], oo[:, hc, :],
                                     start=(hc == 0), stop=(hc == NHC - 1))
                            mm = t.matmul(pb[7][0:1, :], ones_f[:, 0:1], sqt[:, hc % 2, :],
                                     start=(hc == 0), stop=(hc == NHC - 1))
                            cx.inc("sT", mm)

                    if debug and tb == 0:
                        @block.gpsimd
                        def _(g):
                            cx.w(g, "sS")
                            cx.w(g, "sV")
                            g.dma_start(out=dbg_ffn[:, 2, :], in_=oo[:, 0, :]).then_inc(cx.sem["qDbg"], 16)
                            g.dma_start(out=dbg_ffn[:, 3, :], in_=rb[:, 0, :]).then_inc(cx.sem["qDbg"], 16)
                        cx.cnt["qDbg"] += 32

                    ln_rows()

                ces = contextlib.ExitStack()
                with ces:
                    outF = ces.enter_context(nc.sbuf_tensor("outF_%d" % tb, [128, NHC, TCB], F32))
                    outT = ces.enter_context(nc.sbuf_tensor("outT_%d" % tb, [128, 2048], F32))
                    outf_done = [0] * NHC
                    for hc in range(NHC):
                        parx = hc % 2
                        @block.sync
                        def _(sync, hc=hc, parx=parx, tb=tb):
                            cx.w(sync, "sV")
                            if hc == 0:
                                cx.w(sync, "qBx1")
                                cx.w(sync, "sT")
                            sync.dma_start(
                                out=xtc[:, parx, :],
                                in_=x1D[:, hc, tok0:tok0 + TCB]).then_inc(cx.sem["qBx"], 16)
                        cx.cnt["qBx"] += 16

                        @block.vector
                        def _(v, hc=hc, parx=parx):
                            cx.w(v, "sT", lnbc[0])
                            cx.w(v, "qBx")
                            v.tensor_tensor(tf1[:], oo[:, hc, :], pb[2][:, :], ALU.subtract)
                            v.tensor_tensor(tf1[:], tf1[:], pb[3][:, :], ALU.mult)
                            c = cx.inc("sV", v.tensor_tensor(outF[:, hc, :], tf1[:], xtc[:, parx, :], ALU.add))
                            bank_guard[2] = ("sV", c)
                            bank_guard[3] = ("sV", c)
                            outf_done[hc] = c

                    for q in range(TCB // 128):
                        for hc in range(NHC):
                            bt = (q * NHC + hc) % 2
                            @block.tensor
                            def _(t, q=q, hc=hc, bt=bt):
                                cx.w(t, "sV", outf_done[hc])
                                mm_wait_bank(t, bt)
                                mm = t.transpose(pb[bt][:, 0:128], outF[:, hc, q * 128:(q + 1) * 128],
                                            ident_f[:])
                                cx.inc("sT", mm)

                            @block.scalar
                            def _(s, q=q, hc=hc, bt=bt):
                                cx.w(s, "sT")
                                if hc == 0:
                                    cx.w(s, "qOut", cx.v("qOut"))
                                c = cx.inc("sS", s.copy(outT[:, hc * 128:(hc + 1) * 128], pb[bt][:, 0:128]))
                                bank_guard[bt] = ("sS", c)

                        @block.sync
                        def _(sync, q=q, tb=tb):
                            cx.w(sync, "sS")
                            sync.dma_start(out=y_out[tok0 + q * 128: tok0 + (q + 1) * 128, :],
                                           in_=outT[:]).then_inc(cx.sem["qOut"], 16)
                        cx.cnt["qOut"] += 16

            if debug:
                @block.scalar
                def _(s):
                    cx.w(s, "qBx1")
                    s.dma_start(out=dbg_x1, in_=x1D).then_inc(cx.sem["qDbg"], 16)
                cx.cnt["qDbg"] += 16

            @block.sync
            def _(sync):
                cx.w(sync, "qOut")
                if debug:
                    cx.w(sync, "qDbg")

    return nc


def _host_inputs(core, inp):
    x = np.asarray(inp["L0_attn_x"])[0]
    xT = np.ascontiguousarray(x.T)
    sl_t = slice(core * TSL, (core + 1) * TSL)
    sl_d = slice(core * DSL, (core + 1) * DSL)

    Win = np.asarray(inp["L0_attn_Win"])
    win_s = np.concatenate([Win[sl_d], Win[DI + core * DSL: DI + (core + 1) * DSL]], axis=0)
    win_sT = np.ascontiguousarray(win_s.T).reshape(NHC, 128, 2 * DSL)

    Wx = np.asarray(inp["L0_attn_Wx"])
    wxT = np.ascontiguousarray(Wx[:, sl_d].T).reshape(NDC, 128, RDT + 2 * NST)
    Wdt = np.asarray(inp["L0_attn_Wdt"])
    wdtT = np.ascontiguousarray(Wdt[sl_d].T)

    Wout = np.asarray(inp["L0_attn_Wout"])
    woutT = np.ascontiguousarray(Wout.T)
    wo_slice = woutT[sl_d]
    wout_t = np.stack([np.ascontiguousarray(wo_slice[:, hc * 128:(hc + 1) * 128]) for hc in range(NHC)])

    cfc = np.asarray(inp["L0_ffn_c_fc"])
    cfc_s = cfc[:, core * HSL:(core + 1) * HSL]
    cfc_t = np.stack([
        np.stack([np.ascontiguousarray(cfc_s[kc * 128:(kc + 1) * 128, hw * 128:(hw + 1) * 128]) for kc in range(NHC)])
        for hw in range(8)
    ])
    cproj = np.asarray(inp["L0_ffn_c_proj"])
    cprojT = np.ascontiguousarray(cproj.T)
    cpT_s = cprojT[core * HSL:(core + 1) * HSL]
    cprojT_t = np.stack([
        np.stack([np.ascontiguousarray(cpT_s[hw * 128:(hw + 1) * 128, dc * 128:(dc + 1) * 128]) for dc in range(NHC)])
        for hw in range(8)
    ])

    def chunk_r(vec):
        return np.ascontiguousarray(np.asarray(vec)[sl_d].reshape(NDC, 128).T)

    wconv = np.asarray(inp["L0_attn_Wconv"])[sl_d, 0, :]
    wconv_r = np.ascontiguousarray(wconv.reshape(NDC, 128, KCV).transpose(1, 0, 2))
    alog = np.asarray(inp["L0_attn_A_log"])[sl_d]
    alog_r = np.ascontiguousarray(
        alog.reshape(NDC, 128, NST).transpose(1, 0, 2)).reshape(128, NDC * NST)

    return {
        "xT": xT,
        "xTs": np.ascontiguousarray(xT[:, sl_t].reshape(NHC, 128, TSL)),
        "win_sT": win_sT,
        "wxT": wxT,
        "wdtT": wdtT,
        "wout_t": wout_t,
        "cfc_t": cfc_t,
        "cprojT_t": cprojT_t,
        "wconv_r": wconv_r,
        "bconv_r": chunk_r(inp["L0_attn_bconv"]),
        "bdt_r": chunk_r(inp["L0_attn_bdt"]),
        "d_r": chunk_r(inp["L0_attn_D"]),
        "alog_r": alog_r,
        "iden": np.eye(128, dtype=np.float32),
        "cidx": (np.arange(NB, dtype=np.int32) + core * NB).reshape(1, NB),
    }


def kernel(**inputs):
    nc = build(debug=DEBUG)
    in_maps = [_host_inputs(c, inputs) for c in range(NCORE)]
    res = run_bass_kernel_spmd(nc, in_maps, core_ids=list(range(NCORE)))
    out = np.concatenate([res.results[c]["y_out"] for c in range(NCORE)], axis=0)
    return np.ascontiguousarray(out.reshape(1, L, H))


# version 15
# speedup vs baseline: 1.2181x; 1.2181x over previous
import contextlib

import numpy as np
import concourse.bass as bass
import concourse.mybir as mybir
from concourse.bass_utils import run_bass_kernel_spmd

F32 = mybir.dt.float32
BF16 = mybir.dt.bfloat16
I32 = mybir.dt.int32
ALU = mybir.AluOpType
AF = mybir.ActivationFunctionType

NCORE = 8
H = 2048
DI = 4096
NST = 16
KCV = 4
RDT = 128
L = 16384
DSL = DI // NCORE
TSL = L // NCORE
NHC = H // 128
NDC = DSL // 128
HF = 4 * H
NHH = HF // 128
HSL = HF // NCORE
EPS = 1e-5

TCF = 1024
NF = L // TCF
TCM = 2048
NM = L // TCM
TCB = 512
NB = TSL // TCB

DEBUG = False


class Ctx:
    def __init__(self, nc):
        self.nc = nc
        self.sem = {}
        self.cnt = {}

    def new(self, *names):
        for n in names:
            self.sem[n] = self.nc.semaphore(n).__enter__()
            self.cnt[n] = 0

    def inc(self, name, instr, k=1):
        instr.then_inc(self.sem[name], k)
        self.cnt[name] += k
        return self.cnt[name]

    def w(self, eng, name, val=None):
        eng.wait_ge(self.sem[name], self.cnt[name] if val is None else val)

    def v(self, name):
        return self.cnt[name]


def build(debug=False):
    nc = bass.Bass("TRN2", target_bir_lowering=False, debug=False, detect_race_conditions=False)

    xT = nc.dram_tensor("xT", [H, L], F32, kind="ExternalInput").ap()
    xTs = nc.dram_tensor("xTs", [NHC, 128, TSL], F32, kind="ExternalInput").ap()
    win_sT = nc.dram_tensor("win_sT", [NHC, 128, 2 * DSL], F32, kind="ExternalInput").ap()
    wxT = nc.dram_tensor("wxT", [NDC, 128, RDT + 2 * NST], F32, kind="ExternalInput").ap()
    wdtT = nc.dram_tensor("wdtT", [128, DSL], F32, kind="ExternalInput").ap()
    wout_t = nc.dram_tensor("wout_t", [NHC, DSL, 128], F32, kind="ExternalInput").ap()
    cfc_t = nc.dram_tensor("cfc_t", [8, NHC, 128, 128], F32, kind="ExternalInput").ap()
    cprojT_t = nc.dram_tensor("cprojT_t", [8, NHC, 128, 128], F32, kind="ExternalInput").ap()
    wconv_r = nc.dram_tensor("wconv_r", [128, NDC, KCV], F32, kind="ExternalInput").ap()
    bconv_r = nc.dram_tensor("bconv_r", [128, NDC], F32, kind="ExternalInput").ap()
    bdt_r = nc.dram_tensor("bdt_r", [128, NDC], F32, kind="ExternalInput").ap()
    d_r = nc.dram_tensor("d_r", [128, NDC], F32, kind="ExternalInput").ap()
    alog_r = nc.dram_tensor("alog_r", [128, NDC * NST], F32, kind="ExternalInput").ap()
    iden = nc.dram_tensor("iden", [128, 128], F32, kind="ExternalInput").ap()
    cidx = nc.dram_tensor("cidx", [1, NB], I32, kind="ExternalInput").ap()

    y_out = nc.dram_tensor("y_out", [TSL, H], F32, kind="ExternalOutput").ap()

    hsD = nc.dram_tensor("hsD", [128, NDC, L], BF16, kind="Internal").ap()
    sgD = nc.dram_tensor("sgD", [128, NDC, L], BF16, kind="Internal").ap()
    arDT_in = nc.dram_tensor("arDT_in", [128, L], F32, kind="Internal").ap()
    arDT_out = nc.dram_tensor("arDT_out", [128, L], F32, kind="Internal").ap()
    arBC_in = nc.dram_tensor("arBC_in", [32, L], F32, kind="Internal").ap()
    arBC_out = nc.dram_tensor("arBC_out", [32, L], F32, kind="Internal").ap()
    bcD = nc.dram_tensor("bcD", [32, L], BF16, kind="Internal").ap()
    arS_in = nc.dram_tensor("arS_in", [1, 8], F32, kind="Internal").ap()
    arS_out = nc.dram_tensor("arS_out", [1, 8], F32, kind="Internal").ap()
    rowsD = nc.dram_tensor("rowsD", [8, L], F32, kind="Internal").ap()
    y2ag_in = nc.dram_tensor("y2ag_in", [DSL, L], BF16, kind="Internal").ap()
    y2ag_out = nc.dram_tensor("y2ag_out", [DI, L], BF16, kind="Internal").ap()
    woag_in = nc.dram_tensor("woag_in", [NHC, DSL, 128], BF16, kind="Internal").ap()
    woag_out = nc.dram_tensor("woag_out", [NCORE * NHC, DSL, 128], BF16, kind="Internal").ap()
    q1ag_in = nc.dram_tensor("q1ag_in", [8, NHC, 128, 128], BF16, kind="Internal").ap()
    q1ag_out = nc.dram_tensor("q1ag_out", [NHH, NHC, 128, 128], BF16, kind="Internal").ap()
    q2ag_in = nc.dram_tensor("q2ag_in", [8, NHC, 128, 128], BF16, kind="Internal").ap()
    q2ag_out = nc.dram_tensor("q2ag_out", [NHH, NHC, 128, 128], BF16, kind="Internal").ap()
    x1D = nc.dram_tensor("x1D", [128, NHC, TSL], F32, kind="Internal").ap()

    if debug:
        dbg_hs = nc.dram_tensor("dbg_hs", [128, NDC, L], BF16, kind="ExternalOutput").ap()
        dbg_sg = nc.dram_tensor("dbg_sg", [128, NDC, L], BF16, kind="ExternalOutput").ap()
        dbg_dt = nc.dram_tensor("dbg_dt", [128, L], F32, kind="ExternalOutput").ap()
        dbg_bc = nc.dram_tensor("dbg_bc", [32, L], F32, kind="ExternalOutput").ap()
        dbg_y2 = nc.dram_tensor("dbg_y2", [DSL, L], BF16, kind="ExternalOutput").ap()
        dbg_x1 = nc.dram_tensor("dbg_x1", [128, NHC, TSL], F32, kind="ExternalOutput").ap()
        dbg_q1 = nc.dram_tensor("dbg_q1", [8, NHC, 128, 128], BF16, kind="ExternalOutput").ap()
        dbg_scl = nc.dram_tensor("dbg_scl", [4, 8], F32, kind="ExternalOutput").ap()
        dbg_dtc = nc.dram_tensor("dbg_dtc", [128, NDC, TCM], F32, kind="ExternalOutput").ap()
        dbg_sc = nc.dram_tensor("dbg_sc", [128, 6, 1024], BF16, kind="ExternalOutput").ap()
        dbg_st = nc.dram_tensor("dbg_st", [128, NDC * NST], F32, kind="ExternalOutput").ap()
        dbg_ffn = nc.dram_tensor("dbg_ffn", [128, 8, TCB], F32, kind="ExternalOutput").ap()
        dbg_x1n = nc.dram_tensor("dbg_x1n", [128, NHC, TCB], F32, kind="ExternalOutput").ap()

    cx = Ctx(nc)
    rg = [list(range(NCORE))]

    es = contextlib.ExitStack()
    with es:
        def sb(name, shape, dtype):
            return es.enter_context(nc.sbuf_tensor(name, shape, dtype))

        ident_f = sb("ident_f", [128, 128], F32)
        ident_b = sb("ident_b", [128, 128], BF16)
        ones_b = sb("ones_b", [128, 1], BF16)
        ones_f = sb("ones_f", [128, 1], F32)
        epsP = sb("epsP", [128, 1], F32)
        wcv = sb("wcv", [128, NDC, KCV], F32)
        bcv = sb("bcv", [128, NDC], F32)
        bdt_s = sb("bdt_s", [128, NDC], F32)
        dv = sb("dv", [128, NDC], F32)
        A_sb = sb("A_sb", [128, NDC * NST], F32)
        ws = sb("ws", [128, 8], F32)
        scl = sb("scl", [128, 4], F32)
        cid_sb = sb("cid_sb", [1, NB], I32)
        ones_r = sb("ones_r", [1, 128], F32)
        rowA = sb("rowA", [1, 1024], F32)
        rowB = sb("rowB", [1, 1024], F32)
        rowC = sb("rowC", [1, 1024], F32)
        rowD_ = sb("rowD_", [1, 1024], F32)
        rowE = sb("rowE", [1, 1024], F32)
        pb = [es.enter_context(nc.psum_tensor("pb%d" % i, [128, 512], F32)) for i in range(8)]
        block = es.enter_context(nc.Block())

        cx.new("qC", "qCa", "qCs", "qWin", "qWx", "qWo", "qBcd", "qWdt", "sT", "sV", "sS", "sG",
               "qFx", "qFr", "qFb", "qOdt", "qObc", "qOhs", "qOsg",
               "qQl", "qQs", "qSc",
               "qMd", "qMh", "qMb0", "qMb1", "qMy",
               "qBy", "qBx", "qBw0", "qBw1", "qBq0", "qBq1", "qB20", "qB21",
               "qBr", "qBb", "qBsh", "qBx1", "qOut", "qDbg", "qBq2")

        bank_guard = [("sV", 0)] * 8

        def mm_wait_bank(tensor, b):
            sem, c = bank_guard[b]
            tensor.wait_ge(cx.sem[sem], c)

        @block.sync
        def _(sync):
            sync.dma_start(out=ident_f[:], in_=iden).then_inc(cx.sem["qC"], 16)
            sync.dma_start(out=wcv[:], in_=wconv_r).then_inc(cx.sem["qC"], 16)
            sync.dma_start(out=bcv[:], in_=bconv_r).then_inc(cx.sem["qC"], 16)
            sync.dma_start(out=bdt_s[:], in_=bdt_r).then_inc(cx.sem["qC"], 16)
            sync.dma_start(out=dv[:], in_=d_r).then_inc(cx.sem["qC"], 16)
            sync.dma_start(out=cid_sb[:], in_=cidx).then_inc(cx.sem["qC"], 16)
        cx.cnt["qC"] += 6 * 16
        qC_base = cx.v("qC")

        @block.vector
        def _(v):
            v.memset(ones_b[:], 1.0)
            v.memset(ones_f[:], 1.0)
            v.memset(epsP[:], EPS)
            v.memset(ones_r[:], 1.0)
            cx.inc("sV", v.engine_nop())
            cx.w(v, "qC", qC_base)
            cx.inc("sV", v.tensor_copy(ident_b[:], ident_f[:]))
        sV_consts = cx.v("sV")

        with nc.sbuf_tensor("alog_t", [128, NDC * NST], F32) as alog_t:
            @block.sync
            def _(sync):
                sync.dma_start(out=alog_t[:], in_=alog_r).then_inc(cx.sem["qCa"], 16)
            cx.cnt["qCa"] += 16

            @block.scalar
            def _(s):
                cx.w(s, "qCa")
                cx.inc("sS", s.activation(alog_t[:], alog_t[:], AF.Exp))

            @block.vector
            def _(v):
                cx.w(v, "sS")
                cx.inc("sV", v.tensor_scalar(A_sb[:], alog_t[:], -1.0, None, ALU.mult))

        wes = contextlib.ExitStack()
        with wes:
            win_sb = wes.enter_context(nc.sbuf_tensor("win_sb", [128, NHC, 2 * DSL], BF16))
            wx_sb = wes.enter_context(nc.sbuf_tensor("wx_sb", [128, NDC, RDT + 2 * NST], BF16))
            @block.gpsimd
            def _(g):
                cx.w(g, "sV")
                cx.w(g, "sS")
                g.dma_start(out=win_sb[:], in_=win_sT.rearrange("kc p e -> p kc e")).then_inc(cx.sem["qWin"], 16)
                g.dma_start(out=wx_sb[:], in_=wxT.rearrange("dc p m -> p dc m")).then_inc(cx.sem["qWx"], 16)
            cx.cnt["qWin"] += 16
            cx.cnt["qWx"] += 16

            with nc.sbuf_tensor("wstg", [128, NHC * 4 * 128], BF16) as wstg:
                @block.gpsimd
                def _(g):
                    g.dma_start(out=wstg.rearrange("p (hc dcw h) -> p hc dcw h", hc=NHC, dcw=4),
                                in_=wout_t.rearrange("hc (dcw p) h -> p hc dcw h", p=128)).then_inc(cx.sem["qWo"], 16)
                cx.cnt["qWo"] += 16

                @block.sync
                def _(sync):
                    cx.w(sync, "qWo")
                    sync.dma_start(out=woag_in.rearrange("hc (dcw p) h -> p hc dcw h", p=128),
                                   in_=wstg.rearrange("p (hc dcw h) -> p hc dcw h", hc=NHC, dcw=4)).then_inc(cx.sem["qWo"], 16)
                cx.cnt["qWo"] += 16

            for e in range(8):
                b = e % 2
                @block.tensor
                def _(t, e=e, b=b):
                    if e == 0:
                        cx.w(t, "qWin")
                        cx.w(t, "sV", 1)
                    mm_wait_bank(t, b)
                    for kc in range(NHC):
                        mm = t.matmul(pb[b][:, 0:1], win_sb[:, kc, e * 128:(e + 1) * 128],
                                      ones_b[:, 0:1], start=(kc == 0), stop=(kc == NHC - 1))
                    cx.inc("sT", mm)

                @block.vector
                def _(v, e=e, b=b):
                    cx.w(v, "sT")
                    c = cx.inc("sV", v.tensor_copy(ws[:, e:e + 1], pb[b][:, 0:1]))
                    bank_guard[b] = ("sV", c)

            qes = contextlib.ExitStack()
            with qes:
                wq_in = qes.enter_context(nc.sbuf_tensor("wq_in", [128, 1024], F32))
                wq_abs = qes.enter_context(nc.sbuf_tensor("wq_abs", [128, 1024], F32))
                asum = qes.enter_context(nc.sbuf_tensor("asum", [128, 32], F32))
                as_tot = qes.enter_context(nc.sbuf_tensor("as_tot", [128, 2], F32))
                srow = qes.enter_context(nc.sbuf_tensor("srow", [1, 8], F32))
                wq_g1 = qes.enter_context(nc.sbuf_tensor("wq_g1", [128, 1024], F32))
                wq_g2 = qes.enter_context(nc.sbuf_tensor("wq_g2", [128, 1024], F32))
                wq_b = qes.enter_context(nc.sbuf_tensor("wq_b", [128, 1024], BF16))
                mats = [(cfc_t, q1ag_in), (cprojT_t, q2ag_in)]
                for mi, (src, dst) in enumerate(mats):
                    srcf = src.rearrange("a b p q -> (a b) p q")
                    for ch in range(16):
                        @block.sync
                        def _(sync, srcf=srcf, ch=ch):
                            cx.w(sync, "sS")
                            sync.dma_start(
                                out=wq_in.rearrange("p (a q) -> p a q", a=8),
                                in_=srcf[ch * 8:(ch + 1) * 8, :, :].rearrange("a p q -> p a q"),
                            ).then_inc(cx.sem["qQl"], 16)
                        cx.cnt["qQl"] += 16

                        @block.scalar
                        def _(s, mi=mi, ch=ch):
                            cx.w(s, "qQl")
                            cx.w(s, "sV")
                            cx.inc("sS", s.activation(wq_abs[:], wq_in[:], AF.Abs))

                        @block.vector
                        def _(v, mi=mi, ch=ch):
                            cx.w(v, "sS")
                            cx.inc("sV", v.reduce_sum(asum[:, mi * 16 + ch: mi * 16 + ch + 1],
                                                      wq_abs[:], axis=mybir.AxisListType.X))

                @block.vector
                def _(v):
                    v.memset(srow[:], 0.0)
                    v.reduce_sum(as_tot[:, 0:1], asum[:, 0:16], axis=mybir.AxisListType.X)
                    cx.inc("sV", v.reduce_sum(as_tot[:, 1:2], asum[:, 16:32], axis=mybir.AxisListType.X))

                @block.tensor
                def _(t):
                    cx.w(t, "sV")
                    for mi in range(2):
                        b = 2 + mi
                        mm_wait_bank(t, b)
                        mm = t.matmul(pb[b][0:1, 0:1], as_tot[:, mi:mi + 1], ones_f[:, 0:1],
                                 start=True, stop=True)
                    cx.inc("sT", mm)

                @block.vector
                def _(v):
                    cx.w(v, "sT")
                    v.tensor_copy(srow[0:1, 0:1], pb[2][0:1, 0:1])
                    c = cx.inc("sV", v.tensor_copy(srow[0:1, 1:2], pb[3][0:1, 0:1]))
                    bank_guard[2] = ("sV", c)
                    bank_guard[3] = ("sV", c)

                @block.sync
                def _(sync):
                    cx.w(sync, "sV")
                    sync.dma_start(out=arS_in[0:1, 0:8], in_=srow[0:1, 0:8]).then_inc(cx.sem["qSc"], 16)
                cx.cnt["qSc"] += 16

                @block.gpsimd
                def _(g):
                    cx.w(g, "qSc")
                    cx.inc("sG", g.collective_compute(
                        "AllReduce", ALU.add, replica_groups=rg,
                        ins=[arS_in], outs=[arS_out]))

                @block.sync
                def _(sync):
                    cx.w(sync, "sG")
                    sync.dma_start(out=rowA[0:1, 0:2], in_=arS_out[0:1, 0:2]).then_inc(cx.sem["qSc"], 16)
                cx.cnt["qSc"] += 16

                @block.vector
                def _(v):
                    cx.w(v, "qSc")
                    v.tensor_scalar(rowB[0:1, 0:512], rowA[0:1, 0:512], 1.0 / (2048.0 * 8192.0), None, ALU.mult)
                    v.tensor_scalar(rowC[0:1, 0:512], rowB[0:1, 0:512], EPS, None, ALU.max)
                    v.reciprocal(rowD_[0:1, 0:512], rowC[0:1, 0:512])
                    cx.inc("sV", v.tensor_scalar(rowE[0:1, 0:512], rowC[0:1, 0:512], 1.0 / 255.0, None, ALU.mult))

                @block.sync
                def _(sync):
                    cx.w(sync, "sV")
                    sync.dma_start(out=rowsD[7, 0:2], in_=rowD_[0:1, 0:2]).then_inc(cx.sem["qSc"], 16)
                    sync.dma_start(out=rowsD[7, 2:3], in_=rowC[0:1, 0:1]).then_inc(cx.sem["qSc"], 16)
                    sync.dma_start(out=rowsD[7, 3:4], in_=rowE[0:1, 1:2]).then_inc(cx.sem["qSc"], 16)
                    cx.w(sync, "qSc", cx.v("qSc") + 48)
                    sync.dma_start(out=scl[:], in_=rowsD[7:8, 0:4].broadcast_to((128, 4))).then_inc(cx.sem["qCs"], 16)
                cx.cnt["qSc"] += 48
                cx.cnt["qCs"] += 16

                for mi, (src, dst) in enumerate(mats):
                    srcf = src.rearrange("a b p q -> (a b) p q")
                    dstf = dst.rearrange("a b p q -> (a b) p q")
                    for ch in range(16):
                        @block.sync
                        def _(sync, srcf=srcf, ch=ch):
                            cx.w(sync, "sV")
                            sync.dma_start(
                                out=wq_in.rearrange("p (a q) -> p a q", a=8),
                                in_=srcf[ch * 8:(ch + 1) * 8, :, :].rearrange("a p q -> p a q"),
                            ).then_inc(cx.sem["qQl"], 16)
                        cx.cnt["qQl"] += 16

                        @block.vector
                        def _(v, mi=mi, ch=ch):
                            cx.w(v, "qQl")
                            cx.w(v, "qCs")
                            cx.w(v, "qQs", cx.v("qQs"))
                            v.tensor_scalar(wq_abs[:], wq_in[:], scl[:, mi:mi + 1], None, ALU.mult)
                            v.tensor_scalar(wq_g1[:], wq_abs[:], 0.5, None, ALU.is_gt)
                            v.tensor_scalar(wq_g2[:], wq_abs[:], -0.5, None, ALU.is_lt)
                            cx.inc("sV", v.tensor_tensor(wq_b[:], wq_g1[:], wq_g2[:], ALU.subtract))

                        @block.sync
                        def _(sync, dstf=dstf, ch=ch):
                            cx.w(sync, "sV")
                            sync.dma_start(
                                out=dstf[ch * 8:(ch + 1) * 8, :, :].rearrange("a p q -> p a q"),
                                in_=wq_b.rearrange("p (a q) -> p a q", a=8),
                            ).then_inc(cx.sem["qQs"], 16)
                        cx.cnt["qQs"] += 16

            @block.gpsimd
            def _(g):
                cx.w(g, "qQs")
                cx.w(g, "qWo")
                cx.inc("sG", g.collective_compute("AllGather", ALU.bypass, replica_groups=rg,
                                                  ins=[q1ag_in], outs=[q1ag_out]))
                cx.inc("sG", g.collective_compute("AllGather", ALU.bypass, replica_groups=rg,
                                                  ins=[q2ag_in], outs=[q2ag_out]))
                cx.inc("sG", g.collective_compute("AllGather", ALU.bypass, replica_groups=rg,
                                                  ins=[woag_in], outs=[woag_out]))
            sG_weights = cx.v("sG")

            if debug:
                @block.scalar
                def _(s):
                    cx.w(s, "sG", sG_weights)
                    s.dma_start(out=dbg_q1, in_=q1ag_in).then_inc(cx.sem["qDbg"], 16)
                    s.dma_start(out=dbg_scl[0:1, :], in_=arS_in).then_inc(cx.sem["qDbg"], 16)
                    s.dma_start(out=dbg_scl[1:2, :], in_=rowsD[7:8, 0:8]).then_inc(cx.sem["qDbg"], 16)
                    s.dma_start(out=dbg_scl[2:3, 0:4], in_=scl[0:1, :]).then_inc(cx.sem["qDbg"], 16)
                    s.dma_start(out=dbg_scl[3:4, 0:4], in_=scl[127:128, :]).then_inc(cx.sem["qDbg"], 16)
                cx.cnt["qDbg"] += 80

            fes = contextlib.ExitStack()
            with fes:
                xb = fes.enter_context(nc.sbuf_tensor("xb", [128, 2, NHC, TCF], BF16))
                x2t = fes.enter_context(nc.sbuf_tensor("x2t", [128, 2, 4, TCF], BF16))
                proj = fes.enter_context(nc.sbuf_tensor("proj", [128, 4, TCF], F32))
                projg = fes.enter_context(nc.sbuf_tensor("projg", [128, 4, TCF], BF16))
                hs_bf = fes.enter_context(nc.sbuf_tensor("hs_bf", [128, NDC, TCF], BF16))
                sg_bf = fes.enter_context(nc.sbuf_tensor("sg_bf", [128, NDC, TCF], BF16))
                conv_f = fes.enter_context(nc.sbuf_tensor("conv_f", [128, TCF], F32))
                pconv = fes.enter_context(nc.sbuf_tensor("pconv", [128, TCF + 3], F32))
                tail = fes.enter_context(nc.sbuf_tensor("tail", [128, NDC, 3], F32))
                bcmr = fes.enter_context(nc.sbuf_tensor("bcmr", [128, TCF], F32))
                bcr = fes.enter_context(nc.sbuf_tensor("bcr", [128, TCF], F32))
                wsmT = fes.enter_context(nc.sbuf_tensor("wsmT", [128, 512], F32))
                t1f = fes.enter_context(nc.sbuf_tensor("t1f", [128, 512], F32))
                sdt = fes.enter_context(nc.sbuf_tensor("sdt", [128, TCF], F32))
                sbc = fes.enter_context(nc.sbuf_tensor("sbc", [32, TCF], F32))
                @block.vector
                def _(v):
                    cx.inc("sV", v.memset(tail[:], 0.0))

                x2_free = [0, 0]
                xb_free = [(0, 0), (0, 0)]

                for tc in range(NF):
                    tsl = slice(tc * TCF, (tc + 1) * TCF)
                    xpar = tc % 2

                    @block.gpsimd
                    def _(g, tc=tc, tsl=tsl, xpar=xpar):
                        if tc > 0:
                            ft, fs = xb_free[xpar]
                            cx.w(g, "sT", ft)
                            cx.w(g, "sS", fs)
                        else:
                            cx.w(g, "qQs")
                        g.dma_start(
                            out=xb[:, xpar],
                            in_=xT[:, tsl].rearrange("(kc p) t -> p kc t", p=128),
                        ).then_inc(cx.sem["qFx"], 16)
                    cx.cnt["qFx"] += 16

                    for kg in range(4):
                        @block.scalar
                        def _(s, kg=kg, xpar=xpar):
                            cx.w(s, "qFx")
                            cx.w(s, "sT", x2_free[kg % 2])
                            for kk in range(4):
                                ii = s.activation(x2t[:, kg % 2, kk, :], xb[:, xpar, kg * 4 + kk, :], AF.Square)
                            cx.inc("sS", ii)

                        @block.tensor
                        def _(t, kg=kg, xpar=xpar):
                            cx.w(t, "qFx")
                            cx.w(t, "sS")
                            if kg == 0:
                                for b in (4, 5, 6, 7):
                                    mm_wait_bank(t, b)
                            for kk in range(4):
                                kc = kg * 4 + kk
                                for th in range(2):
                                    mm = t.matmul(pb[4 + th][0:1, :], ones_b[:, 0:1],
                                             xb[:, xpar, kc, th * 512:(th + 1) * 512],
                                             start=(kc == 0), stop=(kc == NHC - 1))
                                    mm = t.matmul(pb[6 + th][0:1, :], ones_b[0:1, 0:1].broadcast_to((128, 1)) if False else ones_b[:, 0:1],
                                             x2t[:, kg % 2, kk, th * 512:(th + 1) * 512],
                                             start=(kc == 0), stop=(kc == NHC - 1))
                            x2_free[kg % 2] = cx.inc("sT", mm)

                    @block.vector
                    def _(v, tc=tc):
                        cx.w(v, "sT")
                        cx.w(v, "qFr", cx.v("qFr"))
                        v.tensor_copy(rowA[0:1, 0:512], pb[4][0:1, :])
                        v.tensor_copy(rowA[0:1, 512:1024], pb[5][0:1, :])
                        v.tensor_copy(rowB[0:1, 0:512], pb[6][0:1, :])
                        v.tensor_copy(rowB[0:1, 512:1024], pb[7][0:1, :])
                        v.tensor_scalar(rowC[0:1, 0:TCF], rowA[0:1, 0:TCF], 1.0 / H, None, ALU.mult)
                        v.tensor_scalar(rowB[0:1, 0:TCF], rowB[0:1, 0:TCF], 1.0 / H, None, ALU.mult)
                        v.tensor_tensor(rowD_[0:1, 0:TCF], rowC[0:1, 0:TCF], rowC[0:1, 0:TCF], ALU.mult)
                        c = cx.inc("sV", v.tensor_tensor(rowD_[0:1, 0:TCF], rowB[0:1, 0:TCF], rowD_[0:1, 0:TCF], ALU.subtract))
                        for b in (4, 5, 6, 7):
                            bank_guard[b] = ("sV", c)

                    @block.scalar
                    def _(s):
                        cx.w(s, "sV")
                        cx.inc("sS", s.activation(rowD_[0:1, 0:TCF], rowD_[0:1, 0:TCF], AF.Sqrt,
                                                  bias=epsP[0:1, 0:1]))

                    @block.vector
                    def _(v):
                        cx.w(v, "sS")
                        v.reciprocal(rowE[0:1, 0:TCF], rowD_[0:1, 0:TCF])
                        cx.inc("sV", v.tensor_tensor(rowC[0:1, 0:TCF], rowC[0:1, 0:TCF], rowE[0:1, 0:TCF], ALU.mult))

                    @block.sync
                    def _(sync, tsl=tsl):
                        cx.w(sync, "sV")
                        sync.dma_start(out=rowsD[0, tsl], in_=rowC[0:1, 0:TCF]).then_inc(cx.sem["qFr"], 16)
                        sync.dma_start(out=rowsD[1, tsl], in_=rowE[0:1, 0:TCF]).then_inc(cx.sem["qFr"], 16)
                        cx.w(sync, "qFr", cx.v("qFr") + 32)
                        sync.dma_start(out=bcmr[:], in_=rowsD[0:1, tsl].broadcast_to((128, TCF))).then_inc(cx.sem["qFb"], 16)
                        sync.dma_start(out=bcr[:], in_=rowsD[1:2, tsl].broadcast_to((128, TCF))).then_inc(cx.sem["qFb"], 16)
                    cx.cnt["qFr"] += 32
                    cx.cnt["qFb"] += 32

                    sS_after_sq = cx.v("sS")
                    for e in range(8):
                        for th in range(2):
                            b = (e * 2 + th) % 4
                            @block.tensor
                            def _(t, e=e, th=th, b=b, xpar=xpar):
                                mm_wait_bank(t, b)
                                for kc in range(NHC):
                                    mm = t.matmul(pb[b][:, :], win_sb[:, kc, e * 128:(e + 1) * 128],
                                             xb[:, xpar, kc, th * 512:(th + 1) * 512],
                                             start=(kc == 0), stop=(kc == NHC - 1))
                                cx.inc("sT", mm)

                            @block.vector
                            def _(v, e=e, th=th, b=b):
                                cx.w(v, "sT")
                                cx.w(v, "qFb")
                                cx.w(v, "sS")
                                sl = slice(th * 512, (th + 1) * 512)
                                dst = proj[:, e, sl] if e < 4 else projg[:, e - 4, sl]
                                v.tensor_scalar(wsmT[:], bcmr[:, sl], ws[:, e:e + 1], None, ALU.mult)
                                v.tensor_tensor(t1f[:], pb[b][:, :], bcr[:, sl], ALU.mult)
                                c = cx.inc("sV", v.tensor_tensor(dst, t1f[:], wsmT[:], ALU.subtract))
                                bank_guard[b] = ("sV", c)

                    xb_free[xpar] = (cx.v("sT"), sS_after_sq)

                    for e in range(4):
                        @block.scalar
                        def _(s, e=e):
                            cx.w(s, "sV")
                            cx.w(s, "qOsg", cx.v("qOsg"))
                            cx.inc("sS", s.activation(sg_bf[:, e, :], projg[:, e, :], AF.Silu))

                    for dc in range(NDC):
                        @block.vector
                        def _(v, dc=dc):
                            cx.w(v, "sV")
                            cx.w(v, "sS")
                            cx.w(v, "qOhs", cx.v("qOhs"))
                            p = proj[:, dc, :]
                            v.tensor_copy(pconv[:, 0:3], tail[:, dc, :])
                            v.tensor_copy(pconv[:, 3:TCF + 3], p[:, 0:TCF])
                            v.tensor_scalar(conv_f[:, 0:TCF], pconv[:, 3:TCF + 3], wcv[:, dc, 3:4], None, ALU.mult)
                            for k in (2, 1, 0):
                                v.scalar_tensor_tensor(conv_f[:, 0:TCF], pconv[:, k:k + TCF],
                                                       wcv[:, dc, k:k + 1], conv_f[:, 0:TCF],
                                                       ALU.mult, ALU.add)
                            cx.inc("sV", v.tensor_copy(tail[:, dc, :], p[:, TCF - 3:TCF]))

                        @block.scalar
                        def _(s, dc=dc):
                            cx.w(s, "sV")
                            cx.w(s, "sT")
                            cx.inc("sS", s.activation(hs_bf[:, dc, :], conv_f[:, 0:TCF], AF.Silu,
                                                      bias=bcv[:, dc:dc + 1]))

                        @block.tensor
                        def _(t, dc=dc, tc=tc):
                            cx.w(t, "sS")
                            if dc == 0:
                                if tc == 0:
                                    cx.w(t, "qWx")
                                for b in (4, 5, 6, 7):
                                    mm_wait_bank(t, b)
                            for th in range(2):
                                mm = t.matmul(pb[4 + th][:, :], wx_sb[:, dc, 0:128],
                                         hs_bf[:, dc, th * 512:(th + 1) * 512],
                                         start=(dc == 0), stop=(dc == NDC - 1))
                                mm = t.matmul(pb[6 + th][0:32, :], wx_sb[:, dc, 128:160],
                                         hs_bf[:, dc, th * 512:(th + 1) * 512],
                                         start=(dc == 0), stop=(dc == NDC - 1))
                            cx.inc("sT", mm)

                    @block.vector
                    def _(v):
                        cx.w(v, "sT")
                        cx.w(v, "qOdt", cx.v("qOdt"))
                        cx.w(v, "qObc", cx.v("qObc"))
                        v.tensor_copy(sdt[:, 0:512], pb[4][:, :])
                        v.tensor_copy(sdt[:, 512:1024], pb[5][:, :])
                        v.tensor_copy(sbc[:, 0:512], pb[6][0:32, :])
                        c = cx.inc("sV", v.tensor_copy(sbc[:, 512:1024], pb[7][0:32, :]))
                        for b in (4, 5, 6, 7):
                            bank_guard[b] = ("sV", c)

                    @block.sync
                    def _(sync, tsl=tsl):
                        cx.w(sync, "sV")
                        cx.w(sync, "sS")
                        sync.dma_start(out=arDT_in[:, tsl], in_=sdt[:, 0:TCF]).then_inc(cx.sem["qOdt"], 16)
                        sync.dma_start(out=arBC_in[:, tsl], in_=sbc[:, 0:TCF]).then_inc(cx.sem["qObc"], 16)
                        sync.dma_start(out=hsD[:, :, tsl], in_=hs_bf[:]).then_inc(cx.sem["qOhs"], 16)
                        sync.dma_start(out=sgD[:, :, tsl], in_=sg_bf[:]).then_inc(cx.sem["qOsg"], 16)
                    cx.cnt["qOdt"] += 16
                    cx.cnt["qObc"] += 16
                    cx.cnt["qOhs"] += 16
                    cx.cnt["qOsg"] += 16

                @block.gpsimd
                def _(g):
                    cx.w(g, "qOdt")
                    cx.w(g, "qObc")
                    cx.inc("sG", g.collective_compute("AllReduce", ALU.add, replica_groups=rg,
                                                      ins=[arDT_in], outs=[arDT_out]))
                    cx.inc("sG", g.collective_compute("AllReduce", ALU.add, replica_groups=rg,
                                                      ins=[arBC_in], outs=[arBC_out]))
                sG_ar = cx.v("sG")

                if debug:
                    @block.scalar
                    def _(s):
                        cx.w(s, "qOhs")
                        cx.w(s, "qOsg")
                        cx.w(s, "sG", sG_ar)
                        s.dma_start(out=dbg_hs, in_=hsD).then_inc(cx.sem["qDbg"], 16)
                        s.dma_start(out=dbg_sg, in_=sgD).then_inc(cx.sem["qDbg"], 16)
                        s.dma_start(out=dbg_dt, in_=arDT_out).then_inc(cx.sem["qDbg"], 16)
                        s.dma_start(out=dbg_bc, in_=arBC_out).then_inc(cx.sem["qDbg"], 16)
                    cx.cnt["qDbg"] += 64

        with (
            nc.sbuf_tensor("bcf32", [32, L], F32) as bcf32,
            nc.sbuf_tensor("bcb16", [32, L], BF16) as bcb16,
        ):
            @block.sync
            def _(sync):
                cx.w(sync, "sG", sG_ar)
                cx.w(sync, "qOhs")
                cx.w(sync, "qOsg")
                sync.dma_start(out=bcf32[:], in_=arBC_out).then_inc(cx.sem["qBcd"], 16)
            cx.cnt["qBcd"] += 16

            @block.vector
            def _(v):
                cx.w(v, "qBcd")
                cx.inc("sV", v.tensor_copy(bcb16[:], bcf32[:]))

            @block.sync
            def _(sync):
                cx.w(sync, "sV")
                sync.dma_start(out=bcD, in_=bcb16[:]).then_inc(cx.sem["qBcd"], 16)
            cx.cnt["qBcd"] += 16

        mes = contextlib.ExitStack()
        with mes:
            wdt_m = mes.enter_context(nc.sbuf_tensor("wdt_m", [128, DSL], BF16))
            dtin_bf = mes.enter_context(nc.sbuf_tensor("dtin_bf", [128, TCM], BF16))
            dtc = mes.enter_context(nc.sbuf_tensor("dtc", [128, NDC, TCM], F32))
            dt_bf = mes.enter_context(nc.sbuf_tensor("dt_bf", [128, NDC, TCM], BF16))
            hs_m = mes.enter_context(nc.sbuf_tensor("hs_m", [128, NDC, TCM], BF16))
            sg_m = mes.enter_context(nc.sbuf_tensor("sg_m", [128, NDC, TCM], BF16))
            u_bf = mes.enter_context(nc.sbuf_tensor("u_bf", [128, NDC, TCM], BF16))
            spz = mes.enter_context(nc.sbuf_tensor("spz", [128, 512], F32))
            spa = mes.enter_context(nc.sbuf_tensor("spa", [128, 512], F32))
            sprl = mes.enter_context(nc.sbuf_tensor("sprl", [128, 512], F32))
            Bb = mes.enter_context(nc.sbuf_tensor("Bb", [128, 2, 1024], BF16))
            Cb = mes.enter_context(nc.sbuf_tensor("Cb", [128, 2, 1024], BF16))
            dA_t = mes.enter_context(nc.sbuf_tensor("dA_t", [128, 2, NDC, 1024], BF16))
            dBu_t = mes.enter_context(nc.sbuf_tensor("dBu_t", [128, 2, 1024], BF16))
            h_t = mes.enter_context(nc.sbuf_tensor("h_t", [128, 2, 1024], BF16))
            p_t = mes.enter_context(nc.sbuf_tensor("p_t", [128, 2, NDC, 1024], BF16))
            state = mes.enter_context(nc.sbuf_tensor("state", [128, NDC * NST], F32))
            ydc = mes.enter_context(nc.sbuf_tensor("ydc", [128, 1024], F32))
            yt1 = mes.enter_context(nc.sbuf_tensor("yt1", [128, 1024], BF16))
            y2t = mes.enter_context(nc.sbuf_tensor("y2t", [128, 1024], BF16))
            @block.gpsimd
            def _(g):
                cx.w(g, "qOhs")
                cx.w(g, "qOsg")
                cx.w(g, "qBcd")
                g.dma_start(out=wdt_m[:], in_=wdtT).then_inc(cx.sem["qWdt"], 16)
            cx.cnt["qWdt"] += 16

            @block.vector
            def _(v):
                cx.inc("sV", v.memset(state[:], 0.0))

            free_dA = [0, 0]
            free_bc = [0, 0]
            free_p = [0, 0]

            qMy_half = None
            for tm in range(NM):
                if tm == NM // 2 and qMy_half is None:
                    qMy_half = cx.v("qMy")
                tsl = slice(tm * TCM, (tm + 1) * TCM)

                @block.gpsimd
                def _(g, tsl=tsl, tm=tm):
                    if tm > 0:
                        cx.w(g, "sT")
                    else:
                        cx.w(g, "sG", sG_ar)
                        cx.w(g, "qOhs")
                        cx.w(g, "qOsg")
                        cx.w(g, "qBcd")
                    g.dma_start(out=dtin_bf[:], in_=arDT_out[:, tsl]).then_inc(cx.sem["qMd"], 16)
                cx.cnt["qMd"] += 16

                @block.sync
                def _(sync, tsl=tsl, tm=tm):
                    if tm > 0:
                        cx.w(sync, "sV")
                    else:
                        cx.w(sync, "qOhs")
                        cx.w(sync, "qOsg")
                    sync.dma_start(out=hs_m[:], in_=hsD[:, :, tsl]).then_inc(cx.sem["qMh"], 16)
                    sync.dma_start(out=sg_m[:], in_=sgD[:, :, tsl]).then_inc(cx.sem["qMh"], 16)
                cx.cnt["qMh"] += 32

                for dc in range(NDC):
                    for th in range(4):
                        b = (dc * 4 + th) % 2
                        @block.tensor
                        def _(t, dc=dc, th=th, b=b, tm=tm):
                            cx.w(t, "qMd")
                            if tm == 0 and dc == 0 and th == 0:
                                cx.w(t, "qWdt")
                            mm_wait_bank(t, b)
                            mm = t.matmul(pb[b][:, :], wdt_m[:, dc * 128:(dc + 1) * 128],
                                     dtin_bf[:, th * 512:(th + 1) * 512],
                                     start=True, stop=True)
                            cx.inc("sT", mm)

                        @block.scalar
                        def _(s, dc=dc, b=b):
                            cx.w(s, "sT")
                            cx.w(s, "sV")
                            s.activation(sprl[:], pb[b][:, :], AF.Relu, bias=bdt_s[:, dc:dc + 1])
                            s.activation(spa[:], pb[b][:, :], AF.Abs, bias=bdt_s[:, dc:dc + 1])
                            s.activation(spa[:], spa[:], AF.Exp, scale=-1.0)
                            cx.inc("sS", s.activation(spa[:], spa[:], AF.Ln, bias=ones_f[:, 0:1]))

                        @block.vector
                        def _(v, dc=dc, th=th, b=b):
                            cx.w(v, "sS")
                            sl = slice(th * 512, (th + 1) * 512)
                            v.tensor_tensor(dtc[:, dc, sl], spa[:], sprl[:], ALU.add)
                            c = cx.inc("sV", v.tensor_copy(dt_bf[:, dc, sl], dtc[:, dc, sl]))
                            bank_guard[b] = ("sV", c)

                for dc in range(NDC):
                    @block.vector
                    def _(v, dc=dc):
                        cx.w(v, "qMh")
                        cx.inc("sV", v.tensor_tensor(u_bf[:, dc, :], dt_bf[:, dc, :], hs_m[:, dc, :], ALU.mult))

                if debug and tm == 0:
                    @block.sync
                    def _(sync):
                        cx.w(sync, "sV")
                        sync.dma_start(out=dbg_dtc, in_=dtc[:]).then_inc(cx.sem["qDbg"], 16)
                    cx.cnt["qDbg"] += 16

                for half in range(2):
                    hsl = slice(half * 1024, (half + 1) * 1024)
                    gof = tm * TCM + half * 1024
                    for n in range(NST):
                        par = n % 2
                        qMb = "qMb%d" % par

                        @block.sync
                        def _(sync, n=n, par=par, gof=gof, qMb=qMb, tm=tm, half=half):
                            cx.w(sync, "sV", free_bc[par])
                            if tm == 0 and half == 0 and n < 2:
                                cx.w(sync, "qBcd")
                            sync.dma_start(out=Bb[:, par, :],
                                           in_=bcD[n:n + 1, gof:gof + 1024].broadcast_to((128, 1024))).then_inc(cx.sem[qMb], 16)
                        @block.scalar
                        def _(sc_, n=n, par=par, gof=gof, qMb=qMb, tm=tm, half=half):
                            cx.w(sc_, "sV", free_bc[par])
                            if tm == 0 and half == 0 and n < 2:
                                cx.w(sc_, "qBcd")
                            sc_.dma_start(out=Cb[:, par, :],
                                          in_=bcD[16 + n:17 + n, gof:gof + 1024].broadcast_to((128, 1024))).then_inc(cx.sem[qMb], 16)
                        cx.cnt[qMb] += 32

                        @block.scalar
                        def _(s, n=n, par=par, hsl=hsl):
                            cx.w(s, "sV", free_dA[par])
                            for dc in range(NDC):
                                col = dc * NST + n
                                ii = s.activation(dA_t[:, par, dc, :], dtc[:, dc, hsl], AF.Exp,
                                                  scale=A_sb[:, col:col + 1])
                            cx.inc("sS", ii)

                        @block.vector
                        def _(v, n=n, par=par, hsl=hsl, qMb=qMb):
                            cx.w(v, "sS")
                            cx.w(v, qMb)
                            cx.w(v, "sT", free_p[par])
                            for dc in range(NDC):
                                col = dc * NST + n
                                v.tensor_tensor(dBu_t[:, par, :], u_bf[:, dc, hsl], Bb[:, par, :], ALU.mult)
                                cx.inc("sV", v.tensor_tensor_scan(
                                    h_t[:, par, :], dA_t[:, par, dc, :], dBu_t[:, par, :],
                                    state[:, col:col + 1], ALU.mult, ALU.add))
                                v.tensor_copy(state[:, col:col + 1], h_t[:, par, 1023:1024])
                                cx.inc("sV", v.tensor_tensor(p_t[:, par, dc, :], h_t[:, par, :], Cb[:, par, :], ALU.mult))
                            free_dA[par] = cx.v("sV") - 1
                            free_bc[par] = cx.v("sV")

                        for dc in range(NDC):
                            @block.tensor
                            def _(t, dc=dc, n=n, par=par):
                                cx.w(t, "sV", cx.v("sV") - (NDC - 1 - dc) * 2)
                                if n == 0:
                                    mm_wait_bank(t, 2 * dc)
                                    mm_wait_bank(t, 2 * dc + 1)
                                for q in range(2):
                                    mm = t.matmul(pb[2 * dc + q][:, :], ident_b[:],
                                             p_t[:, par, dc, q * 512:(q + 1) * 512],
                                             start=(n == 0), stop=(n == NST - 1))
                                cx.inc("sT", mm)
                        free_p[par] = cx.v("sT")

                    for dc in range(NDC):
                        @block.scalar
                        def _(s, dc=dc):
                            cx.w(s, "sT")
                            cx.w(s, "sV")
                            s.copy(ydc[:, 0:512], pb[2 * dc][:, :])
                            cx.inc("sS", s.copy(ydc[:, 512:1024], pb[2 * dc + 1][:, :]))

                        @block.vector
                        def _(v, dc=dc, hsl=hsl):
                            cx.w(v, "sS")
                            cx.w(v, "qMy", cx.v("qMy"))
                            v.tensor_copy(yt1[:], ydc[:])
                            v.scalar_tensor_tensor(yt1[:], hs_m[:, dc, hsl], dv[:, dc:dc + 1],
                                                   yt1[:], ALU.mult, ALU.add)
                            c = cx.inc("sV", v.tensor_tensor(y2t[:], yt1[:], sg_m[:, dc, hsl], ALU.mult))
                            bank_guard[2 * dc] = ("sV", c)
                            bank_guard[2 * dc + 1] = ("sV", c)

                        @block.sync
                        def _(sync, dc=dc, gof=gof):
                            cx.w(sync, "sV")
                            sync.dma_start(out=y2ag_in[dc * 128:(dc + 1) * 128, gof:gof + 1024],
                                           in_=y2t[:]).then_inc(cx.sem["qMy"], 16)
                        cx.cnt["qMy"] += 16

            if debug:
                @block.sync
                def _(sync):
                    cx.w(sync, "sV")
                    sync.dma_start(out=dbg_st, in_=state[:]).then_inc(cx.sem["qDbg"], 16)
                cx.cnt["qDbg"] += 16

            @block.gpsimd
            def _(g):
                cx.w(g, "qMy")
                cx.inc("sG", g.collective_compute("AllGather", ALU.bypass, replica_groups=rg,
                                                  ins=[y2ag_in], outs=[y2ag_out]))
            sG_y2 = cx.v("sG")

            if debug:
                @block.scalar
                def _(s):
                    cx.w(s, "sG", sG_y2)
                    s.dma_start(out=dbg_y2, in_=y2ag_in).then_inc(cx.sem["qDbg"], 16)
                cx.cnt["qDbg"] += 16

        bes = contextlib.ExitStack()
        with bes:
            x1c = bes.enter_context(nc.sbuf_tensor("x1c", [128, NHC, TCB], F32))
            x1n = bes.enter_context(nc.sbuf_tensor("x1n", [128, NHC, TCB], BF16))
            oo = bes.enter_context(nc.sbuf_tensor("oo", [128, NHC, TCB], F32))
            bA = bes.enter_context(nc.sbuf_tensor("bA", [128, TCB], F32))
            bB = bes.enter_context(nc.sbuf_tensor("bB", [128, TCB], F32))
            tf1 = bes.enter_context(nc.sbuf_tensor("tf1", [128, TCB], F32))
            sqt = bes.enter_context(nc.sbuf_tensor("sqt", [128, 2, TCB], F32))
            xtc = bes.enter_context(nc.sbuf_tensor("xtc", [128, 2, TCB], F32))
            lnbc = [0]
            def ln_rows(r_scale=None):
                @block.vector
                def _(v):
                    cx.w(v, "sT")
                    cx.w(v, "qBr", cx.v("qBr"))
                    v.tensor_copy(rowA[0:1, 0:TCB], pb[6][0:1, :])
                    v.tensor_copy(rowB[0:1, 0:TCB], pb[7][0:1, :])
                    v.tensor_scalar(rowC[0:1, 0:TCB], rowA[0:1, 0:TCB], 1.0 / H, None, ALU.mult)
                    v.tensor_scalar(rowB[0:1, 0:TCB], rowB[0:1, 0:TCB], 1.0 / H, None, ALU.mult)
                    v.tensor_tensor(rowD_[0:1, 0:TCB], rowC[0:1, 0:TCB], rowC[0:1, 0:TCB], ALU.mult)
                    c = cx.inc("sV", v.tensor_tensor(rowD_[0:1, 0:TCB], rowB[0:1, 0:TCB], rowD_[0:1, 0:TCB], ALU.subtract))
                    bank_guard[6] = ("sV", c)
                    bank_guard[7] = ("sV", c)

                @block.scalar
                def _(s):
                    cx.w(s, "sV")
                    cx.inc("sS", s.activation(rowD_[0:1, 0:TCB], rowD_[0:1, 0:TCB], AF.Sqrt,
                                              bias=epsP[0:1, 0:1]))

                @block.vector
                def _(v):
                    cx.w(v, "sS")
                    v.reciprocal(rowE[0:1, 0:TCB], rowD_[0:1, 0:TCB])
                    if r_scale is not None:
                        v.tensor_scalar(rowE[0:1, 0:TCB], rowE[0:1, 0:TCB], r_scale, None, ALU.mult)
                    cx.inc("sV", v.engine_nop())

                @block.tensor
                def _(t):
                    cx.w(t, "sV")
                    mm_wait_bank(t, 2)
                    mm_wait_bank(t, 3)
                    mm = t.matmul(pb[2][:, :], ones_r[0:1, :], rowC[0:1, 0:TCB], start=True, stop=True)
                    mm = t.matmul(pb[3][:, :], ones_r[0:1, :], rowE[0:1, 0:TCB], start=True, stop=True)
                    cx.inc("sT", mm)
                lnbc[0] = cx.v("sT")

            for tb in range(NB):
                tok0 = tb * TCB

                aes = contextlib.ExitStack()
                with aes:
                    y2b = aes.enter_context(nc.sbuf_tensor("y2b_%d" % tb, [128, 32, TCB], BF16))
                    wo_t = aes.enter_context(nc.sbuf_tensor("wo_t_%d" % tb, [128, 2, 32, 128], BF16))
                    mo = aes.enter_context(nc.sbuf_tensor("mo_%d" % tb, [128, NHC, TCB], F32))
                    @block.gpsimd
                    def _(g, tb=tb):
                        cx.w(g, "qC", qC_base)
                        if tb > 0:
                            cx.w(g, "sT")
                        else:
                            cx.w(g, "sG", sG_y2)
                        off = g.value_load(in_tensor=cid_sb[0:1, tb:tb + 1])
                        g.dma_start(
                            out=y2b[:],
                            in_=y2ag_out.rearrange("(dcg p) t -> p dcg t", p=128)[:, :, bass.ts(off, TCB)],
                        ).then_inc(cx.sem["qBy"], 16)
                    cx.cnt["qBy"] += 16

                    wo_free = [0, 0]
                    for hc in range(NHC):
                        parw = hc % 2
                        qBw = "qBw%d" % parw
                        b = hc % 2

                        @block.sync
                        def _(sync, hc=hc, parw=parw, qBw=qBw, tb=tb):
                            if hc == 0:
                                if tb == 0:
                                    cx.w(sync, "sG", sG_y2)
                                else:
                                    cx.w(sync, "sT")
                            cx.w(sync, "sT", wo_free[parw])
                            for j in range(NCORE):
                                sync.dma_start(
                                    out=wo_t[:, parw, j * 4:(j + 1) * 4, :],
                                    in_=woag_out[j * NHC + hc].rearrange("(dcw p) h -> p dcw h", p=128),
                                ).then_inc(cx.sem[qBw], 16)
                        cx.cnt[qBw] += 16 * NCORE

                        @block.tensor
                        def _(t, hc=hc, b=b, parw=parw, qBw=qBw, tb=tb):
                            cx.w(t, qBw)
                            cx.w(t, "qBy")
                            mm_wait_bank(t, b)
                            if hc == 0:
                                mm_wait_bank(t, 6)
                                mm_wait_bank(t, 7)
                                if tb == 0:
                                    cx.w(t, "sG", sG_weights)
                            for dc in range(32):
                                mm = t.matmul(pb[b][:, :], wo_t[:, parw, dc, :], y2b[:, dc, :],
                                         start=(dc == 0), stop=(dc == 31))
                            wo_free[parw] = cx.inc("sT", mm)

                        @block.scalar
                        def _(s, hc=hc, b=b):
                            cx.w(s, "sT")
                            s.copy(mo[:, hc, :], pb[b][:, :])
                            c = cx.inc("sS", s.activation(oo[:, hc, :], pb[b][:, :], AF.Square))
                            bank_guard[b] = ("sS", c)

                    @block.tensor
                    def _(t):
                        cx.w(t, "sS")
                        mm_wait_bank(t, 6)
                        mm_wait_bank(t, 7)
                        for hc in range(NHC):
                            mm = t.matmul(pb[6][0:1, :], ones_f[:, 0:1], mo[:, hc, :],
                                     start=(hc == 0), stop=(hc == NHC - 1))
                            mm = t.matmul(pb[7][0:1, :], ones_f[:, 0:1], oo[:, hc, :],
                                     start=(hc == 0), stop=(hc == NHC - 1))
                        cx.inc("sT", mm)

                    ln_rows()

                    for hc in range(NHC):
                        parx = hc % 2
                        @block.sync
                        def _(sync, hc=hc, parx=parx, tb=tb):
                            cx.w(sync, "sV")
                            sync.dma_start(
                                out=xtc[:, parx, :],
                                in_=xTs[hc, :, tok0:tok0 + TCB]).then_inc(cx.sem["qBx"], 16)
                        cx.cnt["qBx"] += 16

                        @block.vector
                        def _(v, hc=hc, parx=parx):
                            cx.w(v, "sT", lnbc[0])
                            cx.w(v, "qBx")
                            cx.w(v, "sS")
                            v.tensor_tensor(tf1[:], mo[:, hc, :], pb[2][:, :], ALU.subtract)
                            v.tensor_tensor(tf1[:], tf1[:], pb[3][:, :], ALU.mult)
                            c = cx.inc("sV", v.tensor_tensor(x1c[:, hc, :], tf1[:], xtc[:, parx, :], ALU.add))
                            bank_guard[2] = ("sV", c)
                            bank_guard[3] = ("sV", c)

                        @block.scalar
                        def _(s, hc=hc):
                            cx.w(s, "sV")
                            cx.w(s, "sT")
                            cx.inc("sS", s.activation(sqt[:, hc % 2, :], x1c[:, hc, :], AF.Square))

                        @block.tensor
                        def _(t, hc=hc):
                            cx.w(t, "sS")
                            cx.w(t, "sV")
                            if hc == 0:
                                mm_wait_bank(t, 6)
                                mm_wait_bank(t, 7)
                            mm = t.matmul(pb[6][0:1, :], ones_f[:, 0:1], x1c[:, hc, :],
                                     start=(hc == 0), stop=(hc == NHC - 1))
                            mm = t.matmul(pb[7][0:1, :], ones_f[:, 0:1], sqt[:, hc % 2, :],
                                     start=(hc == 0), stop=(hc == NHC - 1))
                            cx.inc("sT", mm)

                    @block.sync
                    def _(sync, tb=tb):
                        cx.w(sync, "sV")
                        cx.w(sync, "qBx1", cx.v("qBx1"))
                        sync.dma_start(out=x1D[:, :, tok0:tok0 + TCB], in_=x1c[:]).then_inc(cx.sem["qBx1"], 16)
                    cx.cnt["qBx1"] += 16

                    ln_rows(r_scale=scl[0:1, 2:3])

                    for hc in range(NHC):
                        @block.vector
                        def _(v, hc=hc):
                            cx.w(v, "sT", lnbc[0])
                            v.tensor_tensor(tf1[:], x1c[:, hc, :], pb[2][:, :], ALU.subtract)
                            c = cx.inc("sV", v.tensor_tensor(x1n[:, hc, :], tf1[:], pb[3][:, :], ALU.mult))
                            bank_guard[2] = ("sV", c)
                            bank_guard[3] = ("sV", c)

                    if debug and tb == 0:
                        @block.gpsimd
                        def _(g):
                            cx.w(g, "sV")
                            g.dma_start(out=dbg_x1n, in_=x1n[:]).then_inc(cx.sem["qDbg"], 16)
                            g.dma_start(out=dbg_ffn[:, 0, :], in_=x1n[:, 0, :]).then_inc(cx.sem["qDbg"], 16)
                            g.dma_start(out=dbg_ffn[:, 4, :], in_=bA[:]).then_inc(cx.sem["qDbg"], 16)
                            g.dma_start(out=dbg_ffn[:, 5, :], in_=bB[:]).then_inc(cx.sem["qDbg"], 16)
                        cx.cnt["qDbg"] += 64

                sbes = contextlib.ExitStack()
                with sbes:
                    rb = sbes.enter_context(nc.sbuf_tensor("rb_%d" % tb, [128, NHH, TCB], BF16))
                    q1t = sbes.enter_context(nc.sbuf_tensor("q1t_%d" % tb, [128, 3, NHC, 128], BF16))
                    q2t = sbes.enter_context(nc.sbuf_tensor("q2t_%d" % tb, [128, 2, 8, 128], BF16))
                    mx = sbes.enter_context(nc.sbuf_tensor("mx_%d" % tb, [128, TCB], BF16))
                    x2c = sbes.enter_context(nc.sbuf_tensor("x2c_%d" % tb, [128, TCB], BF16))
                    rxb = sbes.enter_context(nc.sbuf_tensor("rxb_%d" % tb, [128, TCB], BF16))
                    xms2 = sbes.enter_context(nc.sbuf_tensor("xms2_%d" % tb, [128, TCB], BF16))
                    qf = sbes.enter_context(nc.sbuf_tensor("qf_%d" % tb, [128, TCB], F32))
                    qi = sbes.enter_context(nc.sbuf_tensor("qi_%d" % tb, [128, TCB], I32))
                    qb = sbes.enter_context(nc.sbuf_tensor("qb_%d" % tb, [128, TCB], BF16))
                    @block.vector
                    def _(v):
                        cx.w(v, "sV")
                        cx.w(v, "sT")
                        cx.inc("sV", v.memset(mx[:], 0.0))

                    q1_free = [0, 0, 0]
                    for hh in range(NHH):
                        parq = hh % 3
                        qBq = ("qBq0", "qBq1", "qBq2")[parq]
                        b = hh % 2

                        def q1load(eng, hh=hh, parq=parq, qBq=qBq):
                            if hh < 3:
                                cx.w(eng, "sT")
                            cx.w(eng, "sT", q1_free[parq])
                            eng.dma_start(
                                out=q1t[:, parq, :, :],
                                in_=q1ag_out[hh].rearrange("kc p q -> p kc q"),
                            ).then_inc(cx.sem[qBq], 16)
                        @block.sync
                        def _(sync, f=q1load):
                            f(sync)
                        cx.cnt[qBq] += 16

                        if debug and tb == 0 and hh == 0:
                            @block.gpsimd
                            def _(g, parq=parq, qBq=qBq):
                                cx.w(g, qBq)
                                g.dma_start(out=dbg_ffn[:, 6, 0:128], in_=q1t[:, parq, 0, :]).then_inc(cx.sem["qDbg"], 16)
                                g.dma_start(out=dbg_ffn[:, 6, 128:256], in_=q1t[:, parq, 1, :]).then_inc(cx.sem["qDbg"], 16)
                            cx.cnt["qDbg"] += 32
                            dbq = cx.v("qDbg")
                            @block.tensor
                            def _(t, g=dbq):
                                cx.w(t, "qDbg", g)

                        @block.tensor
                        def _(t, hh=hh, b=b, parq=parq, qBq=qBq):
                            cx.w(t, qBq)
                            if hh == 0:
                                cx.w(t, "sV")
                            mm_wait_bank(t, b)
                            for kc in range(NHC):
                                mm = t.matmul(pb[b][:, :], q1t[:, parq, kc, :], x1n[:, kc, :],
                                         start=(kc == 0), stop=(kc == NHC - 1))
                            q1_free[parq] = cx.inc("sT", mm)

                        if debug and tb == 0 and hh == 0:
                            @block.scalar
                            def _(s, b=b):
                                cx.w(s, "sT")
                                cx.inc("sS", s.copy(qg[:], pb[b][:, :]))
                            @block.gpsimd
                            def _(g):
                                cx.w(g, "sS")
                                g.dma_start(out=dbg_ffn[:, 7, :], in_=qg[:]).then_inc(cx.sem["qDbg"], 16)
                            cx.cnt["qDbg"] += 16
                            dbq2 = cx.v("qDbg")
                            @block.vector
                            def _(v, g=dbq2):
                                cx.w(v, "qDbg", g)

                        @block.scalar
                        def _(s, hh=hh, b=b):
                            cx.w(s, "sT")
                            cx.inc("sS", s.activation(rb[:, hh, :], pb[b][:, :], AF.Relu))

                        @block.vector
                        def _(v, hh=hh, b=b):
                            cx.w(v, "sS")
                            v.tensor_tensor(x2c[:], rb[:, hh, :], rb[:, hh, :], ALU.mult)
                            c = cx.inc("sV", v.tensor_tensor(mx[:], mx[:], x2c[:], ALU.max))
                            bank_guard[b] = ("sV", c)

                    if debug and tb == 0:
                        @block.gpsimd
                        def _(g):
                            cx.w(g, "sS")
                            cx.w(g, "sV")
                            g.dma_start(out=dbg_ffn[:, 1, :], in_=rb[:, 0, :]).then_inc(cx.sem["qDbg"], 16)
                        cx.cnt["qDbg"] += 16

                    half = 64
                    while half >= 1:
                        @block.scalar
                        def _(sc_, half=half):
                            cx.w(sc_, "sV")
                            sc_.dma_start(out=qb[0:half, :], in_=mx[half:2 * half, :]).then_inc(cx.sem["qBsh"], 16)
                        cx.cnt["qBsh"] += 16

                        @block.vector
                        def _(v, half=half):
                            cx.w(v, "qBsh")
                            cx.inc("sV", v.tensor_tensor(mx[0:half, :], mx[0:half, :], qb[0:half, :], ALU.max))
                        half //= 2

                    @block.vector
                    def _(v):
                        cx.w(v, "qBr", cx.v("qBr"))
                        v.tensor_copy(rowA[0:1, 0:TCB], mx[0:1, :])
                        cx.inc("sV", v.tensor_scalar(rowA[0:1, 0:TCB], rowA[0:1, 0:TCB], EPS, None, ALU.max))

                    @block.sync
                    def _(sync):
                        cx.w(sync, "sV")
                        sync.dma_start(out=rowsD[4, 0:TCB], in_=rowA[0:1, 0:TCB]).then_inc(cx.sem["qBr"], 16)
                        cx.w(sync, "qBr", cx.v("qBr") + 16)
                        sync.dma_start(out=qf[:], in_=rowsD[4:5, 0:TCB].broadcast_to((128, TCB))).then_inc(cx.sem["qBb"], 16)
                    cx.cnt["qBr"] += 16
                    cx.cnt["qBb"] += 16

                    @block.vector
                    def _(v):
                        cx.w(v, "qBb")
                        v.tensor_scalar(xms2[:], qf[:], scl[:, 3:4], None, ALU.mult)
                        v.reciprocal(qf[:], qf[:])
                        cx.inc("sV", v.tensor_scalar(rxb[:], qf[:], 255.0, None, ALU.mult))

                    quant_done = []
                    for hh in range(NHH):
                        @block.vector
                        def _(v, hh=hh):
                            v.tensor_tensor(x2c[:], rb[:, hh, :], rb[:, hh, :], ALU.mult)
                            v.tensor_tensor(qf[:], x2c[:], rxb[:], ALU.mult)
                            v.tensor_copy(qi[:], qf[:])
                            v.tensor_copy(qb[:], qi[:])
                            v.tensor_tensor(qf[:], qf[:], qb[:], ALU.subtract)
                            v.tensor_scalar(qf[:], qf[:], 0.5, None, ALU.is_gt)
                            v.tensor_tensor(x2c[:], qb[:], qf[:], ALU.add)
                            quant_done.append(cx.inc("sV", v.tensor_tensor(rb[:, hh, :], x2c[:], xms2[:], ALU.mult)))

                    q2_free = [0, 0]
                    for dh in range(2):
                        for hh in range(NHH):
                            parq = hh % 2
                            qB2 = "qB2%d" % (parq % 2)

                            def q2load(eng, hh=hh, dh=dh, parq=parq, qB2=qB2):
                                if dh == 0 and hh < 2:
                                    cx.w(eng, "sT")
                                cx.w(eng, "sT", q2_free[parq])
                                eng.dma_start(
                                    out=q2t[:, parq, :, :],
                                    in_=q2ag_out[hh, dh * 8:(dh + 1) * 8].rearrange("dc p q -> p dc q"),
                                ).then_inc(cx.sem[qB2], 16)
                            @block.sync
                            def _(sync, f=q2load):
                                f(sync)
                            cx.cnt[qB2] += 16

                            @block.tensor
                            def _(t, hh=hh, dh=dh, parq=parq, qB2=qB2):
                                cx.w(t, qB2)
                                cx.w(t, "sV", quant_done[hh])
                                if hh == 0:
                                    for bq in range(8):
                                        mm_wait_bank(t, bq)
                                for dcw in range(8):
                                    mm = t.matmul(pb[dcw][:, :], q2t[:, parq, dcw, :], rb[:, hh, :],
                                             start=(hh == 0), stop=(hh == NHH - 1))
                                q2_free[parq] = cx.inc("sT", mm)

                        for dcw in range(8):
                            hc = dh * 8 + dcw
                            @block.scalar
                            def _(s, dcw=dcw, hc=hc):
                                cx.w(s, "sT")
                                c = cx.inc("sS", s.copy(oo[:, hc, :], pb[dcw][:, :]))
                                bank_guard[dcw] = ("sS", c)

                    for hc in range(NHC):
                        @block.scalar
                        def _(s, hc=hc):
                            cx.w(s, "sV" if hc == 0 else "sS")
                            cx.w(s, "sT")
                            cx.inc("sS", s.activation(sqt[:, hc % 2, :], oo[:, hc, :], AF.Square))

                        @block.tensor
                        def _(t, hc=hc):
                            cx.w(t, "sS")
                            if hc == 0:
                                mm_wait_bank(t, 6)
                                mm_wait_bank(t, 7)
                            mm = t.matmul(pb[6][0:1, :], ones_f[:, 0:1], oo[:, hc, :],
                                     start=(hc == 0), stop=(hc == NHC - 1))
                            mm = t.matmul(pb[7][0:1, :], ones_f[:, 0:1], sqt[:, hc % 2, :],
                                     start=(hc == 0), stop=(hc == NHC - 1))
                            cx.inc("sT", mm)

                    if debug and tb == 0:
                        @block.gpsimd
                        def _(g):
                            cx.w(g, "sS")
                            cx.w(g, "sV")
                            g.dma_start(out=dbg_ffn[:, 2, :], in_=oo[:, 0, :]).then_inc(cx.sem["qDbg"], 16)
                            g.dma_start(out=dbg_ffn[:, 3, :], in_=rb[:, 0, :]).then_inc(cx.sem["qDbg"], 16)
                        cx.cnt["qDbg"] += 32

                    ln_rows()

                ces = contextlib.ExitStack()
                with ces:
                    outF = ces.enter_context(nc.sbuf_tensor("outF_%d" % tb, [128, NHC, TCB], F32))
                    outT = ces.enter_context(nc.sbuf_tensor("outT_%d" % tb, [128, 2048], F32))
                    outf_done = [0] * NHC
                    for hc in range(NHC):
                        parx = hc % 2
                        @block.sync
                        def _(sync, hc=hc, parx=parx, tb=tb):
                            cx.w(sync, "sV")
                            if hc == 0:
                                cx.w(sync, "qBx1")
                                cx.w(sync, "sT")
                            sync.dma_start(
                                out=xtc[:, parx, :],
                                in_=x1D[:, hc, tok0:tok0 + TCB]).then_inc(cx.sem["qBx"], 16)
                        cx.cnt["qBx"] += 16

                        @block.vector
                        def _(v, hc=hc, parx=parx):
                            cx.w(v, "sT", lnbc[0])
                            cx.w(v, "qBx")
                            v.tensor_tensor(tf1[:], oo[:, hc, :], pb[2][:, :], ALU.subtract)
                            v.tensor_tensor(tf1[:], tf1[:], pb[3][:, :], ALU.mult)
                            c = cx.inc("sV", v.tensor_tensor(outF[:, hc, :], tf1[:], xtc[:, parx, :], ALU.add))
                            bank_guard[2] = ("sV", c)
                            bank_guard[3] = ("sV", c)
                            outf_done[hc] = c

                    for q in range(TCB // 128):
                        for hc in range(NHC):
                            bt = (q * NHC + hc) % 2
                            @block.tensor
                            def _(t, q=q, hc=hc, bt=bt):
                                cx.w(t, "sV", outf_done[hc])
                                mm_wait_bank(t, bt)
                                mm = t.transpose(pb[bt][:, 0:128], outF[:, hc, q * 128:(q + 1) * 128],
                                            ident_f[:])
                                cx.inc("sT", mm)

                            @block.scalar
                            def _(s, q=q, hc=hc, bt=bt):
                                cx.w(s, "sT")
                                if hc == 0:
                                    cx.w(s, "qOut", cx.v("qOut"))
                                c = cx.inc("sS", s.copy(outT[:, hc * 128:(hc + 1) * 128], pb[bt][:, 0:128]))
                                bank_guard[bt] = ("sS", c)

                        @block.sync
                        def _(sync, q=q, tb=tb):
                            cx.w(sync, "sS")
                            sync.dma_start(out=y_out[tok0 + q * 128: tok0 + (q + 1) * 128, :],
                                           in_=outT[:]).then_inc(cx.sem["qOut"], 16)
                        cx.cnt["qOut"] += 16

            if debug:
                @block.scalar
                def _(s):
                    cx.w(s, "qBx1")
                    s.dma_start(out=dbg_x1, in_=x1D).then_inc(cx.sem["qDbg"], 16)
                cx.cnt["qDbg"] += 16

            @block.sync
            def _(sync):
                cx.w(sync, "qOut")
                if debug:
                    cx.w(sync, "qDbg")

    return nc


def _host_inputs(core, inp):
    x = np.asarray(inp["L0_attn_x"])[0]
    xT = np.ascontiguousarray(x.T)
    sl_t = slice(core * TSL, (core + 1) * TSL)
    sl_d = slice(core * DSL, (core + 1) * DSL)

    Win = np.asarray(inp["L0_attn_Win"])
    win_s = np.concatenate([Win[sl_d], Win[DI + core * DSL: DI + (core + 1) * DSL]], axis=0)
    win_sT = np.ascontiguousarray(win_s.T).reshape(NHC, 128, 2 * DSL)

    Wx = np.asarray(inp["L0_attn_Wx"])
    wxT = np.ascontiguousarray(Wx[:, sl_d].T).reshape(NDC, 128, RDT + 2 * NST)
    Wdt = np.asarray(inp["L0_attn_Wdt"])
    wdtT = np.ascontiguousarray(Wdt[sl_d].T)

    Wout = np.asarray(inp["L0_attn_Wout"])
    woutT = np.ascontiguousarray(Wout.T)
    wo_slice = woutT[sl_d]
    wout_t = np.stack([np.ascontiguousarray(wo_slice[:, hc * 128:(hc + 1) * 128]) for hc in range(NHC)])

    cfc = np.asarray(inp["L0_ffn_c_fc"])
    cfc_s = cfc[:, core * HSL:(core + 1) * HSL]
    cfc_t = np.stack([
        np.stack([np.ascontiguousarray(cfc_s[kc * 128:(kc + 1) * 128, hw * 128:(hw + 1) * 128]) for kc in range(NHC)])
        for hw in range(8)
    ])
    cproj = np.asarray(inp["L0_ffn_c_proj"])
    cprojT = np.ascontiguousarray(cproj.T)
    cpT_s = cprojT[core * HSL:(core + 1) * HSL]
    cprojT_t = np.stack([
        np.stack([np.ascontiguousarray(cpT_s[hw * 128:(hw + 1) * 128, dc * 128:(dc + 1) * 128]) for dc in range(NHC)])
        for hw in range(8)
    ])

    def chunk_r(vec):
        return np.ascontiguousarray(np.asarray(vec)[sl_d].reshape(NDC, 128).T)

    wconv = np.asarray(inp["L0_attn_Wconv"])[sl_d, 0, :]
    wconv_r = np.ascontiguousarray(wconv.reshape(NDC, 128, KCV).transpose(1, 0, 2))
    alog = np.asarray(inp["L0_attn_A_log"])[sl_d]
    alog_r = np.ascontiguousarray(
        alog.reshape(NDC, 128, NST).transpose(1, 0, 2)).reshape(128, NDC * NST)

    return {
        "xT": xT,
        "xTs": np.ascontiguousarray(xT[:, sl_t].reshape(NHC, 128, TSL)),
        "win_sT": win_sT,
        "wxT": wxT,
        "wdtT": wdtT,
        "wout_t": wout_t,
        "cfc_t": cfc_t,
        "cprojT_t": cprojT_t,
        "wconv_r": wconv_r,
        "bconv_r": chunk_r(inp["L0_attn_bconv"]),
        "bdt_r": chunk_r(inp["L0_attn_bdt"]),
        "d_r": chunk_r(inp["L0_attn_D"]),
        "alog_r": alog_r,
        "iden": np.eye(128, dtype=np.float32),
        "cidx": (np.arange(NB, dtype=np.int32) + core * NB).reshape(1, NB),
    }


def kernel(**inputs):
    nc = build(debug=DEBUG)
    in_maps = [_host_inputs(c, inputs) for c in range(NCORE)]
    res = run_bass_kernel_spmd(nc, in_maps, core_ids=list(range(NCORE)))
    out = np.concatenate([res.results[c]["y_out"] for c in range(NCORE)], axis=0)
    return np.ascontiguousarray(out.reshape(1, L, H))


# version 18 (final)
# speedup vs baseline: 1.2411x; 1.0189x over previous
import contextlib

import numpy as np
import concourse.bass as bass
import concourse.mybir as mybir
from concourse.bass_utils import run_bass_kernel_spmd

F32 = mybir.dt.float32
BF16 = mybir.dt.bfloat16
I32 = mybir.dt.int32
ALU = mybir.AluOpType
AF = mybir.ActivationFunctionType

NCORE = 8
H = 2048
DI = 4096
NST = 16
KCV = 4
RDT = 128
L = 16384
DSL = DI // NCORE
TSL = L // NCORE
NHC = H // 128
NDC = DSL // 128
HF = 4 * H
NHH = HF // 128
HSL = HF // NCORE
EPS = 1e-5

TCF = 1024
NF = L // TCF
TCM = 2048
NM = L // TCM
TCB = 512
NB = TSL // TCB

DEBUG = False


class Ctx:
    def __init__(self, nc):
        self.nc = nc
        self.sem = {}
        self.cnt = {}

    def new(self, *names):
        for n in names:
            self.sem[n] = self.nc.semaphore(n).__enter__()
            self.cnt[n] = 0

    def inc(self, name, instr, k=1):
        instr.then_inc(self.sem[name], k)
        self.cnt[name] += k
        return self.cnt[name]

    def w(self, eng, name, val=None):
        eng.wait_ge(self.sem[name], self.cnt[name] if val is None else val)

    def v(self, name):
        return self.cnt[name]


def build(debug=False):
    nc = bass.Bass("TRN2", target_bir_lowering=False, debug=False, detect_race_conditions=False)

    xT = nc.dram_tensor("xT", [H, L], F32, kind="ExternalInput").ap()
    xTs = nc.dram_tensor("xTs", [NHC, 128, TSL], F32, kind="ExternalInput").ap()
    win_sT = nc.dram_tensor("win_sT", [NHC, 128, 2 * DSL], F32, kind="ExternalInput").ap()
    wxT = nc.dram_tensor("wxT", [NDC, 128, RDT + 2 * NST], F32, kind="ExternalInput").ap()
    wdtT = nc.dram_tensor("wdtT", [128, DSL], F32, kind="ExternalInput").ap()
    wout_t = nc.dram_tensor("wout_t", [NHC, DSL, 128], F32, kind="ExternalInput").ap()
    cfc_t = nc.dram_tensor("cfc_t", [8, NHC, 128, 128], F32, kind="ExternalInput").ap()
    cprojT_t = nc.dram_tensor("cprojT_t", [8, NHC, 128, 128], F32, kind="ExternalInput").ap()
    wconv_r = nc.dram_tensor("wconv_r", [128, NDC, KCV], F32, kind="ExternalInput").ap()
    bconv_r = nc.dram_tensor("bconv_r", [128, NDC], F32, kind="ExternalInput").ap()
    bdt_r = nc.dram_tensor("bdt_r", [128, NDC], F32, kind="ExternalInput").ap()
    d_r = nc.dram_tensor("d_r", [128, NDC], F32, kind="ExternalInput").ap()
    alog_r = nc.dram_tensor("alog_r", [128, NDC * NST], F32, kind="ExternalInput").ap()
    iden = nc.dram_tensor("iden", [128, 128], F32, kind="ExternalInput").ap()
    cidx = nc.dram_tensor("cidx", [1, NB], I32, kind="ExternalInput").ap()

    y_out = nc.dram_tensor("y_out", [TSL, H], F32, kind="ExternalOutput").ap()

    hsD = nc.dram_tensor("hsD", [128, NDC, L], BF16, kind="Internal").ap()
    sgD = nc.dram_tensor("sgD", [128, NDC, L], BF16, kind="Internal").ap()
    arDT_in = nc.dram_tensor("arDT_in", [128, L], F32, kind="Internal").ap()
    arDT_out = nc.dram_tensor("arDT_out", [128, L], F32, kind="Internal").ap()
    arBC_in = nc.dram_tensor("arBC_in", [32, L], F32, kind="Internal").ap()
    arBC_out = nc.dram_tensor("arBC_out", [32, L], F32, kind="Internal").ap()
    bcD = nc.dram_tensor("bcD", [32, L], BF16, kind="Internal").ap()
    arS_in = nc.dram_tensor("arS_in", [1, 8], F32, kind="Internal").ap()
    arS_out = nc.dram_tensor("arS_out", [1, 8], F32, kind="Internal").ap()
    rowsD = nc.dram_tensor("rowsD", [8, L], F32, kind="Internal").ap()
    y2ag_in = nc.dram_tensor("y2ag_in", [DSL, L], BF16, kind="Internal").ap()
    y2ag_out = nc.dram_tensor("y2ag_out", [DI, L], BF16, kind="Internal").ap()
    woag_in = nc.dram_tensor("woag_in", [NHC, DSL, 128], BF16, kind="Internal").ap()
    woag_out = nc.dram_tensor("woag_out", [NCORE * NHC, DSL, 128], BF16, kind="Internal").ap()
    q1ag_in = nc.dram_tensor("q1ag_in", [8, NHC, 128, 128], BF16, kind="Internal").ap()
    q1ag_out = nc.dram_tensor("q1ag_out", [NHH, NHC, 128, 128], BF16, kind="Internal").ap()
    q2ag_in = nc.dram_tensor("q2ag_in", [8, NHC, 128, 128], BF16, kind="Internal").ap()
    q2ag_out = nc.dram_tensor("q2ag_out", [NHH, NHC, 128, 128], BF16, kind="Internal").ap()
    x1D = nc.dram_tensor("x1D", [128, NHC, TSL], F32, kind="Internal").ap()

    if debug:
        dbg_hs = nc.dram_tensor("dbg_hs", [128, NDC, L], BF16, kind="ExternalOutput").ap()
        dbg_sg = nc.dram_tensor("dbg_sg", [128, NDC, L], BF16, kind="ExternalOutput").ap()
        dbg_dt = nc.dram_tensor("dbg_dt", [128, L], F32, kind="ExternalOutput").ap()
        dbg_bc = nc.dram_tensor("dbg_bc", [32, L], F32, kind="ExternalOutput").ap()
        dbg_y2 = nc.dram_tensor("dbg_y2", [DSL, L], BF16, kind="ExternalOutput").ap()
        dbg_x1 = nc.dram_tensor("dbg_x1", [128, NHC, TSL], F32, kind="ExternalOutput").ap()
        dbg_q1 = nc.dram_tensor("dbg_q1", [8, NHC, 128, 128], BF16, kind="ExternalOutput").ap()
        dbg_scl = nc.dram_tensor("dbg_scl", [4, 8], F32, kind="ExternalOutput").ap()
        dbg_dtc = nc.dram_tensor("dbg_dtc", [128, NDC, TCM], F32, kind="ExternalOutput").ap()
        dbg_sc = nc.dram_tensor("dbg_sc", [128, 6, 1024], BF16, kind="ExternalOutput").ap()
        dbg_st = nc.dram_tensor("dbg_st", [128, NDC * NST], F32, kind="ExternalOutput").ap()
        dbg_ffn = nc.dram_tensor("dbg_ffn", [128, 8, TCB], F32, kind="ExternalOutput").ap()
        dbg_x1n = nc.dram_tensor("dbg_x1n", [128, NHC, TCB], F32, kind="ExternalOutput").ap()

    cx = Ctx(nc)
    rg = [list(range(NCORE))]

    es = contextlib.ExitStack()
    with es:
        def sb(name, shape, dtype):
            return es.enter_context(nc.sbuf_tensor(name, shape, dtype))

        ident_f = sb("ident_f", [128, 128], F32)
        ident_b = sb("ident_b", [128, 128], BF16)
        ones_b = sb("ones_b", [128, 1], BF16)
        ones_f = sb("ones_f", [128, 1], F32)
        epsP = sb("epsP", [128, 1], F32)
        wcv = sb("wcv", [128, NDC, KCV], F32)
        bcv = sb("bcv", [128, NDC], F32)
        bdt_s = sb("bdt_s", [128, NDC], F32)
        dv = sb("dv", [128, NDC], F32)
        A_sb = sb("A_sb", [128, NDC * NST], F32)
        ws = sb("ws", [128, 8], F32)
        scl = sb("scl", [128, 4], F32)
        cid_sb = sb("cid_sb", [1, NB], I32)
        ones_r = sb("ones_r", [1, 128], F32)
        rowA = sb("rowA", [1, 1024], F32)
        rowB = sb("rowB", [1, 1024], F32)
        rowC = sb("rowC", [1, 1024], F32)
        rowD_ = sb("rowD_", [1, 1024], F32)
        rowE = sb("rowE", [1, 1024], F32)
        pb = [es.enter_context(nc.psum_tensor("pb%d" % i, [128, 512], F32)) for i in range(8)]
        block = es.enter_context(nc.Block())

        cx.new("qC", "qCa", "qCs", "qWin", "qWx", "qWo", "qBcd", "qWdt", "sT", "sV", "sS", "sG",
               "qFx", "qFr", "qFb", "qOdt", "qObc", "qOhs", "qOsg",
               "qQl", "qQs", "qSc",
               "qMd", "qMh", "qMb0", "qMb1", "qMy",
               "qBy", "qBx", "qBw0", "qBw1", "qBq0", "qBq1", "qB20", "qB21",
               "qBr", "qBb", "qBsh", "qBx1", "qOut", "qDbg", "qBq2")

        bank_guard = [("sV", 0)] * 8

        def mm_wait_bank(tensor, b):
            sem, c = bank_guard[b]
            tensor.wait_ge(cx.sem[sem], c)

        @block.sync
        def _(sync):
            sync.dma_start(out=ident_f[:], in_=iden).then_inc(cx.sem["qC"], 16)
            sync.dma_start(out=wcv[:], in_=wconv_r).then_inc(cx.sem["qC"], 16)
            sync.dma_start(out=bcv[:], in_=bconv_r).then_inc(cx.sem["qC"], 16)
            sync.dma_start(out=bdt_s[:], in_=bdt_r).then_inc(cx.sem["qC"], 16)
            sync.dma_start(out=dv[:], in_=d_r).then_inc(cx.sem["qC"], 16)
            sync.dma_start(out=cid_sb[:], in_=cidx).then_inc(cx.sem["qC"], 16)
        cx.cnt["qC"] += 6 * 16
        qC_base = cx.v("qC")

        @block.vector
        def _(v):
            v.memset(ones_b[:], 1.0)
            v.memset(ones_f[:], 1.0)
            v.memset(epsP[:], EPS)
            v.memset(ones_r[:], 1.0)
            cx.inc("sV", v.engine_nop())
            cx.w(v, "qC", qC_base)
            cx.inc("sV", v.tensor_copy(ident_b[:], ident_f[:]))
        sV_consts = cx.v("sV")

        with nc.sbuf_tensor("alog_t", [128, NDC * NST], F32) as alog_t:
            @block.sync
            def _(sync):
                sync.dma_start(out=alog_t[:], in_=alog_r).then_inc(cx.sem["qCa"], 16)
            cx.cnt["qCa"] += 16

            @block.scalar
            def _(s):
                cx.w(s, "qCa")
                cx.inc("sS", s.activation(alog_t[:], alog_t[:], AF.Exp))

            @block.vector
            def _(v):
                cx.w(v, "sS")
                cx.inc("sV", v.tensor_scalar(A_sb[:], alog_t[:], -1.0, None, ALU.mult))

        wes = contextlib.ExitStack()
        with wes:
            win_sb = wes.enter_context(nc.sbuf_tensor("win_sb", [128, NHC, 2 * DSL], BF16))
            wx_sb = wes.enter_context(nc.sbuf_tensor("wx_sb", [128, NDC, RDT + 2 * NST], BF16))
            @block.gpsimd
            def _(g):
                cx.w(g, "sV")
                cx.w(g, "sS")
                g.dma_start(out=win_sb[:], in_=win_sT.rearrange("kc p e -> p kc e")).then_inc(cx.sem["qWin"], 16)
                g.dma_start(out=wx_sb[:], in_=wxT.rearrange("dc p m -> p dc m")).then_inc(cx.sem["qWx"], 16)
            cx.cnt["qWin"] += 16
            cx.cnt["qWx"] += 16

            with nc.sbuf_tensor("wstg", [128, NHC * 4 * 128], BF16) as wstg:
                @block.gpsimd
                def _(g):
                    g.dma_start(out=wstg.rearrange("p (hc dcw h) -> p hc dcw h", hc=NHC, dcw=4),
                                in_=wout_t.rearrange("hc (dcw p) h -> p hc dcw h", p=128)).then_inc(cx.sem["qWo"], 16)
                cx.cnt["qWo"] += 16

                @block.sync
                def _(sync):
                    cx.w(sync, "qWo")
                    sync.dma_start(out=woag_in.rearrange("hc (dcw p) h -> p hc dcw h", p=128),
                                   in_=wstg.rearrange("p (hc dcw h) -> p hc dcw h", hc=NHC, dcw=4)).then_inc(cx.sem["qWo"], 16)
                cx.cnt["qWo"] += 16

            for e in range(8):
                b = e % 2
                @block.tensor
                def _(t, e=e, b=b):
                    if e == 0:
                        cx.w(t, "qWin")
                        cx.w(t, "sV", 1)
                    mm_wait_bank(t, b)
                    for kc in range(NHC):
                        mm = t.matmul(pb[b][:, 0:1], win_sb[:, kc, e * 128:(e + 1) * 128],
                                      ones_b[:, 0:1], start=(kc == 0), stop=(kc == NHC - 1))
                    cx.inc("sT", mm)

                @block.vector
                def _(v, e=e, b=b):
                    cx.w(v, "sT")
                    c = cx.inc("sV", v.tensor_copy(ws[:, e:e + 1], pb[b][:, 0:1]))
                    bank_guard[b] = ("sV", c)

            qes = contextlib.ExitStack()
            with qes:
                wq_in = qes.enter_context(nc.sbuf_tensor("wq_in", [128, 2, 1024], F32))
                wq_abs = qes.enter_context(nc.sbuf_tensor("wq_abs", [128, 2, 1024], F32))
                asum = qes.enter_context(nc.sbuf_tensor("asum", [128, 32], F32))
                as_tot = qes.enter_context(nc.sbuf_tensor("as_tot", [128, 2], F32))
                srow = qes.enter_context(nc.sbuf_tensor("srow", [1, 8], F32))
                wq_g1 = qes.enter_context(nc.sbuf_tensor("wq_g1", [128, 1024], F32))
                wq_g2 = qes.enter_context(nc.sbuf_tensor("wq_g2", [128, 1024], F32))
                wq_b = qes.enter_context(nc.sbuf_tensor("wq_b", [128, 2, 1024], BF16))
                mats = [(cfc_t, q1ag_in), (cprojT_t, q2ag_in)]
                p1_abs = [0, 0]
                p1_red = [0, 0]
                for mi, (src, dst) in enumerate(mats):
                    srcf = src.rearrange("a b p q -> (a b) p q")
                    for ch in range(16):
                        gi = mi * 16 + ch
                        m = gi % 2
                        @block.sync
                        def _(sync, srcf=srcf, ch=ch, gi=gi, m=m):
                            if gi == 0:
                                cx.w(sync, "sS")
                            cx.w(sync, "sS", p1_abs[m])
                            sync.dma_start(
                                out=wq_in[:, m, :].rearrange("p (a q) -> p a q", a=8),
                                in_=srcf[ch * 8:(ch + 1) * 8, :, :].rearrange("a p q -> p a q"),
                            ).then_inc(cx.sem["qQl"], 16)
                        cx.cnt["qQl"] += 16

                        @block.scalar
                        def _(s, m=m):
                            cx.w(s, "qQl")
                            cx.w(s, "sV", p1_red[m])
                            p1_abs[m] = cx.inc("sS", s.activation(wq_abs[:, m, :], wq_in[:, m, :], AF.Abs))

                        @block.vector
                        def _(v, mi=mi, ch=ch, m=m):
                            cx.w(v, "sS")
                            p1_red[m] = cx.inc("sV", v.reduce_sum(asum[:, mi * 16 + ch: mi * 16 + ch + 1],
                                                                  wq_abs[:, m, :], axis=mybir.AxisListType.X))

                @block.vector
                def _(v):
                    v.memset(srow[:], 0.0)
                    v.reduce_sum(as_tot[:, 0:1], asum[:, 0:16], axis=mybir.AxisListType.X)
                    cx.inc("sV", v.reduce_sum(as_tot[:, 1:2], asum[:, 16:32], axis=mybir.AxisListType.X))

                @block.tensor
                def _(t):
                    cx.w(t, "sV")
                    for mi in range(2):
                        b = 2 + mi
                        mm_wait_bank(t, b)
                        mm = t.matmul(pb[b][0:1, 0:1], as_tot[:, mi:mi + 1], ones_f[:, 0:1],
                                 start=True, stop=True)
                    cx.inc("sT", mm)

                @block.vector
                def _(v):
                    cx.w(v, "sT")
                    v.tensor_copy(srow[0:1, 0:1], pb[2][0:1, 0:1])
                    c = cx.inc("sV", v.tensor_copy(srow[0:1, 1:2], pb[3][0:1, 0:1]))
                    bank_guard[2] = ("sV", c)
                    bank_guard[3] = ("sV", c)

                @block.sync
                def _(sync):
                    cx.w(sync, "sV")
                    sync.dma_start(out=arS_in[0:1, 0:8], in_=srow[0:1, 0:8]).then_inc(cx.sem["qSc"], 16)
                cx.cnt["qSc"] += 16

                @block.gpsimd
                def _(g):
                    cx.w(g, "qSc")
                    cx.inc("sG", g.collective_compute(
                        "AllReduce", ALU.add, replica_groups=rg,
                        ins=[arS_in], outs=[arS_out]))

                @block.sync
                def _(sync):
                    cx.w(sync, "sG")
                    sync.dma_start(out=rowA[0:1, 0:2], in_=arS_out[0:1, 0:2]).then_inc(cx.sem["qSc"], 16)
                cx.cnt["qSc"] += 16

                @block.vector
                def _(v):
                    cx.w(v, "qSc")
                    v.tensor_scalar(rowB[0:1, 0:512], rowA[0:1, 0:512], 1.0 / (2048.0 * 8192.0), None, ALU.mult)
                    v.tensor_scalar(rowC[0:1, 0:512], rowB[0:1, 0:512], EPS, None, ALU.max)
                    v.reciprocal(rowD_[0:1, 0:512], rowC[0:1, 0:512])
                    cx.inc("sV", v.tensor_scalar(rowE[0:1, 0:512], rowC[0:1, 0:512], 1.0 / 255.0, None, ALU.mult))

                @block.sync
                def _(sync):
                    cx.w(sync, "sV")
                    sync.dma_start(out=rowsD[7, 0:2], in_=rowD_[0:1, 0:2]).then_inc(cx.sem["qSc"], 16)
                    sync.dma_start(out=rowsD[7, 2:3], in_=rowC[0:1, 0:1]).then_inc(cx.sem["qSc"], 16)
                    sync.dma_start(out=rowsD[7, 3:4], in_=rowE[0:1, 1:2]).then_inc(cx.sem["qSc"], 16)
                    cx.w(sync, "qSc", cx.v("qSc") + 48)
                    sync.dma_start(out=scl[:], in_=rowsD[7:8, 0:4].broadcast_to((128, 4))).then_inc(cx.sem["qCs"], 16)
                cx.cnt["qSc"] += 48
                cx.cnt["qCs"] += 16

                p2_v = [0, 0]
                p2_st = [0, 0]
                for mi, (src, dst) in enumerate(mats):
                    srcf = src.rearrange("a b p q -> (a b) p q")
                    dstf = dst.rearrange("a b p q -> (a b) p q")
                    for ch in range(16):
                        gi = mi * 16 + ch
                        m = gi % 2
                        @block.sync
                        def _(sync, srcf=srcf, ch=ch, gi=gi, m=m):
                            if gi < 2:
                                cx.w(sync, "sS", p1_abs[m])
                            cx.w(sync, "sV", p2_v[m])
                            sync.dma_start(
                                out=wq_in[:, m, :].rearrange("p (a q) -> p a q", a=8),
                                in_=srcf[ch * 8:(ch + 1) * 8, :, :].rearrange("a p q -> p a q"),
                            ).then_inc(cx.sem["qQl"], 16)
                        cx.cnt["qQl"] += 16

                        @block.vector
                        def _(v, mi=mi, m=m):
                            cx.w(v, "qQl")
                            cx.w(v, "qCs")
                            cx.w(v, "qQs", p2_st[m])
                            v.tensor_scalar(wq_abs[:, 0, :], wq_in[:, m, :], scl[:, mi:mi + 1], None, ALU.mult)
                            v.tensor_scalar(wq_g1[:], wq_abs[:, 0, :], 0.5, None, ALU.is_gt)
                            v.tensor_scalar(wq_g2[:], wq_abs[:, 0, :], -0.5, None, ALU.is_lt)
                            p2_v[m] = cx.inc("sV", v.tensor_tensor(wq_b[:, m, :], wq_g1[:], wq_g2[:], ALU.subtract))

                        @block.sync
                        def _(sync, dstf=dstf, ch=ch, m=m):
                            cx.w(sync, "sV")
                            sync.dma_start(
                                out=dstf[ch * 8:(ch + 1) * 8, :, :].rearrange("a p q -> p a q"),
                                in_=wq_b[:, m, :].rearrange("p (a q) -> p a q", a=8),
                            ).then_inc(cx.sem["qQs"], 16)
                            p2_st[m] = cx.cnt["qQs"] + 16
                        cx.cnt["qQs"] += 16

            @block.gpsimd
            def _(g):
                cx.w(g, "qQs")
                cx.w(g, "qWo")
                cx.inc("sG", g.collective_compute("AllGather", ALU.bypass, replica_groups=rg,
                                                  ins=[q1ag_in], outs=[q1ag_out]))
                cx.inc("sG", g.collective_compute("AllGather", ALU.bypass, replica_groups=rg,
                                                  ins=[q2ag_in], outs=[q2ag_out]))
                cx.inc("sG", g.collective_compute("AllGather", ALU.bypass, replica_groups=rg,
                                                  ins=[woag_in], outs=[woag_out]))
            sG_weights = cx.v("sG")

            if debug:
                @block.scalar
                def _(s):
                    cx.w(s, "sG", sG_weights)
                    s.dma_start(out=dbg_q1, in_=q1ag_in).then_inc(cx.sem["qDbg"], 16)
                    s.dma_start(out=dbg_scl[0:1, :], in_=arS_in).then_inc(cx.sem["qDbg"], 16)
                    s.dma_start(out=dbg_scl[1:2, :], in_=rowsD[7:8, 0:8]).then_inc(cx.sem["qDbg"], 16)
                    s.dma_start(out=dbg_scl[2:3, 0:4], in_=scl[0:1, :]).then_inc(cx.sem["qDbg"], 16)
                    s.dma_start(out=dbg_scl[3:4, 0:4], in_=scl[127:128, :]).then_inc(cx.sem["qDbg"], 16)
                cx.cnt["qDbg"] += 80

            fes = contextlib.ExitStack()
            with fes:
                xb = fes.enter_context(nc.sbuf_tensor("xb", [128, 2, NHC, TCF], BF16))
                x2t = fes.enter_context(nc.sbuf_tensor("x2t", [128, 2, 4, TCF], BF16))
                proj = fes.enter_context(nc.sbuf_tensor("proj", [128, 4, TCF], F32))
                projg = fes.enter_context(nc.sbuf_tensor("projg", [128, 4, TCF], BF16))
                hs_bf = fes.enter_context(nc.sbuf_tensor("hs_bf", [128, NDC, TCF], BF16))
                sg_bf = fes.enter_context(nc.sbuf_tensor("sg_bf", [128, NDC, TCF], BF16))
                conv_f = fes.enter_context(nc.sbuf_tensor("conv_f", [128, TCF], F32))
                pconv = fes.enter_context(nc.sbuf_tensor("pconv", [128, TCF + 3], F32))
                tail = fes.enter_context(nc.sbuf_tensor("tail", [128, NDC, 3], F32))
                bcmr = fes.enter_context(nc.sbuf_tensor("bcmr", [128, TCF], F32))
                bcr = fes.enter_context(nc.sbuf_tensor("bcr", [128, TCF], F32))
                wsmT = fes.enter_context(nc.sbuf_tensor("wsmT", [128, 512], F32))
                t1f = fes.enter_context(nc.sbuf_tensor("t1f", [128, 512], F32))
                sdt = fes.enter_context(nc.sbuf_tensor("sdt", [128, TCF], F32))
                sbc = fes.enter_context(nc.sbuf_tensor("sbc", [32, TCF], F32))
                @block.vector
                def _(v):
                    cx.inc("sV", v.memset(tail[:], 0.0))

                x2_free = [0, 0]
                xb_free = [(0, 0), (0, 0)]

                for tc in range(NF):
                    tsl = slice(tc * TCF, (tc + 1) * TCF)
                    xpar = tc % 2

                    @block.gpsimd
                    def _(g, tc=tc, tsl=tsl, xpar=xpar):
                        if tc > 0:
                            ft, fs = xb_free[xpar]
                            cx.w(g, "sT", ft)
                            cx.w(g, "sS", fs)
                        else:
                            cx.w(g, "qQs")
                        g.dma_start(
                            out=xb[:, xpar],
                            in_=xT[:, tsl].rearrange("(kc p) t -> p kc t", p=128),
                        ).then_inc(cx.sem["qFx"], 16)
                    cx.cnt["qFx"] += 16

                    for kg in range(4):
                        @block.scalar
                        def _(s, kg=kg, xpar=xpar):
                            cx.w(s, "qFx")
                            cx.w(s, "sT", x2_free[kg % 2])
                            for kk in range(4):
                                ii = s.activation(x2t[:, kg % 2, kk, :], xb[:, xpar, kg * 4 + kk, :], AF.Square)
                            cx.inc("sS", ii)

                        @block.tensor
                        def _(t, kg=kg, xpar=xpar):
                            cx.w(t, "qFx")
                            cx.w(t, "sS")
                            if kg == 0:
                                for b in (4, 5, 6, 7):
                                    mm_wait_bank(t, b)
                            for kk in range(4):
                                kc = kg * 4 + kk
                                for th in range(2):
                                    mm = t.matmul(pb[4 + th][0:1, :], ones_b[:, 0:1],
                                             xb[:, xpar, kc, th * 512:(th + 1) * 512],
                                             start=(kc == 0), stop=(kc == NHC - 1))
                                    mm = t.matmul(pb[6 + th][0:1, :], ones_b[0:1, 0:1].broadcast_to((128, 1)) if False else ones_b[:, 0:1],
                                             x2t[:, kg % 2, kk, th * 512:(th + 1) * 512],
                                             start=(kc == 0), stop=(kc == NHC - 1))
                            x2_free[kg % 2] = cx.inc("sT", mm)

                    @block.vector
                    def _(v, tc=tc):
                        cx.w(v, "sT")
                        cx.w(v, "qFr", cx.v("qFr"))
                        v.tensor_copy(rowA[0:1, 0:512], pb[4][0:1, :])
                        v.tensor_copy(rowA[0:1, 512:1024], pb[5][0:1, :])
                        v.tensor_copy(rowB[0:1, 0:512], pb[6][0:1, :])
                        v.tensor_copy(rowB[0:1, 512:1024], pb[7][0:1, :])
                        v.tensor_scalar(rowC[0:1, 0:TCF], rowA[0:1, 0:TCF], 1.0 / H, None, ALU.mult)
                        v.tensor_scalar(rowB[0:1, 0:TCF], rowB[0:1, 0:TCF], 1.0 / H, None, ALU.mult)
                        v.tensor_tensor(rowD_[0:1, 0:TCF], rowC[0:1, 0:TCF], rowC[0:1, 0:TCF], ALU.mult)
                        c = cx.inc("sV", v.tensor_tensor(rowD_[0:1, 0:TCF], rowB[0:1, 0:TCF], rowD_[0:1, 0:TCF], ALU.subtract))
                        for b in (4, 5, 6, 7):
                            bank_guard[b] = ("sV", c)

                    @block.scalar
                    def _(s):
                        cx.w(s, "sV")
                        cx.inc("sS", s.activation(rowD_[0:1, 0:TCF], rowD_[0:1, 0:TCF], AF.Sqrt,
                                                  bias=epsP[0:1, 0:1]))

                    @block.vector
                    def _(v):
                        cx.w(v, "sS")
                        v.reciprocal(rowE[0:1, 0:TCF], rowD_[0:1, 0:TCF])
                        cx.inc("sV", v.tensor_tensor(rowC[0:1, 0:TCF], rowC[0:1, 0:TCF], rowE[0:1, 0:TCF], ALU.mult))

                    @block.sync
                    def _(sync, tsl=tsl):
                        cx.w(sync, "sV")
                        sync.dma_start(out=rowsD[0, tsl], in_=rowC[0:1, 0:TCF]).then_inc(cx.sem["qFr"], 16)
                        sync.dma_start(out=rowsD[1, tsl], in_=rowE[0:1, 0:TCF]).then_inc(cx.sem["qFr"], 16)
                        cx.w(sync, "qFr", cx.v("qFr") + 32)
                        sync.dma_start(out=bcmr[:], in_=rowsD[0:1, tsl].broadcast_to((128, TCF))).then_inc(cx.sem["qFb"], 16)
                        sync.dma_start(out=bcr[:], in_=rowsD[1:2, tsl].broadcast_to((128, TCF))).then_inc(cx.sem["qFb"], 16)
                    cx.cnt["qFr"] += 32
                    cx.cnt["qFb"] += 32

                    sS_after_sq = cx.v("sS")
                    for e in range(8):
                        for th in range(2):
                            b = (e * 2 + th) % 4
                            @block.tensor
                            def _(t, e=e, th=th, b=b, xpar=xpar):
                                mm_wait_bank(t, b)
                                for kc in range(NHC):
                                    mm = t.matmul(pb[b][:, :], win_sb[:, kc, e * 128:(e + 1) * 128],
                                             xb[:, xpar, kc, th * 512:(th + 1) * 512],
                                             start=(kc == 0), stop=(kc == NHC - 1))
                                cx.inc("sT", mm)

                            @block.vector
                            def _(v, e=e, th=th, b=b):
                                cx.w(v, "sT")
                                cx.w(v, "qFb")
                                cx.w(v, "sS")
                                sl = slice(th * 512, (th + 1) * 512)
                                dst = proj[:, e, sl] if e < 4 else projg[:, e - 4, sl]
                                v.tensor_scalar(wsmT[:], bcmr[:, sl], ws[:, e:e + 1], None, ALU.mult)
                                v.tensor_tensor(t1f[:], pb[b][:, :], bcr[:, sl], ALU.mult)
                                c = cx.inc("sV", v.tensor_tensor(dst, t1f[:], wsmT[:], ALU.subtract))
                                bank_guard[b] = ("sV", c)

                    xb_free[xpar] = (cx.v("sT"), sS_after_sq)

                    for e in range(4):
                        @block.scalar
                        def _(s, e=e):
                            cx.w(s, "sV")
                            cx.w(s, "qOsg", cx.v("qOsg"))
                            cx.inc("sS", s.activation(sg_bf[:, e, :], projg[:, e, :], AF.Silu))

                    for dc in range(NDC):
                        @block.vector
                        def _(v, dc=dc):
                            cx.w(v, "sV")
                            cx.w(v, "sS")
                            cx.w(v, "qOhs", cx.v("qOhs"))
                            p = proj[:, dc, :]
                            v.tensor_copy(pconv[:, 0:3], tail[:, dc, :])
                            v.tensor_copy(pconv[:, 3:TCF + 3], p[:, 0:TCF])
                            v.tensor_scalar(conv_f[:, 0:TCF], pconv[:, 3:TCF + 3], wcv[:, dc, 3:4], None, ALU.mult)
                            for k in (2, 1, 0):
                                v.scalar_tensor_tensor(conv_f[:, 0:TCF], pconv[:, k:k + TCF],
                                                       wcv[:, dc, k:k + 1], conv_f[:, 0:TCF],
                                                       ALU.mult, ALU.add)
                            cx.inc("sV", v.tensor_copy(tail[:, dc, :], p[:, TCF - 3:TCF]))

                        @block.scalar
                        def _(s, dc=dc):
                            cx.w(s, "sV")
                            cx.w(s, "sT")
                            cx.inc("sS", s.activation(hs_bf[:, dc, :], conv_f[:, 0:TCF], AF.Silu,
                                                      bias=bcv[:, dc:dc + 1]))

                        @block.tensor
                        def _(t, dc=dc, tc=tc):
                            cx.w(t, "sS")
                            if dc == 0:
                                if tc == 0:
                                    cx.w(t, "qWx")
                                for b in (4, 5, 6, 7):
                                    mm_wait_bank(t, b)
                            for th in range(2):
                                mm = t.matmul(pb[4 + th][:, :], wx_sb[:, dc, 0:128],
                                         hs_bf[:, dc, th * 512:(th + 1) * 512],
                                         start=(dc == 0), stop=(dc == NDC - 1))
                                mm = t.matmul(pb[6 + th][0:32, :], wx_sb[:, dc, 128:160],
                                         hs_bf[:, dc, th * 512:(th + 1) * 512],
                                         start=(dc == 0), stop=(dc == NDC - 1))
                            cx.inc("sT", mm)

                    @block.vector
                    def _(v):
                        cx.w(v, "sT")
                        cx.w(v, "qOdt", cx.v("qOdt"))
                        cx.w(v, "qObc", cx.v("qObc"))
                        v.tensor_copy(sdt[:, 0:512], pb[4][:, :])
                        v.tensor_copy(sdt[:, 512:1024], pb[5][:, :])
                        v.tensor_copy(sbc[:, 0:512], pb[6][0:32, :])
                        c = cx.inc("sV", v.tensor_copy(sbc[:, 512:1024], pb[7][0:32, :]))
                        for b in (4, 5, 6, 7):
                            bank_guard[b] = ("sV", c)

                    @block.sync
                    def _(sync, tsl=tsl):
                        cx.w(sync, "sV")
                        cx.w(sync, "sS")
                        sync.dma_start(out=arDT_in[:, tsl], in_=sdt[:, 0:TCF]).then_inc(cx.sem["qOdt"], 16)
                        sync.dma_start(out=arBC_in[:, tsl], in_=sbc[:, 0:TCF]).then_inc(cx.sem["qObc"], 16)
                        sync.dma_start(out=hsD[:, :, tsl], in_=hs_bf[:]).then_inc(cx.sem["qOhs"], 16)
                        sync.dma_start(out=sgD[:, :, tsl], in_=sg_bf[:]).then_inc(cx.sem["qOsg"], 16)
                    cx.cnt["qOdt"] += 16
                    cx.cnt["qObc"] += 16
                    cx.cnt["qOhs"] += 16
                    cx.cnt["qOsg"] += 16

                @block.gpsimd
                def _(g):
                    cx.w(g, "qOdt")
                    cx.w(g, "qObc")
                    cx.inc("sG", g.collective_compute("AllReduce", ALU.add, replica_groups=rg,
                                                      ins=[arDT_in], outs=[arDT_out]))
                    cx.inc("sG", g.collective_compute("AllReduce", ALU.add, replica_groups=rg,
                                                      ins=[arBC_in], outs=[arBC_out]))
                sG_ar = cx.v("sG")

                if debug:
                    @block.scalar
                    def _(s):
                        cx.w(s, "qOhs")
                        cx.w(s, "qOsg")
                        cx.w(s, "sG", sG_ar)
                        s.dma_start(out=dbg_hs, in_=hsD).then_inc(cx.sem["qDbg"], 16)
                        s.dma_start(out=dbg_sg, in_=sgD).then_inc(cx.sem["qDbg"], 16)
                        s.dma_start(out=dbg_dt, in_=arDT_out).then_inc(cx.sem["qDbg"], 16)
                        s.dma_start(out=dbg_bc, in_=arBC_out).then_inc(cx.sem["qDbg"], 16)
                    cx.cnt["qDbg"] += 64

        with (
            nc.sbuf_tensor("bcf32", [32, L], F32) as bcf32,
            nc.sbuf_tensor("bcb16", [32, L], BF16) as bcb16,
        ):
            @block.sync
            def _(sync):
                cx.w(sync, "sG", sG_ar)
                cx.w(sync, "qOhs")
                cx.w(sync, "qOsg")
                sync.dma_start(out=bcf32[:], in_=arBC_out).then_inc(cx.sem["qBcd"], 16)
            cx.cnt["qBcd"] += 16

            @block.vector
            def _(v):
                cx.w(v, "qBcd")
                cx.inc("sV", v.tensor_copy(bcb16[:], bcf32[:]))

            @block.sync
            def _(sync):
                cx.w(sync, "sV")
                sync.dma_start(out=bcD, in_=bcb16[:]).then_inc(cx.sem["qBcd"], 16)
            cx.cnt["qBcd"] += 16

        mes = contextlib.ExitStack()
        with mes:
            wdt_m = mes.enter_context(nc.sbuf_tensor("wdt_m", [128, DSL], BF16))
            dtin_bf = mes.enter_context(nc.sbuf_tensor("dtin_bf", [128, TCM], BF16))
            dtc = mes.enter_context(nc.sbuf_tensor("dtc", [128, NDC, TCM], F32))
            dt_bf = mes.enter_context(nc.sbuf_tensor("dt_bf", [128, NDC, TCM], BF16))
            hs_m = mes.enter_context(nc.sbuf_tensor("hs_m", [128, NDC, TCM], BF16))
            sg_m = mes.enter_context(nc.sbuf_tensor("sg_m", [128, NDC, TCM], BF16))
            u_bf = mes.enter_context(nc.sbuf_tensor("u_bf", [128, NDC, TCM], BF16))
            spz = mes.enter_context(nc.sbuf_tensor("spz", [128, 512], F32))
            spa = mes.enter_context(nc.sbuf_tensor("spa", [128, 512], F32))
            sprl = mes.enter_context(nc.sbuf_tensor("sprl", [128, 512], F32))
            Bb = mes.enter_context(nc.sbuf_tensor("Bb", [128, 2, 1024], BF16))
            Cb = mes.enter_context(nc.sbuf_tensor("Cb", [128, 2, 1024], BF16))
            dA_t = mes.enter_context(nc.sbuf_tensor("dA_t", [128, 2, NDC, 1024], BF16))
            dBu_t = mes.enter_context(nc.sbuf_tensor("dBu_t", [128, 2, 1024], BF16))
            h_t = mes.enter_context(nc.sbuf_tensor("h_t", [128, 2, 1024], BF16))
            p_t = mes.enter_context(nc.sbuf_tensor("p_t", [128, 2, NDC, 1024], BF16))
            state = mes.enter_context(nc.sbuf_tensor("state", [128, NDC * NST], F32))
            ydc = mes.enter_context(nc.sbuf_tensor("ydc", [128, 1024], F32))
            yt1 = mes.enter_context(nc.sbuf_tensor("yt1", [128, 1024], BF16))
            y2t = mes.enter_context(nc.sbuf_tensor("y2t", [128, 1024], BF16))
            @block.gpsimd
            def _(g):
                cx.w(g, "qOhs")
                cx.w(g, "qOsg")
                cx.w(g, "qBcd")
                g.dma_start(out=wdt_m[:], in_=wdtT).then_inc(cx.sem["qWdt"], 16)
            cx.cnt["qWdt"] += 16

            @block.vector
            def _(v):
                cx.inc("sV", v.memset(state[:], 0.0))

            free_dA = [0, 0]
            free_bc = [0, 0]
            free_p = [0, 0]

            qMy_half = None
            for tm in range(NM):
                if tm == NM // 2 and qMy_half is None:
                    qMy_half = cx.v("qMy")
                tsl = slice(tm * TCM, (tm + 1) * TCM)

                @block.gpsimd
                def _(g, tsl=tsl, tm=tm):
                    if tm > 0:
                        cx.w(g, "sT")
                    else:
                        cx.w(g, "sG", sG_ar)
                        cx.w(g, "qOhs")
                        cx.w(g, "qOsg")
                        cx.w(g, "qBcd")
                    g.dma_start(out=dtin_bf[:], in_=arDT_out[:, tsl]).then_inc(cx.sem["qMd"], 16)
                cx.cnt["qMd"] += 16

                @block.sync
                def _(sync, tsl=tsl, tm=tm):
                    if tm > 0:
                        cx.w(sync, "sV")
                    else:
                        cx.w(sync, "qOhs")
                        cx.w(sync, "qOsg")
                    sync.dma_start(out=hs_m[:], in_=hsD[:, :, tsl]).then_inc(cx.sem["qMh"], 16)
                    sync.dma_start(out=sg_m[:], in_=sgD[:, :, tsl]).then_inc(cx.sem["qMh"], 16)
                cx.cnt["qMh"] += 32

                for dc in range(NDC):
                    for th in range(4):
                        b = (dc * 4 + th) % 2
                        @block.tensor
                        def _(t, dc=dc, th=th, b=b, tm=tm):
                            cx.w(t, "qMd")
                            if tm == 0 and dc == 0 and th == 0:
                                cx.w(t, "qWdt")
                            mm_wait_bank(t, b)
                            mm = t.matmul(pb[b][:, :], wdt_m[:, dc * 128:(dc + 1) * 128],
                                     dtin_bf[:, th * 512:(th + 1) * 512],
                                     start=True, stop=True)
                            cx.inc("sT", mm)

                        @block.scalar
                        def _(s, dc=dc, b=b):
                            cx.w(s, "sT")
                            cx.w(s, "sV")
                            s.activation(sprl[:], pb[b][:, :], AF.Relu, bias=bdt_s[:, dc:dc + 1])
                            s.activation(spa[:], pb[b][:, :], AF.Abs, bias=bdt_s[:, dc:dc + 1])
                            s.activation(spa[:], spa[:], AF.Exp, scale=-1.0)
                            cx.inc("sS", s.activation(spa[:], spa[:], AF.Ln, bias=ones_f[:, 0:1]))

                        @block.vector
                        def _(v, dc=dc, th=th, b=b):
                            cx.w(v, "sS")
                            sl = slice(th * 512, (th + 1) * 512)
                            v.tensor_tensor(dtc[:, dc, sl], spa[:], sprl[:], ALU.add)
                            c = cx.inc("sV", v.tensor_copy(dt_bf[:, dc, sl], dtc[:, dc, sl]))
                            bank_guard[b] = ("sV", c)

                for dc in range(NDC):
                    @block.vector
                    def _(v, dc=dc):
                        cx.w(v, "qMh")
                        cx.inc("sV", v.tensor_tensor(u_bf[:, dc, :], dt_bf[:, dc, :], hs_m[:, dc, :], ALU.mult))

                if debug and tm == 0:
                    @block.sync
                    def _(sync):
                        cx.w(sync, "sV")
                        sync.dma_start(out=dbg_dtc, in_=dtc[:]).then_inc(cx.sem["qDbg"], 16)
                    cx.cnt["qDbg"] += 16

                for half in range(2):
                    hsl = slice(half * 1024, (half + 1) * 1024)
                    gof = tm * TCM + half * 1024
                    for n in range(NST):
                        par = n % 2
                        qMb = "qMb%d" % par

                        @block.sync
                        def _(sync, n=n, par=par, gof=gof, qMb=qMb, tm=tm, half=half):
                            cx.w(sync, "sV", free_bc[par])
                            if tm == 0 and half == 0 and n < 2:
                                cx.w(sync, "qBcd")
                            sync.dma_start(out=Bb[:, par, :],
                                           in_=bcD[n:n + 1, gof:gof + 1024].broadcast_to((128, 1024))).then_inc(cx.sem[qMb], 16)
                        @block.scalar
                        def _(sc_, n=n, par=par, gof=gof, qMb=qMb, tm=tm, half=half):
                            cx.w(sc_, "sV", free_bc[par])
                            if tm == 0 and half == 0 and n < 2:
                                cx.w(sc_, "qBcd")
                            sc_.dma_start(out=Cb[:, par, :],
                                          in_=bcD[16 + n:17 + n, gof:gof + 1024].broadcast_to((128, 1024))).then_inc(cx.sem[qMb], 16)
                        cx.cnt[qMb] += 32

                        @block.scalar
                        def _(s, n=n, par=par, hsl=hsl):
                            cx.w(s, "sV", free_dA[par])
                            for dc in range(NDC):
                                col = dc * NST + n
                                ii = s.activation(dA_t[:, par, dc, :], dtc[:, dc, hsl], AF.Exp,
                                                  scale=A_sb[:, col:col + 1])
                            cx.inc("sS", ii)

                        @block.vector
                        def _(v, n=n, par=par, hsl=hsl, qMb=qMb):
                            cx.w(v, "sS")
                            cx.w(v, qMb)
                            cx.w(v, "sT", free_p[par])
                            for dc in range(NDC):
                                col = dc * NST + n
                                v.tensor_tensor(dBu_t[:, par, :], u_bf[:, dc, hsl], Bb[:, par, :], ALU.mult)
                                cx.inc("sV", v.tensor_tensor_scan(
                                    h_t[:, par, :], dA_t[:, par, dc, :], dBu_t[:, par, :],
                                    state[:, col:col + 1], ALU.mult, ALU.add))
                                v.tensor_copy(state[:, col:col + 1], h_t[:, par, 1023:1024])
                                cx.inc("sV", v.tensor_tensor(p_t[:, par, dc, :], h_t[:, par, :], Cb[:, par, :], ALU.mult))
                            free_dA[par] = cx.v("sV") - 1
                            free_bc[par] = cx.v("sV")

                        for dc in range(NDC):
                            @block.tensor
                            def _(t, dc=dc, n=n, par=par):
                                cx.w(t, "sV", cx.v("sV") - (NDC - 1 - dc) * 2)
                                if n == 0:
                                    mm_wait_bank(t, 2 * dc)
                                    mm_wait_bank(t, 2 * dc + 1)
                                for q in range(2):
                                    mm = t.matmul(pb[2 * dc + q][:, :], ident_b[:],
                                             p_t[:, par, dc, q * 512:(q + 1) * 512],
                                             start=(n == 0), stop=(n == NST - 1))
                                cx.inc("sT", mm)
                        free_p[par] = cx.v("sT")

                    for dc in range(NDC):
                        @block.scalar
                        def _(s, dc=dc):
                            cx.w(s, "sT")
                            cx.w(s, "sV")
                            s.copy(ydc[:, 0:512], pb[2 * dc][:, :])
                            cx.inc("sS", s.copy(ydc[:, 512:1024], pb[2 * dc + 1][:, :]))

                        @block.vector
                        def _(v, dc=dc, hsl=hsl):
                            cx.w(v, "sS")
                            cx.w(v, "qMy", cx.v("qMy"))
                            v.tensor_copy(yt1[:], ydc[:])
                            v.scalar_tensor_tensor(yt1[:], hs_m[:, dc, hsl], dv[:, dc:dc + 1],
                                                   yt1[:], ALU.mult, ALU.add)
                            c = cx.inc("sV", v.tensor_tensor(y2t[:], yt1[:], sg_m[:, dc, hsl], ALU.mult))
                            bank_guard[2 * dc] = ("sV", c)
                            bank_guard[2 * dc + 1] = ("sV", c)

                        @block.sync
                        def _(sync, dc=dc, gof=gof):
                            cx.w(sync, "sV")
                            sync.dma_start(out=y2ag_in[dc * 128:(dc + 1) * 128, gof:gof + 1024],
                                           in_=y2t[:]).then_inc(cx.sem["qMy"], 16)
                        cx.cnt["qMy"] += 16

            if debug:
                @block.sync
                def _(sync):
                    cx.w(sync, "sV")
                    sync.dma_start(out=dbg_st, in_=state[:]).then_inc(cx.sem["qDbg"], 16)
                cx.cnt["qDbg"] += 16

            @block.gpsimd
            def _(g):
                cx.w(g, "qMy")
                cx.inc("sG", g.collective_compute("AllGather", ALU.bypass, replica_groups=rg,
                                                  ins=[y2ag_in], outs=[y2ag_out]))
            sG_y2 = cx.v("sG")

            if debug:
                @block.scalar
                def _(s):
                    cx.w(s, "sG", sG_y2)
                    s.dma_start(out=dbg_y2, in_=y2ag_in).then_inc(cx.sem["qDbg"], 16)
                cx.cnt["qDbg"] += 16

        bes = contextlib.ExitStack()
        with bes:
            x1c = bes.enter_context(nc.sbuf_tensor("x1c", [128, NHC, TCB], F32))
            x1n = bes.enter_context(nc.sbuf_tensor("x1n", [128, NHC, TCB], BF16))
            oo = bes.enter_context(nc.sbuf_tensor("oo", [128, NHC, TCB], F32))
            bA = bes.enter_context(nc.sbuf_tensor("bA", [128, TCB], F32))
            bB = bes.enter_context(nc.sbuf_tensor("bB", [128, TCB], F32))
            tf1 = bes.enter_context(nc.sbuf_tensor("tf1", [128, TCB], F32))
            sqt = bes.enter_context(nc.sbuf_tensor("sqt", [128, 2, TCB], F32))
            xtc = bes.enter_context(nc.sbuf_tensor("xtc", [128, 2, TCB], F32))
            lnbc = [0]
            def ln_rows(r_scale=None):
                @block.vector
                def _(v):
                    cx.w(v, "sT")
                    cx.w(v, "qBr", cx.v("qBr"))
                    v.tensor_copy(rowA[0:1, 0:TCB], pb[6][0:1, :])
                    v.tensor_copy(rowB[0:1, 0:TCB], pb[7][0:1, :])
                    v.tensor_scalar(rowC[0:1, 0:TCB], rowA[0:1, 0:TCB], 1.0 / H, None, ALU.mult)
                    v.tensor_scalar(rowB[0:1, 0:TCB], rowB[0:1, 0:TCB], 1.0 / H, None, ALU.mult)
                    v.tensor_tensor(rowD_[0:1, 0:TCB], rowC[0:1, 0:TCB], rowC[0:1, 0:TCB], ALU.mult)
                    c = cx.inc("sV", v.tensor_tensor(rowD_[0:1, 0:TCB], rowB[0:1, 0:TCB], rowD_[0:1, 0:TCB], ALU.subtract))
                    bank_guard[6] = ("sV", c)
                    bank_guard[7] = ("sV", c)

                @block.scalar
                def _(s):
                    cx.w(s, "sV")
                    cx.inc("sS", s.activation(rowD_[0:1, 0:TCB], rowD_[0:1, 0:TCB], AF.Sqrt,
                                              bias=epsP[0:1, 0:1]))

                @block.vector
                def _(v):
                    cx.w(v, "sS")
                    v.reciprocal(rowE[0:1, 0:TCB], rowD_[0:1, 0:TCB])
                    if r_scale is not None:
                        v.tensor_scalar(rowE[0:1, 0:TCB], rowE[0:1, 0:TCB], r_scale, None, ALU.mult)
                    cx.inc("sV", v.engine_nop())

                @block.tensor
                def _(t):
                    cx.w(t, "sV")
                    mm_wait_bank(t, 2)
                    mm_wait_bank(t, 3)
                    mm = t.matmul(pb[2][:, :], ones_r[0:1, :], rowC[0:1, 0:TCB], start=True, stop=True)
                    mm = t.matmul(pb[3][:, :], ones_r[0:1, :], rowE[0:1, 0:TCB], start=True, stop=True)
                    cx.inc("sT", mm)
                lnbc[0] = cx.v("sT")

            for tb in range(NB):
                tok0 = tb * TCB

                aes = contextlib.ExitStack()
                with aes:
                    y2b = aes.enter_context(nc.sbuf_tensor("y2b_%d" % tb, [128, 32, TCB], BF16))
                    wo_t = aes.enter_context(nc.sbuf_tensor("wo_t_%d" % tb, [128, 2, 32, 128], BF16))
                    mo = aes.enter_context(nc.sbuf_tensor("mo_%d" % tb, [128, NHC, TCB], F32))
                    @block.gpsimd
                    def _(g, tb=tb):
                        cx.w(g, "qC", qC_base)
                        if tb > 0:
                            cx.w(g, "sT")
                        else:
                            cx.w(g, "sG", sG_y2)
                        off = g.value_load(in_tensor=cid_sb[0:1, tb:tb + 1])
                        g.dma_start(
                            out=y2b[:],
                            in_=y2ag_out.rearrange("(dcg p) t -> p dcg t", p=128)[:, :, bass.ts(off, TCB)],
                        ).then_inc(cx.sem["qBy"], 16)
                    cx.cnt["qBy"] += 16

                    wo_free = [0, 0]
                    for hc in range(NHC):
                        parw = hc % 2
                        qBw = "qBw%d" % parw
                        b = hc % 2

                        @block.sync
                        def _(sync, hc=hc, parw=parw, qBw=qBw, tb=tb):
                            if hc == 0:
                                if tb == 0:
                                    cx.w(sync, "sG", sG_y2)
                                else:
                                    cx.w(sync, "sT")
                            cx.w(sync, "sT", wo_free[parw])
                            for j in range(NCORE):
                                sync.dma_start(
                                    out=wo_t[:, parw, j * 4:(j + 1) * 4, :],
                                    in_=woag_out[j * NHC + hc].rearrange("(dcw p) h -> p dcw h", p=128),
                                ).then_inc(cx.sem[qBw], 16)
                        cx.cnt[qBw] += 16 * NCORE

                        @block.tensor
                        def _(t, hc=hc, b=b, parw=parw, qBw=qBw, tb=tb):
                            cx.w(t, qBw)
                            cx.w(t, "qBy")
                            mm_wait_bank(t, b)
                            if hc == 0:
                                mm_wait_bank(t, 6)
                                mm_wait_bank(t, 7)
                                if tb == 0:
                                    cx.w(t, "sG", sG_weights)
                            for dc in range(32):
                                mm = t.matmul(pb[b][:, :], wo_t[:, parw, dc, :], y2b[:, dc, :],
                                         start=(dc == 0), stop=(dc == 31))
                            wo_free[parw] = cx.inc("sT", mm)

                        @block.scalar
                        def _(s, hc=hc, b=b):
                            cx.w(s, "sT")
                            s.copy(mo[:, hc, :], pb[b][:, :])
                            c = cx.inc("sS", s.activation(oo[:, hc, :], pb[b][:, :], AF.Square))
                            bank_guard[b] = ("sS", c)

                    @block.tensor
                    def _(t):
                        cx.w(t, "sS")
                        mm_wait_bank(t, 6)
                        mm_wait_bank(t, 7)
                        for hc in range(NHC):
                            mm = t.matmul(pb[6][0:1, :], ones_f[:, 0:1], mo[:, hc, :],
                                     start=(hc == 0), stop=(hc == NHC - 1))
                            mm = t.matmul(pb[7][0:1, :], ones_f[:, 0:1], oo[:, hc, :],
                                     start=(hc == 0), stop=(hc == NHC - 1))
                        cx.inc("sT", mm)

                    ln_rows()

                    for hc in range(NHC):
                        parx = hc % 2
                        @block.sync
                        def _(sync, hc=hc, parx=parx, tb=tb):
                            cx.w(sync, "sV")
                            sync.dma_start(
                                out=xtc[:, parx, :],
                                in_=xTs[hc, :, tok0:tok0 + TCB]).then_inc(cx.sem["qBx"], 16)
                        cx.cnt["qBx"] += 16

                        @block.vector
                        def _(v, hc=hc, parx=parx):
                            cx.w(v, "sT", lnbc[0])
                            cx.w(v, "qBx")
                            cx.w(v, "sS")
                            v.tensor_tensor(tf1[:], mo[:, hc, :], pb[2][:, :], ALU.subtract)
                            v.tensor_tensor(tf1[:], tf1[:], pb[3][:, :], ALU.mult)
                            c = cx.inc("sV", v.tensor_tensor(x1c[:, hc, :], tf1[:], xtc[:, parx, :], ALU.add))
                            bank_guard[2] = ("sV", c)
                            bank_guard[3] = ("sV", c)

                        @block.scalar
                        def _(s, hc=hc):
                            cx.w(s, "sV")
                            cx.w(s, "sT")
                            cx.inc("sS", s.activation(sqt[:, hc % 2, :], x1c[:, hc, :], AF.Square))

                        @block.tensor
                        def _(t, hc=hc):
                            cx.w(t, "sS")
                            cx.w(t, "sV")
                            if hc == 0:
                                mm_wait_bank(t, 6)
                                mm_wait_bank(t, 7)
                            mm = t.matmul(pb[6][0:1, :], ones_f[:, 0:1], x1c[:, hc, :],
                                     start=(hc == 0), stop=(hc == NHC - 1))
                            mm = t.matmul(pb[7][0:1, :], ones_f[:, 0:1], sqt[:, hc % 2, :],
                                     start=(hc == 0), stop=(hc == NHC - 1))
                            cx.inc("sT", mm)

                    @block.sync
                    def _(sync, tb=tb):
                        cx.w(sync, "sV")
                        cx.w(sync, "qBx1", cx.v("qBx1"))
                        sync.dma_start(out=x1D[:, :, tok0:tok0 + TCB], in_=x1c[:]).then_inc(cx.sem["qBx1"], 16)
                    cx.cnt["qBx1"] += 16

                    ln_rows(r_scale=scl[0:1, 2:3])

                    for hc in range(NHC):
                        @block.vector
                        def _(v, hc=hc):
                            cx.w(v, "sT", lnbc[0])
                            v.tensor_tensor(tf1[:], x1c[:, hc, :], pb[2][:, :], ALU.subtract)
                            c = cx.inc("sV", v.tensor_tensor(x1n[:, hc, :], tf1[:], pb[3][:, :], ALU.mult))
                            bank_guard[2] = ("sV", c)
                            bank_guard[3] = ("sV", c)

                    if debug and tb == 0:
                        @block.gpsimd
                        def _(g):
                            cx.w(g, "sV")
                            g.dma_start(out=dbg_x1n, in_=x1n[:]).then_inc(cx.sem["qDbg"], 16)
                            g.dma_start(out=dbg_ffn[:, 0, :], in_=x1n[:, 0, :]).then_inc(cx.sem["qDbg"], 16)
                            g.dma_start(out=dbg_ffn[:, 4, :], in_=bA[:]).then_inc(cx.sem["qDbg"], 16)
                            g.dma_start(out=dbg_ffn[:, 5, :], in_=bB[:]).then_inc(cx.sem["qDbg"], 16)
                        cx.cnt["qDbg"] += 64

                sbes = contextlib.ExitStack()
                with sbes:
                    rb = sbes.enter_context(nc.sbuf_tensor("rb_%d" % tb, [128, NHH, TCB], BF16))
                    q1t = sbes.enter_context(nc.sbuf_tensor("q1t_%d" % tb, [128, 3, NHC, 128], BF16))
                    q2t = sbes.enter_context(nc.sbuf_tensor("q2t_%d" % tb, [128, 2, 8, 128], BF16))
                    mx = sbes.enter_context(nc.sbuf_tensor("mx_%d" % tb, [128, TCB], BF16))
                    x2c = sbes.enter_context(nc.sbuf_tensor("x2c_%d" % tb, [128, TCB], BF16))
                    rxb = sbes.enter_context(nc.sbuf_tensor("rxb_%d" % tb, [128, TCB], BF16))
                    xms2 = sbes.enter_context(nc.sbuf_tensor("xms2_%d" % tb, [128, TCB], BF16))
                    qf = sbes.enter_context(nc.sbuf_tensor("qf_%d" % tb, [128, TCB], F32))
                    qi = sbes.enter_context(nc.sbuf_tensor("qi_%d" % tb, [128, TCB], I32))
                    qb = sbes.enter_context(nc.sbuf_tensor("qb_%d" % tb, [128, TCB], BF16))
                    @block.vector
                    def _(v):
                        cx.w(v, "sV")
                        cx.w(v, "sT")
                        cx.inc("sV", v.memset(mx[:], 0.0))

                    q1_free = [0, 0, 0]
                    for hh in range(NHH):
                        parq = hh % 3
                        qBq = ("qBq0", "qBq1", "qBq2")[parq]
                        b = hh % 2

                        def q1load(eng, hh=hh, parq=parq, qBq=qBq):
                            if hh < 3:
                                cx.w(eng, "sT")
                            cx.w(eng, "sT", q1_free[parq])
                            eng.dma_start(
                                out=q1t[:, parq, :, :],
                                in_=q1ag_out[hh].rearrange("kc p q -> p kc q"),
                            ).then_inc(cx.sem[qBq], 16)
                        @block.sync
                        def _(sync, f=q1load):
                            f(sync)
                        cx.cnt[qBq] += 16

                        if debug and tb == 0 and hh == 0:
                            @block.gpsimd
                            def _(g, parq=parq, qBq=qBq):
                                cx.w(g, qBq)
                                g.dma_start(out=dbg_ffn[:, 6, 0:128], in_=q1t[:, parq, 0, :]).then_inc(cx.sem["qDbg"], 16)
                                g.dma_start(out=dbg_ffn[:, 6, 128:256], in_=q1t[:, parq, 1, :]).then_inc(cx.sem["qDbg"], 16)
                            cx.cnt["qDbg"] += 32
                            dbq = cx.v("qDbg")
                            @block.tensor
                            def _(t, g=dbq):
                                cx.w(t, "qDbg", g)

                        @block.tensor
                        def _(t, hh=hh, b=b, parq=parq, qBq=qBq):
                            cx.w(t, qBq)
                            if hh == 0:
                                cx.w(t, "sV")
                            mm_wait_bank(t, b)
                            for kc in range(NHC):
                                mm = t.matmul(pb[b][:, :], q1t[:, parq, kc, :], x1n[:, kc, :],
                                         start=(kc == 0), stop=(kc == NHC - 1))
                            q1_free[parq] = cx.inc("sT", mm)

                        if debug and tb == 0 and hh == 0:
                            @block.scalar
                            def _(s, b=b):
                                cx.w(s, "sT")
                                cx.inc("sS", s.copy(qg[:], pb[b][:, :]))
                            @block.gpsimd
                            def _(g):
                                cx.w(g, "sS")
                                g.dma_start(out=dbg_ffn[:, 7, :], in_=qg[:]).then_inc(cx.sem["qDbg"], 16)
                            cx.cnt["qDbg"] += 16
                            dbq2 = cx.v("qDbg")
                            @block.vector
                            def _(v, g=dbq2):
                                cx.w(v, "qDbg", g)

                        @block.scalar
                        def _(s, hh=hh, b=b):
                            cx.w(s, "sT")
                            cx.inc("sS", s.activation(rb[:, hh, :], pb[b][:, :], AF.Relu))

                        @block.vector
                        def _(v, hh=hh, b=b):
                            cx.w(v, "sS")
                            v.tensor_tensor(x2c[:], rb[:, hh, :], rb[:, hh, :], ALU.mult)
                            c = cx.inc("sV", v.tensor_tensor(mx[:], mx[:], x2c[:], ALU.max))
                            bank_guard[b] = ("sV", c)

                    if debug and tb == 0:
                        @block.gpsimd
                        def _(g):
                            cx.w(g, "sS")
                            cx.w(g, "sV")
                            g.dma_start(out=dbg_ffn[:, 1, :], in_=rb[:, 0, :]).then_inc(cx.sem["qDbg"], 16)
                        cx.cnt["qDbg"] += 16

                    half = 64
                    while half >= 1:
                        @block.scalar
                        def _(sc_, half=half):
                            cx.w(sc_, "sV")
                            sc_.dma_start(out=qb[0:half, :], in_=mx[half:2 * half, :]).then_inc(cx.sem["qBsh"], 16)
                        cx.cnt["qBsh"] += 16

                        @block.vector
                        def _(v, half=half):
                            cx.w(v, "qBsh")
                            cx.inc("sV", v.tensor_tensor(mx[0:half, :], mx[0:half, :], qb[0:half, :], ALU.max))
                        half //= 2

                    @block.vector
                    def _(v):
                        cx.w(v, "qBr", cx.v("qBr"))
                        v.tensor_copy(rowA[0:1, 0:TCB], mx[0:1, :])
                        cx.inc("sV", v.tensor_scalar(rowA[0:1, 0:TCB], rowA[0:1, 0:TCB], EPS, None, ALU.max))

                    @block.sync
                    def _(sync):
                        cx.w(sync, "sV")
                        sync.dma_start(out=rowsD[4, 0:TCB], in_=rowA[0:1, 0:TCB]).then_inc(cx.sem["qBr"], 16)
                        cx.w(sync, "qBr", cx.v("qBr") + 16)
                        sync.dma_start(out=qf[:], in_=rowsD[4:5, 0:TCB].broadcast_to((128, TCB))).then_inc(cx.sem["qBb"], 16)
                    cx.cnt["qBr"] += 16
                    cx.cnt["qBb"] += 16

                    @block.vector
                    def _(v):
                        cx.w(v, "qBb")
                        v.tensor_scalar(xms2[:], qf[:], scl[:, 3:4], None, ALU.mult)
                        v.reciprocal(qf[:], qf[:])
                        cx.inc("sV", v.tensor_scalar(rxb[:], qf[:], 255.0, None, ALU.mult))

                    quant_done = []
                    for hh in range(NHH):
                        @block.vector
                        def _(v, hh=hh):
                            v.tensor_tensor(x2c[:], rb[:, hh, :], rb[:, hh, :], ALU.mult)
                            v.tensor_tensor(qf[:], x2c[:], rxb[:], ALU.mult)
                            v.tensor_copy(qi[:], qf[:])
                            v.tensor_copy(qb[:], qi[:])
                            v.tensor_tensor(qf[:], qf[:], qb[:], ALU.subtract)
                            v.tensor_scalar(qf[:], qf[:], 0.5, None, ALU.is_gt)
                            v.tensor_tensor(x2c[:], qb[:], qf[:], ALU.add)
                            quant_done.append(cx.inc("sV", v.tensor_tensor(rb[:, hh, :], x2c[:], xms2[:], ALU.mult)))

                    q2_free = [0, 0]
                    for dh in range(2):
                        for hh in range(NHH):
                            parq = hh % 2
                            qB2 = "qB2%d" % (parq % 2)

                            def q2load(eng, hh=hh, dh=dh, parq=parq, qB2=qB2):
                                if dh == 0 and hh < 2:
                                    cx.w(eng, "sT")
                                cx.w(eng, "sT", q2_free[parq])
                                eng.dma_start(
                                    out=q2t[:, parq, :, :],
                                    in_=q2ag_out[hh, dh * 8:(dh + 1) * 8].rearrange("dc p q -> p dc q"),
                                ).then_inc(cx.sem[qB2], 16)
                            @block.sync
                            def _(sync, f=q2load):
                                f(sync)
                            cx.cnt[qB2] += 16

                            @block.tensor
                            def _(t, hh=hh, dh=dh, parq=parq, qB2=qB2):
                                cx.w(t, qB2)
                                cx.w(t, "sV", quant_done[hh])
                                if hh == 0:
                                    for bq in range(8):
                                        mm_wait_bank(t, bq)
                                for dcw in range(8):
                                    mm = t.matmul(pb[dcw][:, :], q2t[:, parq, dcw, :], rb[:, hh, :],
                                             start=(hh == 0), stop=(hh == NHH - 1))
                                q2_free[parq] = cx.inc("sT", mm)

                        for dcw in range(8):
                            hc = dh * 8 + dcw
                            @block.scalar
                            def _(s, dcw=dcw, hc=hc):
                                cx.w(s, "sT")
                                c = cx.inc("sS", s.copy(oo[:, hc, :], pb[dcw][:, :]))
                                bank_guard[dcw] = ("sS", c)

                    for hc in range(NHC):
                        @block.scalar
                        def _(s, hc=hc):
                            cx.w(s, "sV" if hc == 0 else "sS")
                            cx.w(s, "sT")
                            cx.inc("sS", s.activation(sqt[:, hc % 2, :], oo[:, hc, :], AF.Square))

                        @block.tensor
                        def _(t, hc=hc):
                            cx.w(t, "sS")
                            if hc == 0:
                                mm_wait_bank(t, 6)
                                mm_wait_bank(t, 7)
                            mm = t.matmul(pb[6][0:1, :], ones_f[:, 0:1], oo[:, hc, :],
                                     start=(hc == 0), stop=(hc == NHC - 1))
                            mm = t.matmul(pb[7][0:1, :], ones_f[:, 0:1], sqt[:, hc % 2, :],
                                     start=(hc == 0), stop=(hc == NHC - 1))
                            cx.inc("sT", mm)

                    if debug and tb == 0:
                        @block.gpsimd
                        def _(g):
                            cx.w(g, "sS")
                            cx.w(g, "sV")
                            g.dma_start(out=dbg_ffn[:, 2, :], in_=oo[:, 0, :]).then_inc(cx.sem["qDbg"], 16)
                            g.dma_start(out=dbg_ffn[:, 3, :], in_=rb[:, 0, :]).then_inc(cx.sem["qDbg"], 16)
                        cx.cnt["qDbg"] += 32

                    ln_rows()

                ces = contextlib.ExitStack()
                with ces:
                    outF = ces.enter_context(nc.sbuf_tensor("outF_%d" % tb, [128, NHC, TCB], F32))
                    outT = ces.enter_context(nc.sbuf_tensor("outT_%d" % tb, [128, 2048], F32))
                    outf_done = [0] * NHC
                    for hc in range(NHC):
                        parx = hc % 2
                        @block.sync
                        def _(sync, hc=hc, parx=parx, tb=tb):
                            cx.w(sync, "sV")
                            if hc == 0:
                                cx.w(sync, "qBx1")
                                cx.w(sync, "sT")
                            sync.dma_start(
                                out=xtc[:, parx, :],
                                in_=x1D[:, hc, tok0:tok0 + TCB]).then_inc(cx.sem["qBx"], 16)
                        cx.cnt["qBx"] += 16

                        @block.vector
                        def _(v, hc=hc, parx=parx):
                            cx.w(v, "sT", lnbc[0])
                            cx.w(v, "qBx")
                            v.tensor_tensor(tf1[:], oo[:, hc, :], pb[2][:, :], ALU.subtract)
                            v.tensor_tensor(tf1[:], tf1[:], pb[3][:, :], ALU.mult)
                            c = cx.inc("sV", v.tensor_tensor(outF[:, hc, :], tf1[:], xtc[:, parx, :], ALU.add))
                            bank_guard[2] = ("sV", c)
                            bank_guard[3] = ("sV", c)
                            outf_done[hc] = c

                    for q in range(TCB // 128):
                        for hc in range(NHC):
                            bt = (q * NHC + hc) % 2
                            @block.tensor
                            def _(t, q=q, hc=hc, bt=bt):
                                cx.w(t, "sV", outf_done[hc])
                                mm_wait_bank(t, bt)
                                mm = t.transpose(pb[bt][:, 0:128], outF[:, hc, q * 128:(q + 1) * 128],
                                            ident_f[:])
                                cx.inc("sT", mm)

                            @block.scalar
                            def _(s, q=q, hc=hc, bt=bt):
                                cx.w(s, "sT")
                                if hc == 0:
                                    cx.w(s, "qOut", cx.v("qOut"))
                                c = cx.inc("sS", s.copy(outT[:, hc * 128:(hc + 1) * 128], pb[bt][:, 0:128]))
                                bank_guard[bt] = ("sS", c)

                        @block.sync
                        def _(sync, q=q, tb=tb):
                            cx.w(sync, "sS")
                            sync.dma_start(out=y_out[tok0 + q * 128: tok0 + (q + 1) * 128, :],
                                           in_=outT[:]).then_inc(cx.sem["qOut"], 16)
                        cx.cnt["qOut"] += 16

            if debug:
                @block.scalar
                def _(s):
                    cx.w(s, "qBx1")
                    s.dma_start(out=dbg_x1, in_=x1D).then_inc(cx.sem["qDbg"], 16)
                cx.cnt["qDbg"] += 16

            @block.sync
            def _(sync):
                cx.w(sync, "qOut")
                if debug:
                    cx.w(sync, "qDbg")

    return nc


def _host_inputs(core, inp):
    x = np.asarray(inp["L0_attn_x"])[0]
    xT = np.ascontiguousarray(x.T)
    sl_t = slice(core * TSL, (core + 1) * TSL)
    sl_d = slice(core * DSL, (core + 1) * DSL)

    Win = np.asarray(inp["L0_attn_Win"])
    win_s = np.concatenate([Win[sl_d], Win[DI + core * DSL: DI + (core + 1) * DSL]], axis=0)
    win_sT = np.ascontiguousarray(win_s.T).reshape(NHC, 128, 2 * DSL)

    Wx = np.asarray(inp["L0_attn_Wx"])
    wxT = np.ascontiguousarray(Wx[:, sl_d].T).reshape(NDC, 128, RDT + 2 * NST)
    Wdt = np.asarray(inp["L0_attn_Wdt"])
    wdtT = np.ascontiguousarray(Wdt[sl_d].T)

    Wout = np.asarray(inp["L0_attn_Wout"])
    woutT = np.ascontiguousarray(Wout.T)
    wo_slice = woutT[sl_d]
    wout_t = np.stack([np.ascontiguousarray(wo_slice[:, hc * 128:(hc + 1) * 128]) for hc in range(NHC)])

    cfc = np.asarray(inp["L0_ffn_c_fc"])
    cfc_s = cfc[:, core * HSL:(core + 1) * HSL]
    cfc_t = np.stack([
        np.stack([np.ascontiguousarray(cfc_s[kc * 128:(kc + 1) * 128, hw * 128:(hw + 1) * 128]) for kc in range(NHC)])
        for hw in range(8)
    ])
    cproj = np.asarray(inp["L0_ffn_c_proj"])
    cprojT = np.ascontiguousarray(cproj.T)
    cpT_s = cprojT[core * HSL:(core + 1) * HSL]
    cprojT_t = np.stack([
        np.stack([np.ascontiguousarray(cpT_s[hw * 128:(hw + 1) * 128, dc * 128:(dc + 1) * 128]) for dc in range(NHC)])
        for hw in range(8)
    ])

    def chunk_r(vec):
        return np.ascontiguousarray(np.asarray(vec)[sl_d].reshape(NDC, 128).T)

    wconv = np.asarray(inp["L0_attn_Wconv"])[sl_d, 0, :]
    wconv_r = np.ascontiguousarray(wconv.reshape(NDC, 128, KCV).transpose(1, 0, 2))
    alog = np.asarray(inp["L0_attn_A_log"])[sl_d]
    alog_r = np.ascontiguousarray(
        alog.reshape(NDC, 128, NST).transpose(1, 0, 2)).reshape(128, NDC * NST)

    return {
        "xT": xT,
        "xTs": np.ascontiguousarray(xT[:, sl_t].reshape(NHC, 128, TSL)),
        "win_sT": win_sT,
        "wxT": wxT,
        "wdtT": wdtT,
        "wout_t": wout_t,
        "cfc_t": cfc_t,
        "cprojT_t": cprojT_t,
        "wconv_r": wconv_r,
        "bconv_r": chunk_r(inp["L0_attn_bconv"]),
        "bdt_r": chunk_r(inp["L0_attn_bdt"]),
        "d_r": chunk_r(inp["L0_attn_D"]),
        "alog_r": alog_r,
        "iden": np.eye(128, dtype=np.float32),
        "cidx": (np.arange(NB, dtype=np.int32) + core * NB).reshape(1, NB),
    }


def kernel(**inputs):
    nc = build(debug=DEBUG)
    in_maps = [_host_inputs(c, inputs) for c in range(NCORE)]
    res = run_bass_kernel_spmd(nc, in_maps, core_ids=list(range(NCORE)))
    out = np.concatenate([res.results[c]["y_out"] for c in range(NCORE)], axis=0)
    return np.ascontiguousarray(out.reshape(1, L, H))
